# version 05
# baseline (speedup 1.0000x reference)
import os
import numpy as np
from contextlib import ExitStack

os.environ.setdefault("NEURON_SCRATCHPAD_PAGE_SIZE", "2048")

import concourse.bass as bass
import concourse.mybir as mybir
from concourse.bass_utils import run_bass_kernel_spmd

F32 = mybir.dt.float32
BF16 = mybir.dt.bfloat16
AF = mybir.ActivationFunctionType
ALU = mybir.AluOpType
AX = mybir.AxisListType

DM = 4096
D = 128
T = 2048
TOWN = 1024
NCH = 16
NOG = 8
OWN0 = 8
TB = 256
HIDR = 1408
NRANK = 8
HID_PAD = HIDR * NRANK
NSLOT = HID_PAD // 128
HSL = NSLOT // 2
W3R = DM // NRANK
EPS = 1e-5
SCALE_BIG = float(2.0 ** 100)

NF4_CODE = np.array([-1.0, -0.6961928009986877, -0.5250730514526367,
                     -0.39491748809814453, -0.28444138169288635, -0.18477343022823334,
                     -0.09105003625154495, 0.0, 0.07958029955625534, 0.16093020141124725,
                     0.24611230194568634, 0.33791524171829224, 0.44070982933044434,
                     0.5626170039176941, 0.7229568362236023, 1.0], dtype=np.float32)
NF4_BOUNDS = ((NF4_CODE[:-1] + NF4_CODE[1:]) * np.float32(0.5)).astype(np.float32)


def _nf4_deltas():
    deltas = np.zeros(15, dtype=np.float32)
    s = np.float32(NF4_CODE[0])
    for i in range(15):
        dd = np.float32(NF4_CODE[i + 1] - s)
        deltas[i] = dd
        s = np.float32(s + dd)
        assert s == NF4_CODE[i + 1]
    return deltas


NF4_DELTAS = _nf4_deltas()
CB_SIG0 = 0
CB_ONE = 15
CB_EPS = 16


class Sem:
    def __init__(self, stack, nc, name):
        self.h = stack.enter_context(nc.semaphore(name))
        self.n = 0

    def inc(self, instr, k=1):
        instr.then_inc(self.h, k)
        self.n += k


def build_program(nh=32, nkvh=8, do_ffn=True, do_ag=True, dbg=False, strict=False):
    gq = nh // nkvh
    nc = bass.Bass("TRN2", target_bir_lowering=False)
    dt_ = nc.dram_tensor

    x_all = dt_("x_all", [T, DM], F32, kind="ExternalInput").ap()
    wqT = dt_("wqT", [DM, nh * D], F32, kind="ExternalInput").ap()
    wkT = dt_("wkT", [DM, nkvh * D], F32, kind="ExternalInput").ap()
    wvT = dt_("wvT", [DM, nkvh * D], F32, kind="ExternalInput").ap()
    wgT = dt_("wgT", [DM, nh * D], F32, kind="ExternalInput").ap()
    winT = dt_("winT", [DM, 32], F32, kind="ExternalInput").ap()
    woT = dt_("woT", [nh * D, DM], F32, kind="ExternalInput").ap()
    cosf = dt_("cosf", [128, T], F32, kind="ExternalInput").ap()
    sinsg = dt_("sinsg", [128, T], F32, kind="ExternalInput").ap()
    dtb = dt_("dtb", [32, 1], F32, kind="ExternalInput").ap()
    alog = dt_("alog", [32, 1], F32, kind="ExternalInput").ap()
    gnwb = dt_("gnwb", [128, 128], F32, kind="ExternalInput").ap()
    cb = dt_("cb", [128, 32], F32, kind="ExternalInput").ap()
    onehot = dt_("onehot", [32, 32 * 128], F32, kind="ExternalInput").ap()
    id32 = dt_("id32", [32, 32], F32, kind="ExternalInput").ap()
    iden = dt_("iden", [128, 128], BF16, kind="ExternalInput").ap()
    idenf = dt_("idenf", [128, 128], F32, kind="ExternalInput").ap()
    maskA = dt_("maskA", [128, 128], BF16, kind="ExternalInput").ap()
    maskB = dt_("maskB", [128, 128], BF16, kind="ExternalInput").ap()
    w1s = dt_("w1s", [HIDR, DM], F32, kind="ExternalInput").ap()
    w2s = dt_("w2s", [HIDR, DM], F32, kind="ExternalInput").ap()
    w3s = dt_("w3s", [W3R, HID_PAD], F32, kind="ExternalInput").ap()
    out = dt_("out", [TOWN, DM], F32, kind="ExternalOutput").ap()
    if dbg:
        o_x2 = dt_("o_x2", [TOWN, DM], F32, kind="ExternalOutput").ap()
        o_dn = dt_("o_dn", [TOWN, DM], F32, kind="ExternalOutput").ap()
        o_ag = dt_("o_ag", [NRANK * 128, 128], BF16, kind="ExternalOutput").ap()
        o_h = dt_("o_h", [128, 512], BF16, kind="ExternalOutput").ap()

    NT1 = (HIDR // 128) * (DM // 128)
    NT3 = NSLOT * (W3R // 128)
    NTS = 2 * NT1 + NT3
    ag_in = dt_("ag_in", [NTS * 128, 128], BF16)
    ag_out = dt_("ag_out", [NRANK * NTS * 128, 128], BF16)
    xbf = dt_("xbf", [T, DM], BF16)
    kv_sp = dt_("kv_sp", [2 * nkvh * T, 128], F32)
    y_sp = dt_("y_sp", [nh * TOWN, 128], BF16)
    attn_tm = dt_("attn_tm", [TOWN, DM], F32)
    x2sp = dt_("x2sp", [TOWN, DM], F32)
    x2bf = dt_("x2bf", [TOWN, DM], BF16)
    dn_part = dt_("dn_part", [DM, TOWN], F32)
    dn_tm = dt_("dn_tm", [TOWN, DM], F32)
    glb = dt_("glb", [32, NCH], F32)

    def ag1_off(wi, m, c):
        return (wi * NT1 + m * 32 + c) * 128

    def ag3_off(s, cl):
        return (2 * NT1 + s * 4 + cl) * 128

    def agout_off(rank, local):
        return rank * NTS * 128 + local

    est = ExitStack()
    with est:
        e = est.enter_context
        sCL = Sem(est, nc, "sCL")
        sXC = Sem(est, nc, "sXC")
        sXL = Sem(est, nc, "sXL")
        sWL = Sem(est, nc, "sWL")
        sPE = Sem(est, nc, "sPE")
        sDV = Sem(est, nc, "sDV")
        sAC = Sem(est, nc, "sAC")
        sDQL = Sem(est, nc, "sDQL")
        sDQV = Sem(est, nc, "sDQV")
        sDQA = Sem(est, nc, "sDQA")
        sDQS = Sem(est, nc, "sDQS")
        sAG = Sem(est, nc, "sAG")
        sSP = Sem(est, nc, "sSP")
        sFW = Sem(est, nc, "sFW")
        sST = Sem(est, nc, "sST")
        sOUT = Sem(est, nc, "sOUT")

        block = e(nc.Block())

        @block.gpsimd
        def _(_g):
            V, SC, TE, SY, GP = nc.vector, nc.scalar, nc.tensor, nc.sync, nc.gpsimd
            with ExitStack() as top:
                et = top.enter_context
                c_cb = et(nc.sbuf_tensor("c_cb", [128, 32], F32))
                c_iden = et(nc.sbuf_tensor("c_iden", [128, 128], BF16))
                c_idenf = et(nc.sbuf_tensor("c_idenf", [128, 128], F32))
                x_fm = et(nc.sbuf_tensor("x_fm", [128, 32 * TOWN], BF16))
                pA = et(nc.psum_tensor("pA", [128, 512], F32))
                pB = et(nc.psum_tensor("pB", [128, 512], F32))
                pC = et(nc.psum_tensor("pC", [128, 512], F32))
                pD = et(nc.psum_tensor("pD", [128, 512], F32))
                pE = et(nc.psum_tensor("pE", [128, 512], F32))
                pF = et(nc.psum_tensor("pF", [128, 512], F32))
                pU = [et(nc.psum_tensor(f"pU{i}", [128, 512], F32)) for i in range(2)]

                for dst, src in [(c_cb, cb), (c_iden, iden), (c_idenf, idenf)]:
                    sCL.inc(SY.dma_start(out=dst[:], in_=src), 16)
                for i in range(2):
                    sXC.inc(GP.dma_start(out=xbf.ap()[i * TOWN:(i + 1) * TOWN, :],
                                         in_=x_all[i * TOWN:(i + 1) * TOWN, :],
                                         max_dma_last_dim=8192), 16)

                def wait_engines(eng, sems):
                    for s in sems:
                        eng.wait_ge(s.h, s.n)

                def mkinc(sem, eng):
                    def f(instr, k=1):
                        sem.inc(instr, k)
                        if strict:
                            eng.wait_ge(sem.h, sem.n)
                    return f
                def barrier_all(sems):
                    for eng in (V, SC, TE, SY, GP):
                        for s in sems:
                            eng.wait_ge(s.h, s.n)

                dv = mkinc(sDV, nc.vector)
                ac = mkinc(sAC, nc.scalar)
                pe = mkinc(sPE, nc.tensor)
                dqv = mkinc(sDQV, nc.vector)
                dqa = mkinc(sDQA, nc.scalar)

                def load_w_slice(dstbuf, wap, col0, nchunks):
                    GP.wait_ge(sPE.h, sPE.n)
                    sWL.inc(GP.dma_start(
                        out=dstbuf.ap().rearrange("p (c k) -> p c k", k=128)[:, 0:nchunks, :],
                        in_=wap[:, col0:col0 + 128].rearrange("(c p) k -> p c k", p=128)), 16)

                with ExitStack() as attn:
                    ea = attn.enter_context
                    c_cos = ea(nc.sbuf_tensor("c_cos", [128, T], F32))
                    c_sin = ea(nc.sbuf_tensor("c_sin", [128, T], F32))
                    c_oh = ea(nc.sbuf_tensor("c_oh", [32, 32 * 128], F32))
                    c_id32 = ea(nc.sbuf_tensor("c_id32", [32, 32], F32))
                    c_mA = ea(nc.sbuf_tensor("c_mA", [128, 128], BF16))
                    c_mB = ea(nc.sbuf_tensor("c_mB", [128, 128], BF16))
                    c_gnw = ea(nc.sbuf_tensor("c_gnw", [128, 128], F32))
                    c_dtb = ea(nc.sbuf_tensor("c_dtb", [32, 1], F32))
                    c_alog = ea(nc.sbuf_tensor("c_alog", [32, 1], F32))
                    c_expA = ea(nc.sbuf_tensor("c_expA", [32, 1], F32))
                    c_win = ea(nc.sbuf_tensor("c_win", [128, 32 * 32], BF16))
                    gt = ea(nc.sbuf_tensor("gt", [32, T], F32))
                    ngt = ea(nc.sbuf_tensor("ngt", [32, T], F32))
                    zrow = ea(nc.sbuf_tensor("zrow", [32, 128], F32))
                    cd_t = ea(nc.sbuf_tensor("cd_t", [128, 32 * NCH], F32))
                    dttm = ea(nc.sbuf_tensor("dttm", [128, 32 * NCH], F32))
                    expgtm = ea(nc.sbuf_tensor("expgtm", [128, 32 * NCH], F32))
                    glrow = ea(nc.sbuf_tensor("glrow", [32, NCH], F32))
                    DQF = 512
                    dq_w = ea(nc.sbuf_tensor("dq_w", [128, DQF], F32))
                    dq_nm = ea(nc.sbuf_tensor("dq_nm", [128, DQF], F32))
                    dq_pl = [ea(nc.sbuf_tensor(f"dq_pl{i}", [128, DQF], BF16)) for i in range(2)]
                    dq_am = ea(nc.sbuf_tensor("dq_am", [128, DQF // 64], F32))
                    dq_amc = ea(nc.sbuf_tensor("dq_amc", [128, DQF // 64], F32))
                    dq_ram = ea(nc.sbuf_tensor("dq_ram", [128, DQF // 64], F32))
                    dq_o = ea(nc.sbuf_tensor("dq_o", [128, DQF], BF16))
                    dq_tr = ea(nc.sbuf_tensor("dq_tr", [128, DQF], BF16))

                    for dst, src in [(c_cos, cosf), (c_sin, sinsg), (c_oh, onehot),
                                     (c_id32, id32), (c_mA, maskA), (c_mB, maskB),
                                     (c_gnw, gnwb), (c_dtb, dtb), (c_alog, alog)]:
                        sCL.inc(SY.dma_start(out=dst[:], in_=src), 16)
                    sWL.inc(GP.dma_start(
                        out=c_win.ap().rearrange("p (c k) -> p c k", k=32),
                        in_=winT.rearrange("(c p) k -> p c k", p=128)), 16)
                    wait_engines(SC, [sCL])
                    ac(SC.activation(out=c_expA[:], in_=c_alog[:], func=AF.Exp))
                    wait_engines(V, [sCL])
                    dv(V.memset(zrow[:], 0.0))

                    dq_list = []
                    for wi, wap in ((0, w1s), (1, w2s)):
                        for rb in range(HIDR // 128):
                            for cq in range(DM // DQF):
                                src = wap[rb * 128:(rb + 1) * 128, cq * DQF:(cq + 1) * DQF]
                                dests = [ag1_off(wi, rb, cq * (DQF // 128) + j) for j in range(DQF // 128)]
                                dq_list.append((src, dests))
                    for rb in range(W3R // 128):
                        for cq in range(HID_PAD // DQF):
                            src = w3s[rb * 128:(rb + 1) * 128, cq * DQF:(cq + 1) * DQF]
                            dests = [ag3_off(cq * (DQF // 128) + j, rb) for j in range(DQF // 128)]
                            dq_list.append((src, dests))
                    NDQ = len(dq_list)
                    dq_i = [0]

                    def dq_pump(k):
                        for _ in range(k):
                            ti = dq_i[0]
                            if ti >= NDQ:
                                return
                            dq_i[0] += 1
                            src, dests = dq_list[ti]
                            SY.wait_ge(sDQV.h, sDQV.n)
                            sDQL.inc(SY.dma_start(out=dq_w[:], in_=src), 16)
                            V.wait_ge(sDQL.h, sDQL.n)
                            V.wait_ge(sDQS.h, sDQS.n)
                            w3v = dq_w.ap().rearrange("p (b k) -> p b k", k=64)
                            dqv(V.tensor_reduce(out=dq_am[:], in_=w3v, axis=AX.X,
                                                     op=ALU.max, apply_absolute_value=True))
                            V.wait_ge(sDQV.h, sDQV.n)
                            dqv(V.tensor_scalar(out=dq_amc[:], in0=dq_am[:],
                                                     scalar1=1e-30, scalar2=None, op0=ALU.add))
                            V.wait_ge(sDQV.h, sDQV.n)
                            dqv(V.reciprocal(out=dq_ram[:], in_=dq_amc[:]))
                            V.wait_ge(sDQV.h, sDQV.n)
                            dqv(V.tensor_tensor(
                                out=dq_nm.ap().rearrange("p (b k) -> p b k", k=64),
                                in0=w3v,
                                in1=dq_ram.ap().unsqueeze(2).broadcast_to([128, DQF // 64, 64]),
                                op=ALU.mult))
                            norm_done = sDQV.n
                            chain0 = norm_done
                            plane0 = sDQA.n
                            SC.wait_ge(sDQV.h, norm_done)
                            for i in range(15):
                                if i >= 2:
                                    SC.wait_ge(sDQV.h, chain0 + (i - 1) * (DQF // 512))
                                dqa(SC.activation(out=dq_pl[i % 2][:], in_=dq_nm[:],
                                                       func=AF.Sigmoid,
                                                       bias=c_cb[:, CB_SIG0 + i:CB_SIG0 + i + 1],
                                                       scale=SCALE_BIG))
                            for i in range(15):
                                V.wait_ge(sDQA.h, plane0 + i + 1)
                                for cs in range(DQF // 512):
                                    sl = slice(cs * 512, (cs + 1) * 512)
                                    if i == 0:
                                        r = V.tensor_scalar(out=pU[cs][:], in0=dq_pl[0][:, sl],
                                                            scalar1=float(NF4_DELTAS[0]),
                                                            scalar2=float(NF4_CODE[0]),
                                                            op0=ALU.mult, op1=ALU.add)
                                    else:
                                        r = V.scalar_tensor_tensor(out=pU[cs][:],
                                                                   in0=dq_pl[i % 2][:, sl],
                                                                   scalar=float(NF4_DELTAS[i]),
                                                                   in1=pU[cs][:],
                                                                   op0=ALU.mult, op1=ALU.add)
                                    dqv(r)
                            V.wait_ge(sDQV.h, sDQV.n)
                            for cs in range(DQF // 512):
                                sl = slice(cs * 512, (cs + 1) * 512)
                                r = V.tensor_tensor(
                                    out=dq_o[:, sl].rearrange("p (b k) -> p b k", k=64),
                                    in0=pU[cs].ap().rearrange("p (b k) -> p b k", k=64),
                                    in1=dq_am[:, cs * 8:(cs + 1) * 8].unsqueeze(2)
                                          .broadcast_to([128, 8, 64]),
                                    op=ALU.mult)
                                dqv(r)
                            wait_engines(TE, [sAC, sDV])
                            TE.wait_ge(sDQV.h, sDQV.n)
                            for j in range(DQF // 128):
                                tr = TE.transpose(
                                    pF.ap().bitcast(BF16)[:, j * 128:(j + 1) * 128],
                                    dq_o[:, j * 128:(j + 1) * 128], c_iden[:])
                            pe(tr)
                            SC.wait_ge(sPE.h, sPE.n)
                            SC.wait_ge(sDQS.h, sDQS.n)
                            ac(SC.copy(out=dq_tr[:], in_=pF.ap().bitcast(BF16)[:, 0:DQF]))
                            SY.wait_ge(sAC.h, sAC.n)
                            runs = []
                            for dof in dests:
                                if runs and dof == runs[-1][1] + 128:
                                    runs[-1][1] = dof
                                else:
                                    runs.append([dof, dof])
                            off_j = 0
                            for lo, hi in runs:
                                npc = (hi - lo) // 128 + 1
                                sDQS.inc(SY.dma_start(
                                    out=ag_in[lo:lo + npc * 128, :]
                                    .rearrange("(j p) k -> p j k", p=128),
                                    in_=dq_tr[:, off_j * 128:(off_j + npc) * 128]
                                    .rearrange("p (j k) -> p j k", k=128)), 16)
                                off_j += npc

                    with ExitStack() as p1:
                        e1 = p1.enter_context
                        dtT = e1(nc.sbuf_tensor("dtT", [32, T], F32))
                        gsrc = e1(nc.sbuf_tensor("gsrc", [32, T], F32))
                        xfs = e1(nc.sbuf_tensor("xfs", [128, 32 * TB], BF16))
                        wb_k = e1(nc.sbuf_tensor("wb_k", [128, 32 * 128], BF16))
                        wb_v = e1(nc.sbuf_tensor("wb_v", [128, 32 * 128], BF16))
                        rp_a = e1(nc.sbuf_tensor("rp_a", [128, TB], F32))
                        rp_r = e1(nc.sbuf_tensor("rp_r", [128, TB], F32))
                        rp_b = e1(nc.sbuf_tensor("rp_b", [128, TB], F32))
                        khat = e1(nc.sbuf_tensor("khat", [128, TB], F32))
                        sp_st = e1(nc.sbuf_tensor("sp_st", [128, TB], F32))
                        dz = e1(nc.sbuf_tensor("dz", [32, TB], F32))
                        da1 = e1(nc.sbuf_tensor("da1", [32, TB], F32))
                        da2 = e1(nc.sbuf_tensor("da2", [32, TB], F32))

                        SY.wait_ge(sXC.h, sXC.n)
                        for c in range(32):
                            sXL.inc(SY.dma_start(
                                out=x_fm[:, c * TOWN:(c + 1) * TOWN],
                                in_=xbf.ap()[TOWN:2 * TOWN, c * 128:(c + 1) * 128],
                                transpose=True), 16)

                        for half in range(2):
                            for tb in range(TOWN // TB):
                                tg = half * (TOWN // TB) + tb
                                t0 = tg * TB
                                if half == 0:
                                    SY.wait_ge(sPE.h, sPE.n)
                                    for c in range(32):
                                        sXL.inc(SY.dma_start(
                                            out=xfs[:, c * TB:(c + 1) * TB],
                                            in_=xbf.ap()[t0:t0 + TB, c * 128:(c + 1) * 128],
                                            transpose=True), 16)

                                    def xr(c):
                                        return xfs[:, c * TB:(c + 1) * TB]
                                else:
                                    def xr(c):
                                        return x_fm[:, c * TOWN + tb * TB:
                                                    c * TOWN + tb * TB + TB]
                                tsl = slice(t0, t0 + TB)
                                wait_engines(TE, [sXL, sCL, sWL, sAC, sDV])
                                for c in range(32):
                                    mm = TE.matmul(pD[0:32, 0:TB], c_win[:, c * 32:(c + 1) * 32],
                                                   xr(c), start=(c == 0), stop=(c == 31))
                                pe(mm)
                                SC.wait_ge(sPE.h, sPE.n)
                                ac(SC.activation(out=dz[:], in_=pD[0:32, 0:TB],
                                                      func=AF.Identity, bias=c_dtb[:]))
                                SC.wait_ge(sAC.h, sAC.n)
                                ac(SC.activation(out=da1[:], in_=dz[:], func=AF.Abs))
                                SC.wait_ge(sAC.h, sAC.n)
                                ac(SC.activation(out=da1[:], in_=da1[:], func=AF.Exp,
                                                      scale=-1.0))
                                SC.wait_ge(sAC.h, sAC.n)
                                ac(SC.activation(out=da1[:], in_=da1[:], func=AF.Ln,
                                                      bias=c_cb[0:32, CB_ONE:CB_ONE + 1]))
                                wait_engines(V, [sAC])
                                dv(V.tensor_scalar(out=da2[:], in0=dz[:], scalar1=0.0,
                                                        scalar2=None, op0=ALU.max))
                                V.wait_ge(sDV.h, sDV.n)
                                dv(V.tensor_tensor(out=dtT[:, tsl], in0=da2[:],
                                                        in1=da1[:], op=ALU.add))
                                for kv in range(nkvh):
                                    for which in ("k", "v"):
                                        wb = wb_k if which == "k" else wb_v
                                        load_w_slice(wb, wkT if which == "k" else wvT,
                                                     kv * 128, 32)
                                        wait_engines(TE, [sWL, sAC, sDV])
                                        for c in range(32):
                                            mm = TE.matmul(pA[:, 0:TB],
                                                           wb[:, c * 128:(c + 1) * 128],
                                                           xr(c), start=(c == 0), stop=(c == 31))
                                        pe(mm)
                                        V.wait_ge(sPE.h, sPE.n)
                                        if which == "k":
                                            dv(V.tensor_tensor(out=rp_a[:], in0=pA[:, 0:TB],
                                                                    in1=c_cos[:, tsl], op=ALU.mult))
                                            dv(V.tensor_copy(out=rp_r[0:64, :],
                                                                  in_=pA[64:128, 0:TB]))
                                            dv(V.tensor_copy(out=rp_r[64:128, :],
                                                                  in_=pA[0:64, 0:TB]))
                                            V.wait_ge(sDV.h, sDV.n)
                                            dv(V.tensor_tensor(out=rp_b[:], in0=rp_r[:],
                                                                    in1=c_sin[:, tsl], op=ALU.mult))
                                            V.wait_ge(sDV.h, sDV.n)
                                            dv(V.tensor_tensor(out=khat[:], in0=rp_a[:],
                                                                    in1=rp_b[:], op=ALU.add))
                                            row0 = kv * T + t0
                                        else:
                                            dv(V.tensor_copy(out=khat[:], in_=pA[:, 0:TB]))
                                            row0 = nkvh * T + kv * T + t0
                                        wait_engines(TE, [sDV, sAC])
                                        for j in range(TB // 128):
                                            tr = TE.transpose(
                                                pB[:, j * 128:(j + 1) * 128],
                                                khat[:, j * 128:(j + 1) * 128], c_idenf[:])
                                        pe(tr)
                                        SC.wait_ge(sPE.h, sPE.n)
                                        SC.wait_ge(sSP.h, sSP.n)
                                        ac(SC.copy(out=sp_st[:], in_=pB[:, 0:TB]))
                                        SY.wait_ge(sAC.h, sAC.n)
                                        sSP.inc(SY.dma_start(
                                            out=kv_sp[row0:row0 + TB, :]
                                            .rearrange("(j p) k -> p j k", p=128),
                                            in_=sp_st[:].rearrange("p (j k) -> p j k", k=128)), 16)
                                    dq_pump(1)

                        wait_engines(V, [sDV, sAC])
                        dv(V.tensor_scalar(out=gsrc[:], in0=dtT[:], scalar1=c_expA[:],
                                                scalar2=-1.0, op0=ALU.mult, op1=ALU.mult))
                        V.wait_ge(sDV.h, sDV.n)
                        for n in range(NCH):
                            csl = slice(n * 128, (n + 1) * 128)
                            dv(V.tensor_tensor_scan(out=gt[:, csl], data0=gsrc[:, csl],
                                                         data1=zrow[:], initial=0.0,
                                                         op0=ALU.add, op1=ALU.add))
                        V.wait_ge(sDV.h, sDV.n)
                        dv(V.tensor_scalar(out=ngt[:], in0=gt[:], scalar1=-1.0,
                                                scalar2=None, op0=ALU.mult))
                        dv(V.tensor_copy(
                            out=glrow[:],
                            in_=gt.ap().rearrange("p (n k) -> p n k", k=128)[:, :, 127]))
                        V.wait_ge(sDV.h, sDV.n)
                        SY.wait_ge(sDV.h, sDV.n)
                        sST.inc(SY.dma_start(out=glb.ap(), in_=glrow[:]), 16)
                        SY.wait_ge(sST.h, sST.n)
                        sST.inc(SY.dma_start(
                            out=cd_t[:],
                            in_=glb.ap().rearrange("h n -> (h n)").unsqueeze(0)
                            .broadcast_to([128, 32 * NCH])), 16)
                        SC.wait_ge(sST.h, sST.n)
                        ac(SC.activation(out=cd_t[:], in_=cd_t[:], func=AF.Exp))
                        for n in range(NCH):
                            csl = slice(n * 128, (n + 1) * 128)
                            wait_engines(TE, [sDV, sAC])
                            TE.matmul(pD[:, 0:32], dtT[:, csl], c_id32[:], start=True, stop=True)
                            mm = TE.matmul(pD[:, 64:96], gt[:, csl], c_id32[:], start=True, stop=True)
                            pe(mm)
                            SC.wait_ge(sPE.h, sPE.n)
                            ac(SC.copy(out=dttm[:, n * 32:(n + 1) * 32], in_=pD[:, 0:32]))
                            ac(SC.activation(out=expgtm[:, n * 32:(n + 1) * 32],
                                                  in_=pD[:, 64:96], func=AF.Exp))
                        dq_pump(4)


                    barrier_all([sSP, sST, sDQS, sXL, sDV, sAC, sPE])
                    with ExitStack() as p2:
                        e2 = p2.enter_context
                        q_fm = e2(nc.sbuf_tensor("q_fm", [128, gq * TOWN], BF16))
                        sgate = e2(nc.sbuf_tensor("sgate", [128, NOG * gq * 128], BF16))
                        kT_sb = e2(nc.sbuf_tensor("kT_sb", [128, NCH * 128], F32))
                        vT_sb = e2(nc.sbuf_tensor("vT_sb", [128, NCH * 128], F32))
                        kf_sb = e2(nc.sbuf_tensor("kf_sb", [128, 128], BF16))
                        S_st = [e2(nc.sbuf_tensor(f"S_st{i}", [128, gq * 128], F32))
                                for i in range(2)]
                        S_bf = e2(nc.sbuf_tensor("S_bf", [128, 128], BF16))
                        e_sb = e2(nc.sbuf_tensor("e_sb", [128, 128], F32))
                        edec = e2(nc.sbuf_tensor("edec", [128, 1], F32))
                        xc_sb = e2(nc.sbuf_tensor("xc_sb", [128, 128], BF16))
                        kd_sb = e2(nc.sbuf_tensor("kd_sb", [128, 128], BF16))
                        s2_sb = e2(nc.sbuf_tensor("s2_sb", [128, 128], BF16))
                        ytmp = e2(nc.sbuf_tensor("ytmp", [128, 128], F32))
                        y_sb = [e2(nc.sbuf_tensor(f"y_sb{i}", [128, 128], F32))
                                for i in range(4)]
                        ss4 = e2(nc.sbuf_tensor("ss4", [128, 4], F32))
                        rstd4 = e2(nc.sbuf_tensor("rstd4", [128, 4], F32))
                        ydump = e2(nc.sbuf_tensor("ydump", [128, 128], BF16))
                        yg_sb = e2(nc.sbuf_tensor("yg_sb", [128, 128], BF16))
                        sil_t = e2(nc.sbuf_tensor("sil_t", [128, 128], F32))
                        wb_q = e2(nc.sbuf_tensor("wb_q", [128, 32 * 128], BF16))
                        wb_g = e2(nc.sbuf_tensor("wb_g", [128, 32 * 128], BF16))
                        rp2a = e2(nc.sbuf_tensor("rp2a", [128, 512], F32))
                        rp2r = e2(nc.sbuf_tensor("rp2r", [128, 512], F32))
                        rp2b = e2(nc.sbuf_tensor("rp2b", [128, 512], F32))

                        for kv in range(nkvh):
                            for hh in range(gq):
                                h = kv * gq + hh
                                load_w_slice(wb_q, wqT, h * 128, 32)
                                wait_engines(TE, [sWL, sDV, sAC])
                                for tb2 in range(2):
                                    tsl2 = slice(TOWN + tb2 * 512, TOWN + (tb2 + 1) * 512)
                                    wait_engines(TE, [sDV])
                                    for c in range(32):
                                        mm = TE.matmul(
                                            pA[:], wb_q[:, c * 128:(c + 1) * 128],
                                            x_fm[:, c * TOWN + tb2 * 512:c * TOWN + tb2 * 512 + 512],
                                            start=(c == 0), stop=(c == 31))
                                    pe(mm)
                                    V.wait_ge(sPE.h, sPE.n)
                                    dv(V.tensor_tensor(out=rp2a[:], in0=pA[:],
                                                            in1=c_cos[:, tsl2], op=ALU.mult))
                                    dv(V.tensor_copy(out=rp2r[0:64, :], in_=pA[64:128, :]))
                                    dv(V.tensor_copy(out=rp2r[64:128, :], in_=pA[0:64, :]))
                                    V.wait_ge(sDV.h, sDV.n)
                                    dv(V.tensor_tensor(out=rp2b[:], in0=rp2r[:],
                                                            in1=c_sin[:, tsl2], op=ALU.mult))
                                    V.wait_ge(sDV.h, sDV.n)
                                    dv(V.tensor_tensor(
                                        out=q_fm[:, hh * TOWN + tb2 * 512:hh * TOWN + tb2 * 512 + 512],
                                        in0=rp2a[:], in1=rp2b[:], op=ALU.add))
                                load_w_slice(wb_g, wgT, h * 128, 32)
                                wait_engines(TE, [sWL, sAC])
                                for g in range(NOG):
                                    wait_engines(TE, [sAC])
                                    for c in range(32):
                                        mm = TE.matmul(
                                            pB[:, 0:128],
                                            x_fm[:, c * TOWN + g * 128:c * TOWN + (g + 1) * 128],
                                            wb_g[:, c * 128:(c + 1) * 128],
                                            start=(c == 0), stop=(c == 31))
                                    pe(mm)
                                    SC.wait_ge(sPE.h, sPE.n)
                                    ac(SC.copy(
                                        out=sgate[:, (g * gq + hh) * 128:(g * gq + hh + 1) * 128],
                                        in_=pB[:, 0:128]))
                                dq_pump(1)
                            SY.wait_ge(sSP.h, sSP.n)
                            SY.wait_ge(sDV.h, sDV.n)
                            SY.wait_ge(sPE.h, sPE.n)
                            sXL.inc(SY.dma_start(
                                out=kT_sb.ap().rearrange("p (n k) -> p n k", k=128),
                                in_=kv_sp[kv * T:(kv + 1) * T, :]
                                .rearrange("(n p) k -> p n k", p=128)), 16)
                            sXL.inc(SY.dma_start(
                                out=vT_sb.ap().rearrange("p (n k) -> p n k", k=128),
                                in_=kv_sp[nkvh * T + kv * T:nkvh * T + (kv + 1) * T, :]
                                .rearrange("(n p) k -> p n k", p=128)), 16)
                            wait_engines(V, [sXL, sAC])
                            dv(V.memset(S_st[0][:], 0.0))

                            for n in range(NCH):
                                own = n >= OWN0
                                csl = slice(n * 128, (n + 1) * 128)
                                par = n % 2
                                if own:
                                    wait_engines(TE, [sDV, sAC])
                                    tr = TE.transpose(pF[:, 0:128],
                                                      kT_sb[:, csl], c_idenf[:])
                                    pe(tr)
                                    V.wait_ge(sPE.h, sPE.n)
                                    dv(V.tensor_copy(out=kf_sb[:], in_=pF[:, 0:128]))
                                for hh in range(gq):
                                    h = kv * gq + hh
                                    wait_engines(V, [sAC])
                                    dv(V.tensor_scalar(
                                        out=xc_sb[:], in0=vT_sb[:, csl],
                                        scalar1=dttm[:, n * 32 + h:n * 32 + h + 1],
                                        scalar2=None, op0=ALU.mult))
                                    ohsl = c_oh[:, h * 128:(h + 1) * 128]
                                    ohcol = c_oh[:, h * 128:h * 128 + 1]
                                    if own:
                                        wait_engines(TE, [sAC, sDV])
                                        TE.matmul(pE[:, 0:128], ohsl, gt[:, csl],
                                                  start=True, stop=False)
                                        TE.matmul(pE[:, 0:128], ngt[:, csl], ohsl,
                                                  start=False, stop=False)
                                        TE.matmul(pE[:, 0:128], c_mA[:], c_mB[:],
                                                  start=False, stop=True)
                                        mm = TE.matmul(pD[:, 0:128], kf_sb[:],
                                                       q_fm[:, hh * TOWN + (n - OWN0) * 128:
                                                            hh * TOWN + (n - OWN0 + 1) * 128],
                                                       start=True, stop=True)
                                        pe(mm)
                                        SC.wait_ge(sPE.h, sPE.n)
                                        SC.wait_ge(sDV.h, sDV.n)
                                        ac(SC.activation(out=e_sb[:], in_=pE[:, 0:128],
                                                              func=AF.Exp))
                                        wait_engines(V, [sAC])
                                        dv(V.tensor_tensor(out=s2_sb[:], in0=pD[:, 0:128],
                                                                in1=e_sb[:], op=ALU.mult))
                                        dv(V.tensor_scalar(out=kd_sb[:], in0=kT_sb[:, csl],
                                                                scalar1=e_sb[:, 127:128],
                                                                scalar2=None, op0=ALU.mult))
                                        dv(V.tensor_copy(
                                            out=S_bf[:], in_=S_st[par][:, hh * 128:(hh + 1) * 128]))
                                        V.wait_ge(sDV.h, sDV.n)
                                        wait_engines(TE, [sDV, sAC])
                                        TE.matmul(pB[:, 0:128],
                                                  q_fm[:, hh * TOWN + (n - OWN0) * 128:
                                                       hh * TOWN + (n - OWN0 + 1) * 128],
                                                  S_bf[:], start=True, stop=True)
                                        TE.matmul(pA[:, 0:128], s2_sb[:], xc_sb[:],
                                                  start=True, stop=True)
                                        mm = TE.matmul(pC[:, 0:128], kd_sb[:], xc_sb[:],
                                                       start=True, stop=True)
                                        pe(mm)
                                        SC.wait_ge(sPE.h, sPE.n)
                                        SC.wait_ge(sDV.h, sDV.n)
                                        ac(SC.activation(
                                            out=ytmp[:], in_=pB[:, 0:128], func=AF.Identity,
                                            scale=expgtm[:, n * 32 + h:n * 32 + h + 1]))
                                        wait_engines(V, [sAC])
                                        dv(V.tensor_tensor(out=y_sb[hh][:], in0=pA[:, 0:128],
                                                                in1=ytmp[:], op=ALU.add))
                                        V.wait_ge(sDV.h, sDV.n)
                                        SC.wait_ge(sDV.h, sDV.n)
                                        ac(SC.activation(out=ydump[:], in_=y_sb[hh][:],
                                                              func=AF.Square,
                                                              accum_out=ss4[:, hh:hh + 1]))
                                        dv(V.scalar_tensor_tensor(
                                            out=S_st[1 - par][:, hh * 128:(hh + 1) * 128],
                                            in0=S_st[par][:, hh * 128:(hh + 1) * 128],
                                            scalar=cd_t[:, h * NCH + n:h * NCH + n + 1],
                                            in1=pC[:, 0:128], op0=ALU.mult, op1=ALU.add))
                                    else:
                                        wait_engines(TE, [sAC, sDV])
                                        TE.matmul(pE[:, 0:1], ohsl,
                                                  gt[:, n * 128 + 127:n * 128 + 128],
                                                  start=True, stop=False)
                                        mm = TE.matmul(pE[:, 0:1], ngt[:, csl], ohcol,
                                                       start=False, stop=True)
                                        pe(mm)
                                        SC.wait_ge(sPE.h, sPE.n)
                                        SC.wait_ge(sDV.h, sDV.n)
                                        ac(SC.activation(out=edec[:], in_=pE[:, 0:1],
                                                              func=AF.Exp))
                                        wait_engines(V, [sAC])
                                        dv(V.tensor_scalar(out=kd_sb[:], in0=kT_sb[:, csl],
                                                                scalar1=edec[:], scalar2=None,
                                                                op0=ALU.mult))
                                        V.wait_ge(sDV.h, sDV.n)
                                        wait_engines(TE, [sDV])
                                        mm = TE.matmul(pC[:, 0:128], kd_sb[:], xc_sb[:],
                                                       start=True, stop=True)
                                        pe(mm)
                                        V.wait_ge(sPE.h, sPE.n)
                                        dv(V.scalar_tensor_tensor(
                                            out=S_st[1 - par][:, hh * 128:(hh + 1) * 128],
                                            in0=S_st[par][:, hh * 128:(hh + 1) * 128],
                                            scalar=cd_t[:, h * NCH + n:h * NCH + n + 1],
                                            in1=pC[:, 0:128], op0=ALU.mult, op1=ALU.add))
                                if own:
                                    g = n - OWN0
                                    SC.wait_ge(sDV.h, sDV.n)
                                    SC.wait_ge(sAC.h, sAC.n)
                                    ac(SC.activation(out=rstd4[:], in_=ss4[:], func=AF.Ln,
                                                          scale=1.0 / 128.0,
                                                          bias=c_cb[:, CB_EPS:CB_EPS + 1]))
                                    SC.wait_ge(sAC.h, sAC.n)
                                    ac(SC.activation(out=rstd4[:], in_=rstd4[:],
                                                          func=AF.Exp, scale=-0.5))
                                    wait_engines(V, [sAC])
                                    for hh in range(gq):
                                        h = kv * gq + hh
                                        SC.wait_ge(sDV.h, sDV.n)
                                        ac(SC.activation(
                                            out=sil_t[:],
                                            in_=sgate[:, (g * gq + hh) * 128:
                                                      (g * gq + hh + 1) * 128],
                                            func=AF.Silu))
                                        dv(V.scalar_tensor_tensor(
                                            out=ytmp[:], in0=y_sb[hh][:],
                                            scalar=rstd4[:, hh:hh + 1], in1=c_gnw[:],
                                            op0=ALU.mult, op1=ALU.mult))
                                        V.wait_ge(sDV.h, sDV.n)
                                        V.wait_ge(sAC.h, sAC.n)
                                        V.wait_ge(sSP.h, sSP.n)
                                        dv(V.tensor_tensor(
                                            out=yg_sb[:], in0=ytmp[:],
                                            in1=sil_t[:],
                                            op=ALU.mult))
                                        SY.wait_ge(sDV.h, sDV.n)
                                        sSP.inc(SY.dma_start(
                                            out=y_sp[h * TOWN + g * 128:h * TOWN + (g + 1) * 128, :],
                                            in_=yg_sb[:]), 16)
                            dq_pump(3)

                        dq_pump(NDQ)
                        if do_ag:
                            GP.wait_ge(sDQS.h, sDQS.n)
                            cc = GP.collective_compute(
                                "AllGather", ALU.bypass, replica_groups=[list(range(NRANK))],
                                ins=[ag_in.ap().opt()], outs=[ag_out.ap().opt()])
                            sAG.inc(cc, 1)
                        elif do_ffn:
                            GP.wait_ge(sDQS.h, sDQS.n)
                            for r in range(NRANK):
                                d = GP.dma_start(out=ag_out.ap()[r * NTS * 128:(r + 1) * NTS * 128, :],
                                                 in_=ag_in.ap())
                                sAG.inc(d, 16)

                barrier_all([sSP, sST, sDQS, sXL, sDV, sAC, sPE, sAG])
                with ExitStack() as p3:
                    e3 = p3.enter_context
                    yb = e3(nc.sbuf_tensor("yb", [128, nh * TOWN], BF16))
                    wb_o = e3(nc.sbuf_tensor("wb_o", [128, nh * 128], BF16))
                    wo_st = e3(nc.sbuf_tensor("wo_st", [128, 512], F32))
                    wo_o = e3(nc.sbuf_tensor("wo_o", [128, 512], F32))
                    SY.wait_ge(sSP.h, sSP.n)
                    for hec in range(nh):
                        sXL.inc(SY.dma_start(
                            out=yb[:, hec * TOWN:(hec + 1) * TOWN],
                            in_=y_sp[hec * TOWN:(hec + 1) * TOWN, :],
                            transpose=True), 16)
                    for m in range(32):
                        load_w_slice(wb_o, woT, m * 128, nh)
                        for th in range(2):
                            wait_engines(TE, [sWL, sXL, sDV])
                            for hec in range(nh):
                                mm = TE.matmul(
                                    pA[:], wb_o[:, hec * 128:(hec + 1) * 128],
                                    yb[:, hec * TOWN + th * 512:hec * TOWN + th * 512 + 512],
                                    start=(hec == 0), stop=(hec == nh - 1))
                            pe(mm)
                            V.wait_ge(sPE.h, sPE.n)
                            dv(V.tensor_copy(out=wo_st[:], in_=pA[:]))
                            wait_engines(TE, [sDV, sAC])
                            for j in range(4):
                                tr = TE.transpose(pF[:, j * 128:(j + 1) * 128],
                                                  wo_st[:, j * 128:(j + 1) * 128],
                                                  c_idenf[:])
                            pe(tr)
                            V.wait_ge(sPE.h, sPE.n)
                            V.wait_ge(sST.h, sST.n)
                            dv(V.tensor_copy(out=wo_o[:], in_=pF[:]))
                            SY.wait_ge(sDV.h, sDV.n)
                            sST.inc(SY.dma_start(
                                out=attn_tm.ap()[th * 512:(th + 1) * 512,
                                                 m * 128:(m + 1) * 128]
                                .rearrange("(j p) k -> p j k", p=128),
                                in_=wo_o[:].rearrange("p (j k) -> p j k", k=128)), 16)

                barrier_all([sSP, sST, sXL, sDV, sAC, sPE])
                with ExitStack() as p4:
                    e4 = p4.enter_context
                    ln_x = e4(nc.sbuf_tensor("ln_x", [128, DM], F32))
                    ln_a = e4(nc.sbuf_tensor("ln_a", [128, DM], F32))
                    ln_b = e4(nc.sbuf_tensor("ln_b", [128, DM], BF16))
                    ln_s = e4(nc.sbuf_tensor("ln_s", [128, 8], F32))

                    def ln_pass(res_ap, add_ap, spill_f32, spill_bf):
                        SY.wait_ge(sST.h, sST.n)
                        SY.wait_ge(sOUT.h, sOUT.n)
                        sXL.inc(SY.dma_start(out=ln_x[:], in_=res_ap), 16)
                        sXL.inc(SY.dma_start(out=ln_a[:], in_=add_ap), 16)
                        wait_engines(V, [sXL, sAC])
                        dv(V.tensor_tensor(out=ln_x[:], in0=ln_x[:], in1=ln_a[:],
                                                op=ALU.add))
                        SC.wait_ge(sDV.h, sDV.n)
                        ac(SC.activation(out=ln_b[:], in_=ln_x[:], func=AF.Square,
                                              accum_out=ln_s[:, 0:1]))
                        SC.wait_ge(sAC.h, sAC.n)
                        ac(SC.activation(out=ln_b[:], in_=ln_x[:], func=AF.Identity,
                                              accum_out=ln_s[:, 1:2]))
                        ac(SC.mul(out=ln_s[:, 2:3], in_=ln_s[:, 1:2], mul=1.0 / DM))
                        ac(SC.mul(out=ln_s[:, 3:4], in_=ln_s[:, 0:1], mul=1.0 / DM))
                        wait_engines(V, [sAC])
                        dv(V.tensor_scalar(out=ln_s[:, 4:5], in0=ln_s[:, 2:3],
                                                scalar1=ln_s[:, 2:3], scalar2=None,
                                                op0=ALU.mult))
                        V.wait_ge(sDV.h, sDV.n)
                        dv(V.tensor_tensor(out=ln_s[:, 5:6], in0=ln_s[:, 3:4],
                                                in1=ln_s[:, 4:5], op=ALU.subtract))
                        SC.wait_ge(sDV.h, sDV.n)
                        ac(SC.activation(out=ln_s[:, 6:7], in_=ln_s[:, 5:6], func=AF.Ln,
                                              bias=c_cb[:, CB_EPS:CB_EPS + 1]))
                        SC.wait_ge(sAC.h, sAC.n)
                        ac(SC.activation(out=ln_s[:, 7:8], in_=ln_s[:, 6:7], func=AF.Exp,
                                              scale=-0.5))
                        wait_engines(V, [sAC])
                        dv(V.tensor_scalar(out=ln_x[:], in0=ln_x[:],
                                                scalar1=ln_s[:, 2:3], scalar2=ln_s[:, 7:8],
                                                op0=ALU.subtract, op1=ALU.mult))
                        SY.wait_ge(sDV.h, sDV.n)
                        if spill_f32 is not None:
                            sST.inc(SY.dma_start(out=spill_f32, in_=ln_x[:]), 16)
                        if spill_bf is not None:
                            dv(V.tensor_copy(out=ln_b[:], in_=ln_x[:]))
                            SY.wait_ge(sDV.h, sDV.n)
                            sST.inc(SY.dma_start(out=spill_bf, in_=ln_b[:]), 16)

                    SY.wait_ge(sST.h, sST.n)
                    for g in range(NOG):
                        ln_pass(x_all[TOWN + g * 128:TOWN + (g + 1) * 128, :],
                                attn_tm.ap()[g * 128:(g + 1) * 128, :],
                                x2sp.ap()[g * 128:(g + 1) * 128, :],
                                x2bf.ap()[g * 128:(g + 1) * 128, :] if do_ffn else None)

                    if not do_ffn:
                        for g in range(NOG):
                            SY.wait_ge(sST.h, sST.n)
                            SY.wait_ge(sOUT.h, sOUT.n)
                            sXL.inc(SY.dma_start(out=ln_x[:],
                                                 in_=x2sp.ap()[g * 128:(g + 1) * 128, :]), 16)
                            SY.wait_ge(sXL.h, sXL.n)
                            sOUT.inc(SY.dma_start(out=out[g * 128:(g + 1) * 128, :],
                                                  in_=ln_x[:]), 16)
                        SY.wait_ge(sOUT.h, sOUT.n)
                        return

                    SY.wait_ge(sST.h, sST.n)
                    SY.wait_ge(sPE.h, sPE.n)
                    for c in range(32):
                        sXL.inc(SY.dma_start(out=x_fm[:, c * TOWN:(c + 1) * TOWN],
                                             in_=x2bf.ap()[:, c * 128:(c + 1) * 128],
                                             transpose=True), 16)

                    barrier_all([sST, sXL, sDV, sAC, sPE])
                    with ExitStack() as p5:
                        e5 = p5.enter_context
                        h_fm = e5(nc.sbuf_tensor("h_fm", [128, HSL * 512], BF16))
                        fw1 = e5(nc.sbuf_tensor("fw1", [128, 32 * 128], BF16))
                        fw2 = e5(nc.sbuf_tensor("fw2", [128, 32 * 128], BF16))
                        w3b = [e5(nc.sbuf_tensor(f"w3b{i}", [128, HSL * 128], BF16))
                               for i in range(2)]
                        sg_sb = e5(nc.sbuf_tensor("sg_sb", [128, 512], F32))
                        dn_st = e5(nc.sbuf_tensor("dn_st", [128, 512], F32))
                        dn_p = e5(nc.sbuf_tensor("dn_p", [128, 512], F32))
                        dn_o = e5(nc.sbuf_tensor("dn_o", [128, 512], F32))

                        SY.wait_ge(sAG.h, sAG.n)

                        for tt in (1, 0):
                            tof = tt * 512
                            for hhf in range(2):
                                for sl_ in range(HSL):
                                    s = hhf * HSL + sl_
                                    rank, ml = s // 11, s % 11
                                    SY.wait_ge(sPE.h, sPE.n)
                                    base1 = agout_off(rank, ag1_off(0, ml, 0))
                                    base2 = agout_off(rank, ag1_off(1, ml, 0))
                                    sFW.inc(SY.dma_start(
                                        out=fw1.ap().rearrange("p (c k) -> p c k", k=128),
                                        in_=ag_out.ap()[base1:base1 + 32 * 128, :]
                                        .rearrange("(c p) k -> p c k", p=128)), 16)
                                    sFW.inc(SY.dma_start(
                                        out=fw2.ap().rearrange("p (c k) -> p c k", k=128),
                                        in_=ag_out.ap()[base2:base2 + 32 * 128, :]
                                        .rearrange("(c p) k -> p c k", p=128)), 16)
                                    wait_engines(TE, [sFW, sXL, sDV, sAC])
                                    for c in range(32):
                                        mm = TE.matmul(pA[:], fw1[:, c * 128:(c + 1) * 128],
                                                       x_fm[:, c * TOWN + tof:c * TOWN + tof + 512],
                                                       start=(c == 0), stop=(c == 31))
                                    pe(mm)
                                    for c in range(32):
                                        mm = TE.matmul(pB[:], fw2[:, c * 128:(c + 1) * 128],
                                                       x_fm[:, c * TOWN + tof:c * TOWN + tof + 512],
                                                       start=(c == 0), stop=(c == 31))
                                    pe(mm)
                                    SC.wait_ge(sPE.h, sPE.n - 1)
                                    SC.wait_ge(sDV.h, sDV.n)
                                    ac(SC.activation(out=sg_sb[:], in_=pA[:], func=AF.Silu))
                                    wait_engines(V, [sAC])
                                    V.wait_ge(sPE.h, sPE.n)
                                    dv(V.tensor_tensor(out=h_fm[:, sl_ * 512:(sl_ + 1) * 512],
                                                            in0=pB[:], in1=sg_sb[:], op=ALU.mult))
                                    if dbg and tt == 0 and hhf == 0 and sl_ == 0:
                                        SY.wait_ge(sDV.h, sDV.n)
                                        sOUT.inc(SY.dma_start(out=o_h, in_=h_fm[:, 0:512]), 16)
                                for m in range(32):
                                    rank3, cl = m // 4, m % 4
                                    wb3 = w3b[m % 2]
                                    s0 = hhf * HSL
                                    base3 = agout_off(rank3, ag3_off(s0, 0))
                                    SY.wait_ge(sPE.h, sPE.n)
                                    sFW.inc(SY.dma_start(
                                        out=wb3.ap().rearrange("p (s k) -> p s k", k=128),
                                        in_=ag_out.ap()[base3:base3 + HSL * 4 * 128, :]
                                        .rearrange("(s q p) k -> p s q k", q=4, p=128)[:, :, cl, :]),
                                        16)
                                    wait_engines(TE, [sFW, sDV])
                                    for s_ in range(HSL):
                                        mm = TE.matmul(pC[:], wb3[:, s_ * 128:(s_ + 1) * 128],
                                                       h_fm[:, s_ * 512:(s_ + 1) * 512],
                                                       start=(s_ == 0), stop=(s_ == HSL - 1))
                                    pe(mm)
                                    V.wait_ge(sPE.h, sPE.n)
                                    if hhf == 0:
                                        V.wait_ge(sST.h, sST.n)
                                        dv(V.tensor_copy(out=dn_st[:], in_=pC[:]))
                                        SY.wait_ge(sDV.h, sDV.n)
                                        sST.inc(SY.dma_start(
                                            out=dn_part.ap()[m * 128:(m + 1) * 128,
                                                             tof:tof + 512],
                                            in_=dn_st[:]), 16)
                                    else:
                                        SY.wait_ge(sST.h, sST.n)
                                        sXL.inc(SY.dma_start(
                                            out=dn_p[:],
                                            in_=dn_part.ap()[m * 128:(m + 1) * 128,
                                                             tof:tof + 512]), 16)
                                        V.wait_ge(sXL.h, sXL.n)
                                        V.wait_ge(sST.h, sST.n)
                                        dv(V.tensor_tensor(out=dn_st[:], in0=pC[:],
                                                                in1=dn_p[:], op=ALU.add))
                                        wait_engines(TE, [sDV, sAC])
                                        for j in range(4):
                                            tr = TE.transpose(pF[:, j * 128:(j + 1) * 128],
                                                              dn_st[:, j * 128:(j + 1) * 128],
                                                              c_idenf[:])
                                        pe(tr)
                                        V.wait_ge(sPE.h, sPE.n)
                                        V.wait_ge(sST.h, sST.n)
                                        dv(V.tensor_copy(out=dn_o[:], in_=pF[:]))
                                        SY.wait_ge(sDV.h, sDV.n)
                                        sST.inc(SY.dma_start(
                                            out=dn_tm.ap()[tof:tof + 512, m * 128:(m + 1) * 128]
                                            .rearrange("(j p) k -> p j k", p=128),
                                            in_=dn_o[:].rearrange("p (j k) -> p j k", k=128)), 16)

                    if dbg:
                        SY.wait_ge(sST.h, sST.n)
                        for g in range(NOG):
                            sXL.inc(SY.dma_start(out=ln_x[:], in_=x2sp.ap()[g*128:(g+1)*128, :]), 16)
                            SY.wait_ge(sXL.h, sXL.n)
                            SY.wait_ge(sOUT.h, sOUT.n)
                            sOUT.inc(SY.dma_start(out=o_x2[g*128:(g+1)*128, :], in_=ln_x[:]), 16)
                            sXL.inc(SY.dma_start(out=ln_x[:], in_=dn_tm.ap()[g*128:(g+1)*128, :]), 16)
                            SY.wait_ge(sXL.h, sXL.n)
                            sOUT.inc(SY.dma_start(out=o_dn[g*128:(g+1)*128, :], in_=ln_x[:]), 16)
                        for r in range(NRANK):
                            sXL.inc(SY.dma_start(out=ln_b[:, 0:128],
                                                 in_=ag_out.ap()[agout_off(r, 0):agout_off(r, 0) + 128, :]), 16)
                            SY.wait_ge(sXL.h, sXL.n)
                            SY.wait_ge(sOUT.h, sOUT.n)
                            sOUT.inc(SY.dma_start(out=o_ag[r*128:(r+1)*128, :], in_=ln_b[:, 0:128]), 16)

                    SY.wait_ge(sST.h, sST.n)
                    for g in range(NOG):
                        ln_pass(x2sp.ap()[g * 128:(g + 1) * 128, :],
                                dn_tm.ap()[g * 128:(g + 1) * 128, :], None, None)
                        SY.wait_ge(sDV.h, sDV.n)
                        sOUT.inc(SY.dma_start(out=out[g * 128:(g + 1) * 128, :],
                                              in_=ln_x[:]), 16)
                    SY.wait_ge(sOUT.h, sOUT.n)

    return nc



def make_consts(nh=32):
    import ml_dtypes
    cbv = np.zeros((128, 32), dtype=np.float32)
    cbv[:, 0:15] = (-(NF4_BOUNDS.astype(np.float64) * 2.0 ** 100)).astype(np.float32)[None, :]
    cbv[:, CB_ONE] = 1.0
    cbv[:, CB_EPS] = EPS
    oh = np.zeros((32, 32 * 128), dtype=np.float32)
    for h in range(32):
        oh[h, h * 128:(h + 1) * 128] = 1.0
    id32 = np.eye(32, dtype=np.float32)
    idenb = np.eye(128, dtype=ml_dtypes.bfloat16)
    idenf = np.eye(128, dtype=np.float32)
    k = np.arange(128)
    mA = (k[:, None] < k[None, :]).astype(ml_dtypes.bfloat16)
    mB = (-200.0 * (k[:, None] >= k[None, :])).astype(ml_dtypes.bfloat16)
    return cbv, oh, id32, idenb, idenf, mA, mB


def make_rope(half):
    inv = (1.0 / (np.float32(10000.0) **
                  (np.arange(0, 128, 2, dtype=np.float32) / np.float32(128.0)))).astype(np.float32)
    pos = np.arange(T, dtype=np.float32)
    if half == 0:
        pos = np.maximum(pos - TOWN, 0.0).astype(np.float32)
    ang = pos[None, :] * inv[:, None]
    cosf = np.concatenate([np.cos(ang), np.cos(ang)], axis=0).astype(np.float32)
    sins = np.sin(ang).astype(np.float32)
    sinsg = np.concatenate([-sins, sins], axis=0).astype(np.float32)
    return cosf, sinsg


def kernel(L0_attn_hidden_states, L0_attn_Wq, L0_attn_Wk, L0_attn_Wv, L0_attn_Wo,
           L0_attn_Wg, L0_attn_Win, L0_attn_dt_bias, L0_attn_A_log_bias,
           L0_attn_gnorm_w, L0_ffn_w1, L0_ffn_w2, L0_ffn_w3):
    x = np.asarray(L0_attn_hidden_states, dtype=np.float32)
    B = x.shape[0]
    cbv, oh, id32, idenb, idenf, mA, mB = make_consts()
    wq = np.ascontiguousarray(np.asarray(L0_attn_Wq, np.float32).T)
    wk = np.ascontiguousarray(np.asarray(L0_attn_Wk, np.float32).T)
    wv = np.ascontiguousarray(np.asarray(L0_attn_Wv, np.float32).T)
    wg = np.ascontiguousarray(np.asarray(L0_attn_Wg, np.float32).T)
    win = np.ascontiguousarray(np.asarray(L0_attn_Win, np.float32).T)
    wo = np.ascontiguousarray(np.asarray(L0_attn_Wo, np.float32).T)
    dtbv = np.asarray(L0_attn_dt_bias, np.float32).reshape(32, 1)
    alogv = np.asarray(L0_attn_A_log_bias, np.float32).reshape(32, 1)
    gnw = np.broadcast_to(np.asarray(L0_attn_gnorm_w, np.float32)[None, :],
                          (128, 128)).copy()
    w1 = np.asarray(L0_ffn_w1, np.float32)
    w2 = np.asarray(L0_ffn_w2, np.float32)
    w3 = np.asarray(L0_ffn_w3, np.float32)
    HID = w1.shape[0]

    in_maps = []
    for r in range(NRANK):
        b, half = r // 2, r % 2
        xa = np.zeros((T, DM), dtype=np.float32)
        xa[TOWN:] = x[b, half * TOWN:(half + 1) * TOWN]
        if half == 1:
            xa[:TOWN] = x[b, 0:TOWN]
        cosf, sinsg = make_rope(half)
        w1r = np.zeros((HIDR, DM), dtype=np.float32)
        w2r = np.zeros((HIDR, DM), dtype=np.float32)
        lo = r * HIDR
        hi = min((r + 1) * HIDR, HID)
        if hi > lo:
            w1r[0:hi - lo] = w1[lo:hi]
            w2r[0:hi - lo] = w2[lo:hi]
        w3r = np.zeros((W3R, HID_PAD), dtype=np.float32)
        w3r[:, 0:HID] = w3[r * W3R:(r + 1) * W3R, :]
        in_maps.append(dict(
            x_all=xa, wqT=wq, wkT=wk, wvT=wv, wgT=wg, winT=win, woT=wo,
            cosf=cosf, sinsg=sinsg, dtb=dtbv, alog=alogv, gnwb=gnw, cb=cbv,
            onehot=oh, id32=id32, iden=idenb, idenf=idenf, maskA=mA, maskB=mB,
            w1s=w1r, w2s=w2r, w3s=w3r))

    nc = build_program(strict=True)
    res = run_bass_kernel_spmd(nc, in_maps, core_ids=list(range(NRANK)))
    outv = np.zeros((B, 2048, DM), dtype=np.float32)
    for r in range(NRANK):
        b, half = r // 2, r % 2
        outv[b, half * TOWN:(half + 1) * TOWN] = np.asarray(res.results[r]["out"])
    return outv


# version 06
# speedup vs baseline: 1.0271x; 1.0271x over previous
import os
import numpy as np
from contextlib import ExitStack

os.environ.setdefault("NEURON_SCRATCHPAD_PAGE_SIZE", "2048")

import concourse.bass as bass
import concourse.mybir as mybir
from concourse.bass_utils import run_bass_kernel_spmd

F32 = mybir.dt.float32
BF16 = mybir.dt.bfloat16
AF = mybir.ActivationFunctionType
ALU = mybir.AluOpType
AX = mybir.AxisListType

DM = 4096
D = 128
T = 2048
TOWN = 1024
NCH = 16
NOG = 8
OWN0 = 8
TB = 256
HIDR = 1408
NRANK = 8
HID_PAD = HIDR * NRANK
NSLOT = HID_PAD // 128
HSL = NSLOT // 2
W3R = DM // NRANK
EPS = 1e-5
SCALE_BIG = float(2.0 ** 100)

NF4_CODE = np.array([-1.0, -0.6961928009986877, -0.5250730514526367,
                     -0.39491748809814453, -0.28444138169288635, -0.18477343022823334,
                     -0.09105003625154495, 0.0, 0.07958029955625534, 0.16093020141124725,
                     0.24611230194568634, 0.33791524171829224, 0.44070982933044434,
                     0.5626170039176941, 0.7229568362236023, 1.0], dtype=np.float32)
NF4_BOUNDS = ((NF4_CODE[:-1] + NF4_CODE[1:]) * np.float32(0.5)).astype(np.float32)


def _nf4_deltas():
    deltas = np.zeros(15, dtype=np.float32)
    s = np.float32(NF4_CODE[0])
    for i in range(15):
        dd = np.float32(NF4_CODE[i + 1] - s)
        deltas[i] = dd
        s = np.float32(s + dd)
        assert s == NF4_CODE[i + 1]
    return deltas


NF4_DELTAS = _nf4_deltas()
CB_SIG0 = 0
CB_ONE = 15
CB_EPS = 16


class Sem:
    def __init__(self, stack, nc, name):
        self.h = stack.enter_context(nc.semaphore(name))
        self.n = 0

    def inc(self, instr, k=1):
        instr.then_inc(self.h, k)
        self.n += k


def build_program(nh=32, nkvh=8, do_ffn=True, do_ag=True, dbg=False, strict=False):
    gq = nh // nkvh
    nc = bass.Bass("TRN2", target_bir_lowering=False)
    dt_ = nc.dram_tensor

    x_all = dt_("x_all", [T, DM], F32, kind="ExternalInput").ap()
    wqT = dt_("wqT", [DM, nh * D], F32, kind="ExternalInput").ap()
    wkT = dt_("wkT", [DM, nkvh * D], F32, kind="ExternalInput").ap()
    wvT = dt_("wvT", [DM, nkvh * D], F32, kind="ExternalInput").ap()
    wgT = dt_("wgT", [DM, nh * D], F32, kind="ExternalInput").ap()
    winT = dt_("winT", [DM, 32], F32, kind="ExternalInput").ap()
    woT = dt_("woT", [nh * D, DM], F32, kind="ExternalInput").ap()
    cosf = dt_("cosf", [128, T], F32, kind="ExternalInput").ap()
    sinsg = dt_("sinsg", [128, T], F32, kind="ExternalInput").ap()
    dtb = dt_("dtb", [32, 1], F32, kind="ExternalInput").ap()
    alog = dt_("alog", [32, 1], F32, kind="ExternalInput").ap()
    gnwb = dt_("gnwb", [128, 128], F32, kind="ExternalInput").ap()
    cb = dt_("cb", [128, 32], F32, kind="ExternalInput").ap()
    onehot = dt_("onehot", [32, 32 * 128], F32, kind="ExternalInput").ap()
    id32 = dt_("id32", [32, 32], F32, kind="ExternalInput").ap()
    iden = dt_("iden", [128, 128], BF16, kind="ExternalInput").ap()
    idenf = dt_("idenf", [128, 128], F32, kind="ExternalInput").ap()
    maskA = dt_("maskA", [128, 128], BF16, kind="ExternalInput").ap()
    maskB = dt_("maskB", [128, 128], BF16, kind="ExternalInput").ap()
    w1s = dt_("w1s", [HIDR, DM], F32, kind="ExternalInput").ap()
    w2s = dt_("w2s", [HIDR, DM], F32, kind="ExternalInput").ap()
    w3s = dt_("w3s", [W3R, HID_PAD], F32, kind="ExternalInput").ap()
    out = dt_("out", [TOWN, DM], F32, kind="ExternalOutput").ap()
    if dbg:
        o_x2 = dt_("o_x2", [TOWN, DM], F32, kind="ExternalOutput").ap()
        o_dn = dt_("o_dn", [TOWN, DM], F32, kind="ExternalOutput").ap()
        o_ag = dt_("o_ag", [NRANK * 128, 128], BF16, kind="ExternalOutput").ap()
        o_h = dt_("o_h", [128, 512], BF16, kind="ExternalOutput").ap()

    NT1 = (HIDR // 128) * (DM // 128)
    NT3 = NSLOT * (W3R // 128)
    NTS = 2 * NT1 + NT3
    ag_in = dt_("ag_in", [NTS * 128, 128], BF16)
    ag_out = dt_("ag_out", [NRANK * NTS * 128, 128], BF16)
    xbf = dt_("xbf", [T, DM], BF16)
    kv_sp = dt_("kv_sp", [2 * nkvh * T, 128], F32)
    y_sp = dt_("y_sp", [nh * TOWN, 128], BF16)
    attn_tm = dt_("attn_tm", [TOWN, DM], F32)
    x2sp = dt_("x2sp", [TOWN, DM], F32)
    x2bf = dt_("x2bf", [TOWN, DM], BF16)
    dn_part = dt_("dn_part", [DM, TOWN], F32)
    dn_tm = dt_("dn_tm", [TOWN, DM], F32)
    glb = dt_("glb", [32, NCH], F32)

    def ag1_off(wi, m, c):
        return (wi * NT1 + m * 32 + c) * 128

    def ag3_off(s, cl):
        return (2 * NT1 + s * 4 + cl) * 128

    def agout_off(rank, local):
        return rank * NTS * 128 + local

    est = ExitStack()
    with est:
        e = est.enter_context
        sCL = Sem(est, nc, "sCL")
        sXC = Sem(est, nc, "sXC")
        sXL = Sem(est, nc, "sXL")
        sWL = Sem(est, nc, "sWL")
        sPE = Sem(est, nc, "sPE")
        sDV = Sem(est, nc, "sDV")
        sAC = Sem(est, nc, "sAC")
        sDQL = Sem(est, nc, "sDQL")
        sDQV = Sem(est, nc, "sDQV")
        sDQA = Sem(est, nc, "sDQA")
        sDQS = Sem(est, nc, "sDQS")
        sAG = Sem(est, nc, "sAG")
        sSP = Sem(est, nc, "sSP")
        sFW = Sem(est, nc, "sFW")
        sST = Sem(est, nc, "sST")
        sOUT = Sem(est, nc, "sOUT")

        block = e(nc.Block())

        @block.gpsimd
        def _(_g):
            V, SC, TE, SY, GP = nc.vector, nc.scalar, nc.tensor, nc.sync, nc.gpsimd
            with ExitStack() as top:
                et = top.enter_context
                c_cb = et(nc.sbuf_tensor("c_cb", [128, 32], F32))
                c_iden = et(nc.sbuf_tensor("c_iden", [128, 128], BF16))
                c_idenf = et(nc.sbuf_tensor("c_idenf", [128, 128], F32))
                x_fm = et(nc.sbuf_tensor("x_fm", [128, 32 * TOWN], BF16))
                pA = et(nc.psum_tensor("pA", [128, 512], F32))
                pB = et(nc.psum_tensor("pB", [128, 512], F32))
                pC = et(nc.psum_tensor("pC", [128, 512], F32))
                pD = et(nc.psum_tensor("pD", [128, 512], F32))
                pE = et(nc.psum_tensor("pE", [128, 512], F32))
                pF = et(nc.psum_tensor("pF", [128, 512], F32))
                pU = [et(nc.psum_tensor(f"pU{i}", [128, 512], F32)) for i in range(2)]

                for dst, src in [(c_cb, cb), (c_iden, iden), (c_idenf, idenf)]:
                    sCL.inc(SY.dma_start(out=dst[:], in_=src), 16)
                for i in range(2):
                    sXC.inc(GP.dma_start(out=xbf.ap()[i * TOWN:(i + 1) * TOWN, :],
                                         in_=x_all[i * TOWN:(i + 1) * TOWN, :],
                                         max_dma_last_dim=8192), 16)

                def wait_engines(eng, sems):
                    for s in sems:
                        eng.wait_ge(s.h, s.n)

                def mkinc(sem, eng):
                    def f(instr, k=1):
                        sem.inc(instr, k)
                        if strict:
                            eng.wait_ge(sem.h, sem.n)
                    return f
                def barrier_all(sems):
                    for eng in (V, SC, TE, SY, GP):
                        for s in sems:
                            eng.wait_ge(s.h, s.n)

                dv = mkinc(sDV, nc.vector)
                ac = mkinc(sAC, nc.scalar)
                pe = mkinc(sPE, nc.tensor)
                dqv = mkinc(sDQV, nc.vector)
                dqa = mkinc(sDQA, nc.scalar)

                def load_w_slice(dstbuf, wap, col0, nchunks):
                    GP.wait_ge(sPE.h, sPE.n)
                    sWL.inc(GP.dma_start(
                        out=dstbuf.ap().rearrange("p (c k) -> p c k", k=128)[:, 0:nchunks, :],
                        in_=wap[:, col0:col0 + 128].rearrange("(c p) k -> p c k", p=128)), 16)

                with ExitStack() as attn:
                    ea = attn.enter_context
                    c_cos = ea(nc.sbuf_tensor("c_cos", [128, T], F32))
                    c_sin = ea(nc.sbuf_tensor("c_sin", [128, T], F32))
                    c_oh = ea(nc.sbuf_tensor("c_oh", [32, 32 * 128], F32))
                    c_id32 = ea(nc.sbuf_tensor("c_id32", [32, 32], F32))
                    c_mA = ea(nc.sbuf_tensor("c_mA", [128, 128], BF16))
                    c_mB = ea(nc.sbuf_tensor("c_mB", [128, 128], BF16))
                    c_gnw = ea(nc.sbuf_tensor("c_gnw", [128, 128], F32))
                    c_dtb = ea(nc.sbuf_tensor("c_dtb", [32, 1], F32))
                    c_alog = ea(nc.sbuf_tensor("c_alog", [32, 1], F32))
                    c_expA = ea(nc.sbuf_tensor("c_expA", [32, 1], F32))
                    c_win = ea(nc.sbuf_tensor("c_win", [128, 32 * 32], BF16))
                    gt = ea(nc.sbuf_tensor("gt", [32, T], F32))
                    ngt = ea(nc.sbuf_tensor("ngt", [32, T], F32))
                    zrow = ea(nc.sbuf_tensor("zrow", [32, 128], F32))
                    cd_t = ea(nc.sbuf_tensor("cd_t", [128, 32 * NCH], F32))
                    dttm = ea(nc.sbuf_tensor("dttm", [128, 32 * NCH], F32))
                    expgtm = ea(nc.sbuf_tensor("expgtm", [128, 32 * NCH], F32))
                    glrow = ea(nc.sbuf_tensor("glrow", [32, NCH], F32))
                    DQF = 512
                    dq_w = ea(nc.sbuf_tensor("dq_w", [128, DQF], F32))
                    dq_nm = ea(nc.sbuf_tensor("dq_nm", [128, DQF], F32))
                    dq_pl = [ea(nc.sbuf_tensor(f"dq_pl{i}", [128, DQF], BF16)) for i in range(2)]
                    dq_am = ea(nc.sbuf_tensor("dq_am", [128, DQF // 64], F32))
                    dq_amc = ea(nc.sbuf_tensor("dq_amc", [128, DQF // 64], F32))
                    dq_ram = ea(nc.sbuf_tensor("dq_ram", [128, DQF // 64], F32))
                    dq_o = ea(nc.sbuf_tensor("dq_o", [128, DQF], BF16))
                    dq_tr = ea(nc.sbuf_tensor("dq_tr", [128, DQF], BF16))

                    for dst, src in [(c_cos, cosf), (c_sin, sinsg), (c_oh, onehot),
                                     (c_id32, id32), (c_mA, maskA), (c_mB, maskB),
                                     (c_gnw, gnwb), (c_dtb, dtb), (c_alog, alog)]:
                        sCL.inc(SY.dma_start(out=dst[:], in_=src), 16)
                    sWL.inc(GP.dma_start(
                        out=c_win.ap().rearrange("p (c k) -> p c k", k=32),
                        in_=winT.rearrange("(c p) k -> p c k", p=128)), 16)
                    wait_engines(SC, [sCL])
                    ac(SC.activation(out=c_expA[:], in_=c_alog[:], func=AF.Exp))
                    wait_engines(V, [sCL])
                    dv(V.memset(zrow[:], 0.0))

                    dq_list = []
                    for wi, wap in ((0, w1s), (1, w2s)):
                        for rb in range(HIDR // 128):
                            for cq in range(DM // DQF):
                                src = wap[rb * 128:(rb + 1) * 128, cq * DQF:(cq + 1) * DQF]
                                dests = [ag1_off(wi, rb, cq * (DQF // 128) + j) for j in range(DQF // 128)]
                                dq_list.append((src, dests))
                    for rb in range(W3R // 128):
                        for cq in range(HID_PAD // DQF):
                            src = w3s[rb * 128:(rb + 1) * 128, cq * DQF:(cq + 1) * DQF]
                            dests = [ag3_off(cq * (DQF // 128) + j, rb) for j in range(DQF // 128)]
                            dq_list.append((src, dests))
                    NDQ = len(dq_list)
                    dq_i = [0]

                    def dq_pump(k):
                        for _ in range(k):
                            ti = dq_i[0]
                            if ti >= NDQ:
                                return
                            dq_i[0] += 1
                            src, dests = dq_list[ti]
                            SY.wait_ge(sDQV.h, sDQV.n)
                            sDQL.inc(SY.dma_start(out=dq_w[:], in_=src), 16)
                            V.wait_ge(sDQL.h, sDQL.n)
                            V.wait_ge(sDQS.h, sDQS.n)
                            w3v = dq_w.ap().rearrange("p (b k) -> p b k", k=64)
                            dqv(V.tensor_reduce(out=dq_am[:], in_=w3v, axis=AX.X,
                                                     op=ALU.max, apply_absolute_value=True))
                            V.wait_ge(sDQV.h, sDQV.n)
                            dqv(V.tensor_scalar(out=dq_amc[:], in0=dq_am[:],
                                                     scalar1=1e-30, scalar2=None, op0=ALU.add))
                            V.wait_ge(sDQV.h, sDQV.n)
                            dqv(V.reciprocal(out=dq_ram[:], in_=dq_amc[:]))
                            V.wait_ge(sDQV.h, sDQV.n)
                            dqv(V.tensor_tensor(
                                out=dq_nm.ap().rearrange("p (b k) -> p b k", k=64),
                                in0=w3v,
                                in1=dq_ram.ap().unsqueeze(2).broadcast_to([128, DQF // 64, 64]),
                                op=ALU.mult))
                            norm_done = sDQV.n
                            chain0 = norm_done
                            plane0 = sDQA.n
                            SC.wait_ge(sDQV.h, norm_done)
                            for i in range(15):
                                if i >= 2:
                                    SC.wait_ge(sDQV.h, chain0 + (i - 1) * (DQF // 512))
                                dqa(SC.activation(out=dq_pl[i % 2][:], in_=dq_nm[:],
                                                       func=AF.Sigmoid,
                                                       bias=c_cb[:, CB_SIG0 + i:CB_SIG0 + i + 1],
                                                       scale=SCALE_BIG))
                            for i in range(15):
                                V.wait_ge(sDQA.h, plane0 + i + 1)
                                for cs in range(DQF // 512):
                                    sl = slice(cs * 512, (cs + 1) * 512)
                                    if i == 0:
                                        r = V.tensor_scalar(out=pU[cs][:], in0=dq_pl[0][:, sl],
                                                            scalar1=float(NF4_DELTAS[0]),
                                                            scalar2=float(NF4_CODE[0]),
                                                            op0=ALU.mult, op1=ALU.add)
                                    else:
                                        r = V.scalar_tensor_tensor(out=pU[cs][:],
                                                                   in0=dq_pl[i % 2][:, sl],
                                                                   scalar=float(NF4_DELTAS[i]),
                                                                   in1=pU[cs][:],
                                                                   op0=ALU.mult, op1=ALU.add)
                                    dqv(r)
                            V.wait_ge(sDQV.h, sDQV.n)
                            for cs in range(DQF // 512):
                                sl = slice(cs * 512, (cs + 1) * 512)
                                r = V.tensor_tensor(
                                    out=dq_o[:, sl].rearrange("p (b k) -> p b k", k=64),
                                    in0=pU[cs].ap().rearrange("p (b k) -> p b k", k=64),
                                    in1=dq_am[:, cs * 8:(cs + 1) * 8].unsqueeze(2)
                                          .broadcast_to([128, 8, 64]),
                                    op=ALU.mult)
                                dqv(r)
                            wait_engines(TE, [sAC, sDV])
                            TE.wait_ge(sDQV.h, sDQV.n)
                            for j in range(DQF // 128):
                                tr = TE.transpose(
                                    pF.ap().bitcast(BF16)[:, j * 128:(j + 1) * 128],
                                    dq_o[:, j * 128:(j + 1) * 128], c_iden[:])
                            pe(tr)
                            SC.wait_ge(sPE.h, sPE.n)
                            SC.wait_ge(sDQS.h, sDQS.n)
                            ac(SC.copy(out=dq_tr[:], in_=pF.ap().bitcast(BF16)[:, 0:DQF]))
                            SY.wait_ge(sAC.h, sAC.n)
                            runs = []
                            for dof in dests:
                                if runs and dof == runs[-1][1] + 128:
                                    runs[-1][1] = dof
                                else:
                                    runs.append([dof, dof])
                            off_j = 0
                            for lo, hi in runs:
                                npc = (hi - lo) // 128 + 1
                                sDQS.inc(SY.dma_start(
                                    out=ag_in[lo:lo + npc * 128, :]
                                    .rearrange("(j p) k -> p j k", p=128),
                                    in_=dq_tr[:, off_j * 128:(off_j + npc) * 128]
                                    .rearrange("p (j k) -> p j k", k=128)), 16)
                                off_j += npc

                    with ExitStack() as p1:
                        e1 = p1.enter_context
                        dtT = e1(nc.sbuf_tensor("dtT", [32, T], F32))
                        gsrc = e1(nc.sbuf_tensor("gsrc", [32, T], F32))
                        xfs = e1(nc.sbuf_tensor("xfs", [128, 32 * TB], BF16))
                        wb_k = e1(nc.sbuf_tensor("wb_k", [128, 32 * 128], BF16))
                        wb_v = e1(nc.sbuf_tensor("wb_v", [128, 32 * 128], BF16))
                        rp_a = e1(nc.sbuf_tensor("rp_a", [128, TB], F32))
                        rp_r = e1(nc.sbuf_tensor("rp_r", [128, TB], F32))
                        rp_b = e1(nc.sbuf_tensor("rp_b", [128, TB], F32))
                        khat = e1(nc.sbuf_tensor("khat", [128, TB], F32))
                        sp_st = e1(nc.sbuf_tensor("sp_st", [128, TB], F32))
                        dz = e1(nc.sbuf_tensor("dz", [32, TB], F32))
                        da1 = e1(nc.sbuf_tensor("da1", [32, TB], F32))
                        da2 = e1(nc.sbuf_tensor("da2", [32, TB], F32))

                        SY.wait_ge(sXC.h, sXC.n)
                        for c in range(32):
                            sXL.inc(SY.dma_start(
                                out=x_fm[:, c * TOWN:(c + 1) * TOWN],
                                in_=xbf.ap()[TOWN:2 * TOWN, c * 128:(c + 1) * 128],
                                transpose=True), 16)

                        for half in range(2):
                            for tb in range(TOWN // TB):
                                tg = half * (TOWN // TB) + tb
                                t0 = tg * TB
                                if half == 0:
                                    SY.wait_ge(sPE.h, sPE.n)
                                    for c in range(32):
                                        sXL.inc(SY.dma_start(
                                            out=xfs[:, c * TB:(c + 1) * TB],
                                            in_=xbf.ap()[t0:t0 + TB, c * 128:(c + 1) * 128],
                                            transpose=True), 16)

                                    def xr(c):
                                        return xfs[:, c * TB:(c + 1) * TB]
                                else:
                                    def xr(c):
                                        return x_fm[:, c * TOWN + tb * TB:
                                                    c * TOWN + tb * TB + TB]
                                tsl = slice(t0, t0 + TB)
                                wait_engines(TE, [sXL, sCL, sWL, sAC, sDV])
                                for c in range(32):
                                    mm = TE.matmul(pD[0:32, 0:TB], c_win[:, c * 32:(c + 1) * 32],
                                                   xr(c), start=(c == 0), stop=(c == 31))
                                pe(mm)
                                SC.wait_ge(sPE.h, sPE.n)
                                ac(SC.activation(out=dz[:], in_=pD[0:32, 0:TB],
                                                      func=AF.Identity, bias=c_dtb[:]))
                                SC.wait_ge(sAC.h, sAC.n)
                                ac(SC.activation(out=da1[:], in_=dz[:], func=AF.Abs))
                                SC.wait_ge(sAC.h, sAC.n)
                                ac(SC.activation(out=da1[:], in_=da1[:], func=AF.Exp,
                                                      scale=-1.0))
                                SC.wait_ge(sAC.h, sAC.n)
                                ac(SC.activation(out=da1[:], in_=da1[:], func=AF.Ln,
                                                      bias=c_cb[0:32, CB_ONE:CB_ONE + 1]))
                                wait_engines(V, [sAC])
                                dv(V.tensor_scalar(out=da2[:], in0=dz[:], scalar1=0.0,
                                                        scalar2=None, op0=ALU.max))
                                V.wait_ge(sDV.h, sDV.n)
                                dv(V.tensor_tensor(out=dtT[:, tsl], in0=da2[:],
                                                        in1=da1[:], op=ALU.add))
                                for kv in range(nkvh):
                                    for which in ("k", "v"):
                                        wb = wb_k if which == "k" else wb_v
                                        load_w_slice(wb, wkT if which == "k" else wvT,
                                                     kv * 128, 32)
                                        wait_engines(TE, [sWL, sAC, sDV])
                                        for c in range(32):
                                            mm = TE.matmul(pA[:, 0:TB],
                                                           wb[:, c * 128:(c + 1) * 128],
                                                           xr(c), start=(c == 0), stop=(c == 31))
                                        pe(mm)
                                        V.wait_ge(sPE.h, sPE.n)
                                        if which == "k":
                                            dv(V.tensor_tensor(out=rp_a[:], in0=pA[:, 0:TB],
                                                                    in1=c_cos[:, tsl], op=ALU.mult))
                                            dv(V.tensor_copy(out=rp_r[0:64, :],
                                                                  in_=pA[64:128, 0:TB]))
                                            dv(V.tensor_copy(out=rp_r[64:128, :],
                                                                  in_=pA[0:64, 0:TB]))
                                            V.wait_ge(sDV.h, sDV.n)
                                            dv(V.tensor_tensor(out=rp_b[:], in0=rp_r[:],
                                                                    in1=c_sin[:, tsl], op=ALU.mult))
                                            V.wait_ge(sDV.h, sDV.n)
                                            dv(V.tensor_tensor(out=khat[:], in0=rp_a[:],
                                                                    in1=rp_b[:], op=ALU.add))
                                            row0 = kv * T + t0
                                        else:
                                            dv(V.tensor_copy(out=khat[:], in_=pA[:, 0:TB]))
                                            row0 = nkvh * T + kv * T + t0
                                        wait_engines(TE, [sDV, sAC])
                                        for j in range(TB // 128):
                                            tr = TE.transpose(
                                                pB[:, j * 128:(j + 1) * 128],
                                                khat[:, j * 128:(j + 1) * 128], c_idenf[:])
                                        pe(tr)
                                        SC.wait_ge(sPE.h, sPE.n)
                                        SC.wait_ge(sSP.h, sSP.n)
                                        ac(SC.copy(out=sp_st[:], in_=pB[:, 0:TB]))
                                        SY.wait_ge(sAC.h, sAC.n)
                                        sSP.inc(SY.dma_start(
                                            out=kv_sp[row0:row0 + TB, :]
                                            .rearrange("(j p) k -> p j k", p=128),
                                            in_=sp_st[:].rearrange("p (j k) -> p j k", k=128)), 16)
                                    dq_pump(1)

                        wait_engines(V, [sDV, sAC])
                        dv(V.tensor_scalar(out=gsrc[:], in0=dtT[:], scalar1=c_expA[:],
                                                scalar2=-1.0, op0=ALU.mult, op1=ALU.mult))
                        V.wait_ge(sDV.h, sDV.n)
                        for n in range(NCH):
                            csl = slice(n * 128, (n + 1) * 128)
                            dv(V.tensor_tensor_scan(out=gt[:, csl], data0=gsrc[:, csl],
                                                         data1=zrow[:], initial=0.0,
                                                         op0=ALU.add, op1=ALU.add))
                        V.wait_ge(sDV.h, sDV.n)
                        dv(V.tensor_scalar(out=ngt[:], in0=gt[:], scalar1=-1.0,
                                                scalar2=None, op0=ALU.mult))
                        dv(V.tensor_copy(
                            out=glrow[:],
                            in_=gt.ap().rearrange("p (n k) -> p n k", k=128)[:, :, 127]))
                        V.wait_ge(sDV.h, sDV.n)
                        SY.wait_ge(sDV.h, sDV.n)
                        sST.inc(SY.dma_start(out=glb.ap(), in_=glrow[:]), 16)
                        SY.wait_ge(sST.h, sST.n)
                        sST.inc(SY.dma_start(
                            out=cd_t[:],
                            in_=glb.ap().rearrange("h n -> (h n)").unsqueeze(0)
                            .broadcast_to([128, 32 * NCH])), 16)
                        SC.wait_ge(sST.h, sST.n)
                        ac(SC.activation(out=cd_t[:], in_=cd_t[:], func=AF.Exp))
                        for n in range(NCH):
                            csl = slice(n * 128, (n + 1) * 128)
                            wait_engines(TE, [sDV, sAC])
                            TE.matmul(pD[:, 0:32], dtT[:, csl], c_id32[:], start=True, stop=True)
                            mm = TE.matmul(pD[:, 64:96], gt[:, csl], c_id32[:], start=True, stop=True)
                            pe(mm)
                            SC.wait_ge(sPE.h, sPE.n)
                            ac(SC.copy(out=dttm[:, n * 32:(n + 1) * 32], in_=pD[:, 0:32]))
                            ac(SC.activation(out=expgtm[:, n * 32:(n + 1) * 32],
                                                  in_=pD[:, 64:96], func=AF.Exp))
                        dq_pump(4)


                    barrier_all([sSP, sST, sDQS, sXL, sDV, sAC, sPE])
                    with ExitStack() as p2:
                        e2 = p2.enter_context
                        q_fm = e2(nc.sbuf_tensor("q_fm", [128, gq * TOWN], BF16))
                        sgate = e2(nc.sbuf_tensor("sgate", [128, NOG * gq * 128], BF16))
                        kT_sb = e2(nc.sbuf_tensor("kT_sb", [128, NCH * 128], F32))
                        vT_sb = e2(nc.sbuf_tensor("vT_sb", [128, NCH * 128], F32))
                        kf_sb = e2(nc.sbuf_tensor("kf_sb", [128, 128], BF16))
                        S_st = [e2(nc.sbuf_tensor(f"S_st{i}", [128, gq * 128], F32))
                                for i in range(2)]
                        S_bf = e2(nc.sbuf_tensor("S_bf", [128, 128], BF16))
                        e_sb = e2(nc.sbuf_tensor("e_sb", [128, 128], F32))
                        edec = e2(nc.sbuf_tensor("edec", [128, 1], F32))
                        xc_sb = e2(nc.sbuf_tensor("xc_sb", [128, 128], BF16))
                        kd_sb = e2(nc.sbuf_tensor("kd_sb", [128, 128], BF16))
                        s2_sb = e2(nc.sbuf_tensor("s2_sb", [128, 128], BF16))
                        ytmp = e2(nc.sbuf_tensor("ytmp", [128, 128], F32))
                        y_sb = [e2(nc.sbuf_tensor(f"y_sb{i}", [128, 128], F32))
                                for i in range(4)]
                        ss4 = e2(nc.sbuf_tensor("ss4", [128, 4], F32))
                        rstd4 = e2(nc.sbuf_tensor("rstd4", [128, 4], F32))
                        ydump = e2(nc.sbuf_tensor("ydump", [128, 128], BF16))
                        yg_sb = e2(nc.sbuf_tensor("yg_sb", [128, 128], BF16))
                        sil_t = e2(nc.sbuf_tensor("sil_t", [128, 128], F32))
                        wb_q = e2(nc.sbuf_tensor("wb_q", [128, 32 * 128], BF16))
                        wb_g = e2(nc.sbuf_tensor("wb_g", [128, 32 * 128], BF16))
                        rp2a = e2(nc.sbuf_tensor("rp2a", [128, 512], F32))
                        rp2r = e2(nc.sbuf_tensor("rp2r", [128, 512], F32))
                        rp2b = e2(nc.sbuf_tensor("rp2b", [128, 512], F32))

                        for kv in range(nkvh):
                            for hh in range(gq):
                                h = kv * gq + hh
                                load_w_slice(wb_q, wqT, h * 128, 32)
                                wait_engines(TE, [sWL, sDV, sAC])
                                for tb2 in range(2):
                                    tsl2 = slice(TOWN + tb2 * 512, TOWN + (tb2 + 1) * 512)
                                    wait_engines(TE, [sDV])
                                    for c in range(32):
                                        mm = TE.matmul(
                                            pA[:], wb_q[:, c * 128:(c + 1) * 128],
                                            x_fm[:, c * TOWN + tb2 * 512:c * TOWN + tb2 * 512 + 512],
                                            start=(c == 0), stop=(c == 31))
                                    pe(mm)
                                    V.wait_ge(sPE.h, sPE.n)
                                    dv(V.tensor_tensor(out=rp2a[:], in0=pA[:],
                                                            in1=c_cos[:, tsl2], op=ALU.mult))
                                    dv(V.tensor_copy(out=rp2r[0:64, :], in_=pA[64:128, :]))
                                    dv(V.tensor_copy(out=rp2r[64:128, :], in_=pA[0:64, :]))
                                    V.wait_ge(sDV.h, sDV.n)
                                    dv(V.tensor_tensor(out=rp2b[:], in0=rp2r[:],
                                                            in1=c_sin[:, tsl2], op=ALU.mult))
                                    V.wait_ge(sDV.h, sDV.n)
                                    dv(V.tensor_tensor(
                                        out=q_fm[:, hh * TOWN + tb2 * 512:hh * TOWN + tb2 * 512 + 512],
                                        in0=rp2a[:], in1=rp2b[:], op=ALU.add))
                                load_w_slice(wb_g, wgT, h * 128, 32)
                                wait_engines(TE, [sWL, sAC])
                                for g in range(NOG):
                                    wait_engines(TE, [sAC])
                                    for c in range(32):
                                        mm = TE.matmul(
                                            pB[:, 0:128],
                                            x_fm[:, c * TOWN + g * 128:c * TOWN + (g + 1) * 128],
                                            wb_g[:, c * 128:(c + 1) * 128],
                                            start=(c == 0), stop=(c == 31))
                                    pe(mm)
                                    SC.wait_ge(sPE.h, sPE.n)
                                    ac(SC.copy(
                                        out=sgate[:, (g * gq + hh) * 128:(g * gq + hh + 1) * 128],
                                        in_=pB[:, 0:128]))
                                dq_pump(1)
                            SY.wait_ge(sSP.h, sSP.n)
                            SY.wait_ge(sDV.h, sDV.n)
                            SY.wait_ge(sPE.h, sPE.n)
                            sXL.inc(SY.dma_start(
                                out=kT_sb.ap().rearrange("p (n k) -> p n k", k=128),
                                in_=kv_sp[kv * T:(kv + 1) * T, :]
                                .rearrange("(n p) k -> p n k", p=128)), 16)
                            sXL.inc(SY.dma_start(
                                out=vT_sb.ap().rearrange("p (n k) -> p n k", k=128),
                                in_=kv_sp[nkvh * T + kv * T:nkvh * T + (kv + 1) * T, :]
                                .rearrange("(n p) k -> p n k", p=128)), 16)
                            wait_engines(V, [sXL, sAC])
                            dv(V.memset(S_st[0][:], 0.0))

                            for n in range(NCH):
                                own = n >= OWN0
                                csl = slice(n * 128, (n + 1) * 128)
                                par = n % 2
                                if own:
                                    wait_engines(TE, [sDV, sAC])
                                    tr = TE.transpose(pF[:, 0:128],
                                                      kT_sb[:, csl], c_idenf[:])
                                    pe(tr)
                                    V.wait_ge(sPE.h, sPE.n)
                                    dv(V.tensor_copy(out=kf_sb[:], in_=pF[:, 0:128]))
                                for hh in range(gq):
                                    h = kv * gq + hh
                                    wait_engines(V, [sAC])
                                    dv(V.tensor_scalar(
                                        out=xc_sb[:], in0=vT_sb[:, csl],
                                        scalar1=dttm[:, n * 32 + h:n * 32 + h + 1],
                                        scalar2=None, op0=ALU.mult))
                                    ohsl = c_oh[:, h * 128:(h + 1) * 128]
                                    ohcol = c_oh[:, h * 128:h * 128 + 1]
                                    if own:
                                        wait_engines(TE, [sAC, sDV])
                                        TE.matmul(pE[:, 0:128], ohsl, gt[:, csl],
                                                  start=True, stop=False)
                                        TE.matmul(pE[:, 0:128], ngt[:, csl], ohsl,
                                                  start=False, stop=False)
                                        TE.matmul(pE[:, 0:128], c_mA[:], c_mB[:],
                                                  start=False, stop=True)
                                        mm = TE.matmul(pD[:, 0:128], kf_sb[:],
                                                       q_fm[:, hh * TOWN + (n - OWN0) * 128:
                                                            hh * TOWN + (n - OWN0 + 1) * 128],
                                                       start=True, stop=True)
                                        pe(mm)
                                        SC.wait_ge(sPE.h, sPE.n)
                                        SC.wait_ge(sDV.h, sDV.n)
                                        ac(SC.activation(out=e_sb[:], in_=pE[:, 0:128],
                                                              func=AF.Exp))
                                        wait_engines(V, [sAC])
                                        dv(V.tensor_tensor(out=s2_sb[:], in0=pD[:, 0:128],
                                                                in1=e_sb[:], op=ALU.mult))
                                        dv(V.tensor_scalar(out=kd_sb[:], in0=kT_sb[:, csl],
                                                                scalar1=e_sb[:, 127:128],
                                                                scalar2=None, op0=ALU.mult))
                                        dv(V.tensor_copy(
                                            out=S_bf[:], in_=S_st[par][:, hh * 128:(hh + 1) * 128]))
                                        V.wait_ge(sDV.h, sDV.n)
                                        wait_engines(TE, [sDV, sAC])
                                        TE.matmul(pB[:, 0:128],
                                                  q_fm[:, hh * TOWN + (n - OWN0) * 128:
                                                       hh * TOWN + (n - OWN0 + 1) * 128],
                                                  S_bf[:], start=True, stop=True)
                                        TE.matmul(pA[:, 0:128], s2_sb[:], xc_sb[:],
                                                  start=True, stop=True)
                                        mm = TE.matmul(pC[:, 0:128], kd_sb[:], xc_sb[:],
                                                       start=True, stop=True)
                                        pe(mm)
                                        SC.wait_ge(sPE.h, sPE.n)
                                        SC.wait_ge(sDV.h, sDV.n)
                                        ac(SC.activation(
                                            out=ytmp[:], in_=pB[:, 0:128], func=AF.Identity,
                                            scale=expgtm[:, n * 32 + h:n * 32 + h + 1]))
                                        wait_engines(V, [sAC])
                                        dv(V.tensor_tensor(out=y_sb[hh][:], in0=pA[:, 0:128],
                                                                in1=ytmp[:], op=ALU.add))
                                        V.wait_ge(sDV.h, sDV.n)
                                        SC.wait_ge(sDV.h, sDV.n)
                                        ac(SC.activation(out=ydump[:], in_=y_sb[hh][:],
                                                              func=AF.Square,
                                                              accum_out=ss4[:, hh:hh + 1]))
                                        dv(V.scalar_tensor_tensor(
                                            out=S_st[1 - par][:, hh * 128:(hh + 1) * 128],
                                            in0=S_st[par][:, hh * 128:(hh + 1) * 128],
                                            scalar=cd_t[:, h * NCH + n:h * NCH + n + 1],
                                            in1=pC[:, 0:128], op0=ALU.mult, op1=ALU.add))
                                    else:
                                        wait_engines(TE, [sAC, sDV])
                                        TE.matmul(pE[:, 0:1], ohsl,
                                                  gt[:, n * 128 + 127:n * 128 + 128],
                                                  start=True, stop=False)
                                        mm = TE.matmul(pE[:, 0:1], ngt[:, csl], ohcol,
                                                       start=False, stop=True)
                                        pe(mm)
                                        SC.wait_ge(sPE.h, sPE.n)
                                        SC.wait_ge(sDV.h, sDV.n)
                                        ac(SC.activation(out=edec[:], in_=pE[:, 0:1],
                                                              func=AF.Exp))
                                        wait_engines(V, [sAC])
                                        dv(V.tensor_scalar(out=kd_sb[:], in0=kT_sb[:, csl],
                                                                scalar1=edec[:], scalar2=None,
                                                                op0=ALU.mult))
                                        V.wait_ge(sDV.h, sDV.n)
                                        wait_engines(TE, [sDV])
                                        mm = TE.matmul(pC[:, 0:128], kd_sb[:], xc_sb[:],
                                                       start=True, stop=True)
                                        pe(mm)
                                        V.wait_ge(sPE.h, sPE.n)
                                        dv(V.scalar_tensor_tensor(
                                            out=S_st[1 - par][:, hh * 128:(hh + 1) * 128],
                                            in0=S_st[par][:, hh * 128:(hh + 1) * 128],
                                            scalar=cd_t[:, h * NCH + n:h * NCH + n + 1],
                                            in1=pC[:, 0:128], op0=ALU.mult, op1=ALU.add))
                                if own:
                                    g = n - OWN0
                                    SC.wait_ge(sDV.h, sDV.n)
                                    SC.wait_ge(sAC.h, sAC.n)
                                    ac(SC.activation(out=rstd4[:], in_=ss4[:], func=AF.Ln,
                                                          scale=1.0 / 128.0,
                                                          bias=c_cb[:, CB_EPS:CB_EPS + 1]))
                                    SC.wait_ge(sAC.h, sAC.n)
                                    ac(SC.activation(out=rstd4[:], in_=rstd4[:],
                                                          func=AF.Exp, scale=-0.5))
                                    wait_engines(V, [sAC])
                                    for hh in range(gq):
                                        h = kv * gq + hh
                                        SC.wait_ge(sDV.h, sDV.n)
                                        ac(SC.activation(
                                            out=sil_t[:],
                                            in_=sgate[:, (g * gq + hh) * 128:
                                                      (g * gq + hh + 1) * 128],
                                            func=AF.Silu))
                                        dv(V.scalar_tensor_tensor(
                                            out=ytmp[:], in0=y_sb[hh][:],
                                            scalar=rstd4[:, hh:hh + 1], in1=c_gnw[:],
                                            op0=ALU.mult, op1=ALU.mult))
                                        V.wait_ge(sDV.h, sDV.n)
                                        V.wait_ge(sAC.h, sAC.n)
                                        V.wait_ge(sSP.h, sSP.n)
                                        dv(V.tensor_tensor(
                                            out=yg_sb[:], in0=ytmp[:],
                                            in1=sil_t[:],
                                            op=ALU.mult))
                                        SY.wait_ge(sDV.h, sDV.n)
                                        sSP.inc(SY.dma_start(
                                            out=y_sp[h * TOWN + g * 128:h * TOWN + (g + 1) * 128, :],
                                            in_=yg_sb[:]), 16)
                            dq_pump(3)

                        dq_pump(NDQ)
                        if do_ag:
                            GP.wait_ge(sDQS.h, sDQS.n)
                            cc = GP.collective_compute(
                                "AllGather", ALU.bypass, replica_groups=[list(range(NRANK))],
                                ins=[ag_in.ap().opt()], outs=[ag_out.ap().opt()])
                            sAG.inc(cc, 1)
                        elif do_ffn:
                            GP.wait_ge(sDQS.h, sDQS.n)
                            for r in range(NRANK):
                                d = GP.dma_start(out=ag_out.ap()[r * NTS * 128:(r + 1) * NTS * 128, :],
                                                 in_=ag_in.ap())
                                sAG.inc(d, 16)

                barrier_all([sSP, sST, sDQS, sXL, sDV, sAC, sPE, sAG])
                with ExitStack() as p3:
                    e3 = p3.enter_context
                    yb = e3(nc.sbuf_tensor("yb", [128, nh * TOWN], BF16))
                    wb_o = e3(nc.sbuf_tensor("wb_o", [128, nh * 128], BF16))
                    wo_st = e3(nc.sbuf_tensor("wo_st", [128, 512], F32))
                    wo_o = e3(nc.sbuf_tensor("wo_o", [128, 512], F32))
                    SY.wait_ge(sSP.h, sSP.n)
                    for hec in range(nh):
                        sXL.inc(SY.dma_start(
                            out=yb[:, hec * TOWN:(hec + 1) * TOWN],
                            in_=y_sp[hec * TOWN:(hec + 1) * TOWN, :],
                            transpose=True), 16)
                    for m in range(32):
                        load_w_slice(wb_o, woT, m * 128, nh)
                        for th in range(2):
                            wait_engines(TE, [sWL, sXL, sDV])
                            for hec in range(nh):
                                mm = TE.matmul(
                                    pA[:], wb_o[:, hec * 128:(hec + 1) * 128],
                                    yb[:, hec * TOWN + th * 512:hec * TOWN + th * 512 + 512],
                                    start=(hec == 0), stop=(hec == nh - 1))
                            pe(mm)
                            V.wait_ge(sPE.h, sPE.n)
                            dv(V.tensor_copy(out=wo_st[:], in_=pA[:]))
                            wait_engines(TE, [sDV, sAC])
                            for j in range(4):
                                tr = TE.transpose(pF[:, j * 128:(j + 1) * 128],
                                                  wo_st[:, j * 128:(j + 1) * 128],
                                                  c_idenf[:])
                            pe(tr)
                            V.wait_ge(sPE.h, sPE.n)
                            V.wait_ge(sST.h, sST.n)
                            dv(V.tensor_copy(out=wo_o[:], in_=pF[:]))
                            SY.wait_ge(sDV.h, sDV.n)
                            sST.inc(SY.dma_start(
                                out=attn_tm.ap()[th * 512:(th + 1) * 512,
                                                 m * 128:(m + 1) * 128]
                                .rearrange("(j p) k -> p j k", p=128),
                                in_=wo_o[:].rearrange("p (j k) -> p j k", k=128)), 16)

                barrier_all([sSP, sST, sXL, sDV, sAC, sPE])
                with ExitStack() as p4:
                    e4 = p4.enter_context
                    ln_x = e4(nc.sbuf_tensor("ln_x", [128, DM], F32))
                    ln_a = e4(nc.sbuf_tensor("ln_a", [128, DM], F32))
                    ln_b = e4(nc.sbuf_tensor("ln_b", [128, DM], BF16))
                    ln_s = e4(nc.sbuf_tensor("ln_s", [128, 8], F32))

                    def ln_pass(res_ap, add_ap, spill_f32, spill_bf):
                        SY.wait_ge(sST.h, sST.n)
                        SY.wait_ge(sOUT.h, sOUT.n)
                        sXL.inc(SY.dma_start(out=ln_x[:], in_=res_ap), 16)
                        sXL.inc(SY.dma_start(out=ln_a[:], in_=add_ap), 16)
                        wait_engines(V, [sXL, sAC])
                        dv(V.tensor_tensor(out=ln_x[:], in0=ln_x[:], in1=ln_a[:],
                                                op=ALU.add))
                        SC.wait_ge(sDV.h, sDV.n)
                        ac(SC.activation(out=ln_b[:], in_=ln_x[:], func=AF.Square,
                                              accum_out=ln_s[:, 0:1]))
                        SC.wait_ge(sAC.h, sAC.n)
                        ac(SC.activation(out=ln_b[:], in_=ln_x[:], func=AF.Identity,
                                              accum_out=ln_s[:, 1:2]))
                        ac(SC.mul(out=ln_s[:, 2:3], in_=ln_s[:, 1:2], mul=1.0 / DM))
                        ac(SC.mul(out=ln_s[:, 3:4], in_=ln_s[:, 0:1], mul=1.0 / DM))
                        wait_engines(V, [sAC])
                        dv(V.tensor_scalar(out=ln_s[:, 4:5], in0=ln_s[:, 2:3],
                                                scalar1=ln_s[:, 2:3], scalar2=None,
                                                op0=ALU.mult))
                        V.wait_ge(sDV.h, sDV.n)
                        dv(V.tensor_tensor(out=ln_s[:, 5:6], in0=ln_s[:, 3:4],
                                                in1=ln_s[:, 4:5], op=ALU.subtract))
                        SC.wait_ge(sDV.h, sDV.n)
                        ac(SC.activation(out=ln_s[:, 6:7], in_=ln_s[:, 5:6], func=AF.Ln,
                                              bias=c_cb[:, CB_EPS:CB_EPS + 1]))
                        SC.wait_ge(sAC.h, sAC.n)
                        ac(SC.activation(out=ln_s[:, 7:8], in_=ln_s[:, 6:7], func=AF.Exp,
                                              scale=-0.5))
                        wait_engines(V, [sAC])
                        dv(V.tensor_scalar(out=ln_x[:], in0=ln_x[:],
                                                scalar1=ln_s[:, 2:3], scalar2=ln_s[:, 7:8],
                                                op0=ALU.subtract, op1=ALU.mult))
                        SY.wait_ge(sDV.h, sDV.n)
                        if spill_f32 is not None:
                            sST.inc(SY.dma_start(out=spill_f32, in_=ln_x[:]), 16)
                        if spill_bf is not None:
                            dv(V.tensor_copy(out=ln_b[:], in_=ln_x[:]))
                            SY.wait_ge(sDV.h, sDV.n)
                            sST.inc(SY.dma_start(out=spill_bf, in_=ln_b[:]), 16)

                    SY.wait_ge(sST.h, sST.n)
                    for g in range(NOG):
                        ln_pass(x_all[TOWN + g * 128:TOWN + (g + 1) * 128, :],
                                attn_tm.ap()[g * 128:(g + 1) * 128, :],
                                x2sp.ap()[g * 128:(g + 1) * 128, :],
                                x2bf.ap()[g * 128:(g + 1) * 128, :] if do_ffn else None)

                    if not do_ffn:
                        for g in range(NOG):
                            SY.wait_ge(sST.h, sST.n)
                            SY.wait_ge(sOUT.h, sOUT.n)
                            sXL.inc(SY.dma_start(out=ln_x[:],
                                                 in_=x2sp.ap()[g * 128:(g + 1) * 128, :]), 16)
                            SY.wait_ge(sXL.h, sXL.n)
                            sOUT.inc(SY.dma_start(out=out[g * 128:(g + 1) * 128, :],
                                                  in_=ln_x[:]), 16)
                        SY.wait_ge(sOUT.h, sOUT.n)
                        return

                    SY.wait_ge(sST.h, sST.n)
                    SY.wait_ge(sPE.h, sPE.n)
                    for c in range(32):
                        sXL.inc(SY.dma_start(out=x_fm[:, c * TOWN:(c + 1) * TOWN],
                                             in_=x2bf.ap()[:, c * 128:(c + 1) * 128],
                                             transpose=True), 16)

                    barrier_all([sST, sXL, sDV, sAC, sPE])
                    with ExitStack() as p5:
                        e5 = p5.enter_context
                        h_fm = e5(nc.sbuf_tensor("h_fm", [128, HSL * 512], BF16))
                        fw1 = e5(nc.sbuf_tensor("fw1", [128, 32 * 128], BF16))
                        fw2 = e5(nc.sbuf_tensor("fw2", [128, 32 * 128], BF16))
                        w3b = [e5(nc.sbuf_tensor(f"w3b{i}", [128, HSL * 128], BF16))
                               for i in range(2)]
                        sg_sb = e5(nc.sbuf_tensor("sg_sb", [128, 512], F32))
                        dn_st = e5(nc.sbuf_tensor("dn_st", [128, 512], F32))
                        dn_p = e5(nc.sbuf_tensor("dn_p", [128, 512], F32))
                        dn_o = e5(nc.sbuf_tensor("dn_o", [128, 512], F32))

                        SY.wait_ge(sAG.h, sAG.n)

                        for tt in (1, 0):
                            tof = tt * 512
                            for hhf in range(2):
                                for sl_ in range(HSL):
                                    s = hhf * HSL + sl_
                                    rank, ml = s // 11, s % 11
                                    SY.wait_ge(sPE.h, sPE.n)
                                    base1 = agout_off(rank, ag1_off(0, ml, 0))
                                    base2 = agout_off(rank, ag1_off(1, ml, 0))
                                    sFW.inc(SY.dma_start(
                                        out=fw1.ap().rearrange("p (c k) -> p c k", k=128),
                                        in_=ag_out.ap()[base1:base1 + 32 * 128, :]
                                        .rearrange("(c p) k -> p c k", p=128)), 16)
                                    sFW.inc(SY.dma_start(
                                        out=fw2.ap().rearrange("p (c k) -> p c k", k=128),
                                        in_=ag_out.ap()[base2:base2 + 32 * 128, :]
                                        .rearrange("(c p) k -> p c k", p=128)), 16)
                                    wait_engines(TE, [sFW, sXL, sDV, sAC])
                                    for c in range(32):
                                        mm = TE.matmul(pA[:], fw1[:, c * 128:(c + 1) * 128],
                                                       x_fm[:, c * TOWN + tof:c * TOWN + tof + 512],
                                                       start=(c == 0), stop=(c == 31))
                                    pe(mm)
                                    for c in range(32):
                                        mm = TE.matmul(pB[:], fw2[:, c * 128:(c + 1) * 128],
                                                       x_fm[:, c * TOWN + tof:c * TOWN + tof + 512],
                                                       start=(c == 0), stop=(c == 31))
                                    pe(mm)
                                    SC.wait_ge(sPE.h, sPE.n - 1)
                                    SC.wait_ge(sDV.h, sDV.n)
                                    ac(SC.activation(out=sg_sb[:], in_=pA[:], func=AF.Silu))
                                    wait_engines(V, [sAC])
                                    V.wait_ge(sPE.h, sPE.n)
                                    dv(V.tensor_tensor(out=h_fm[:, sl_ * 512:(sl_ + 1) * 512],
                                                            in0=pB[:], in1=sg_sb[:], op=ALU.mult))
                                    if dbg and tt == 0 and hhf == 0 and sl_ == 0:
                                        SY.wait_ge(sDV.h, sDV.n)
                                        sOUT.inc(SY.dma_start(out=o_h, in_=h_fm[:, 0:512]), 16)
                                for m in range(32):
                                    rank3, cl = m // 4, m % 4
                                    wb3 = w3b[m % 2]
                                    s0 = hhf * HSL
                                    base3 = agout_off(rank3, ag3_off(s0, 0))
                                    SY.wait_ge(sPE.h, sPE.n)
                                    sFW.inc(SY.dma_start(
                                        out=wb3.ap().rearrange("p (s k) -> p s k", k=128),
                                        in_=ag_out.ap()[base3:base3 + HSL * 4 * 128, :]
                                        .rearrange("(s q p) k -> p s q k", q=4, p=128)[:, :, cl, :]),
                                        16)
                                    wait_engines(TE, [sFW, sDV])
                                    for s_ in range(HSL):
                                        mm = TE.matmul(pC[:], wb3[:, s_ * 128:(s_ + 1) * 128],
                                                       h_fm[:, s_ * 512:(s_ + 1) * 512],
                                                       start=(s_ == 0), stop=(s_ == HSL - 1))
                                    pe(mm)
                                    V.wait_ge(sPE.h, sPE.n)
                                    if hhf == 0:
                                        V.wait_ge(sST.h, sST.n)
                                        dv(V.tensor_copy(out=dn_st[:], in_=pC[:]))
                                        SY.wait_ge(sDV.h, sDV.n)
                                        sST.inc(SY.dma_start(
                                            out=dn_part.ap()[m * 128:(m + 1) * 128,
                                                             tof:tof + 512],
                                            in_=dn_st[:]), 16)
                                    else:
                                        SY.wait_ge(sST.h, sST.n)
                                        sXL.inc(SY.dma_start(
                                            out=dn_p[:],
                                            in_=dn_part.ap()[m * 128:(m + 1) * 128,
                                                             tof:tof + 512]), 16)
                                        V.wait_ge(sXL.h, sXL.n)
                                        V.wait_ge(sST.h, sST.n)
                                        dv(V.tensor_tensor(out=dn_st[:], in0=pC[:],
                                                                in1=dn_p[:], op=ALU.add))
                                        wait_engines(TE, [sDV, sAC])
                                        for j in range(4):
                                            tr = TE.transpose(pF[:, j * 128:(j + 1) * 128],
                                                              dn_st[:, j * 128:(j + 1) * 128],
                                                              c_idenf[:])
                                        pe(tr)
                                        V.wait_ge(sPE.h, sPE.n)
                                        V.wait_ge(sST.h, sST.n)
                                        dv(V.tensor_copy(out=dn_o[:], in_=pF[:]))
                                        SY.wait_ge(sDV.h, sDV.n)
                                        sST.inc(SY.dma_start(
                                            out=dn_tm.ap()[tof:tof + 512, m * 128:(m + 1) * 128]
                                            .rearrange("(j p) k -> p j k", p=128),
                                            in_=dn_o[:].rearrange("p (j k) -> p j k", k=128)), 16)

                    if dbg:
                        SY.wait_ge(sST.h, sST.n)
                        for g in range(NOG):
                            sXL.inc(SY.dma_start(out=ln_x[:], in_=x2sp.ap()[g*128:(g+1)*128, :]), 16)
                            SY.wait_ge(sXL.h, sXL.n)
                            SY.wait_ge(sOUT.h, sOUT.n)
                            sOUT.inc(SY.dma_start(out=o_x2[g*128:(g+1)*128, :], in_=ln_x[:]), 16)
                            sXL.inc(SY.dma_start(out=ln_x[:], in_=dn_tm.ap()[g*128:(g+1)*128, :]), 16)
                            SY.wait_ge(sXL.h, sXL.n)
                            sOUT.inc(SY.dma_start(out=o_dn[g*128:(g+1)*128, :], in_=ln_x[:]), 16)
                        for r in range(NRANK):
                            sXL.inc(SY.dma_start(out=ln_b[:, 0:128],
                                                 in_=ag_out.ap()[agout_off(r, 0):agout_off(r, 0) + 128, :]), 16)
                            SY.wait_ge(sXL.h, sXL.n)
                            SY.wait_ge(sOUT.h, sOUT.n)
                            sOUT.inc(SY.dma_start(out=o_ag[r*128:(r+1)*128, :], in_=ln_b[:, 0:128]), 16)

                    SY.wait_ge(sST.h, sST.n)
                    for g in range(NOG):
                        ln_pass(x2sp.ap()[g * 128:(g + 1) * 128, :],
                                dn_tm.ap()[g * 128:(g + 1) * 128, :], None, None)
                        SY.wait_ge(sDV.h, sDV.n)
                        sOUT.inc(SY.dma_start(out=out[g * 128:(g + 1) * 128, :],
                                              in_=ln_x[:]), 16)
                    SY.wait_ge(sOUT.h, sOUT.n)

    return nc



def make_consts(nh=32):
    import ml_dtypes
    cbv = np.zeros((128, 32), dtype=np.float32)
    cbv[:, 0:15] = (-(NF4_BOUNDS.astype(np.float64) * 2.0 ** 100)).astype(np.float32)[None, :]
    cbv[:, CB_ONE] = 1.0
    cbv[:, CB_EPS] = EPS
    oh = np.zeros((32, 32 * 128), dtype=np.float32)
    for h in range(32):
        oh[h, h * 128:(h + 1) * 128] = 1.0
    id32 = np.eye(32, dtype=np.float32)
    idenb = np.eye(128, dtype=ml_dtypes.bfloat16)
    idenf = np.eye(128, dtype=np.float32)
    k = np.arange(128)
    mA = (k[:, None] < k[None, :]).astype(ml_dtypes.bfloat16)
    mB = (-200.0 * (k[:, None] >= k[None, :])).astype(ml_dtypes.bfloat16)
    return cbv, oh, id32, idenb, idenf, mA, mB


def make_rope(half):
    inv = (1.0 / (np.float32(10000.0) **
                  (np.arange(0, 128, 2, dtype=np.float32) / np.float32(128.0)))).astype(np.float32)
    pos = np.arange(T, dtype=np.float32)
    if half == 0:
        pos = np.maximum(pos - TOWN, 0.0).astype(np.float32)
    ang = pos[None, :] * inv[:, None]
    cosf = np.concatenate([np.cos(ang), np.cos(ang)], axis=0).astype(np.float32)
    sins = np.sin(ang).astype(np.float32)
    sinsg = np.concatenate([-sins, sins], axis=0).astype(np.float32)
    return cosf, sinsg


def kernel(L0_attn_hidden_states, L0_attn_Wq, L0_attn_Wk, L0_attn_Wv, L0_attn_Wo,
           L0_attn_Wg, L0_attn_Win, L0_attn_dt_bias, L0_attn_A_log_bias,
           L0_attn_gnorm_w, L0_ffn_w1, L0_ffn_w2, L0_ffn_w3):
    x = np.asarray(L0_attn_hidden_states, dtype=np.float32)
    B = x.shape[0]
    cbv, oh, id32, idenb, idenf, mA, mB = make_consts()
    wq = np.ascontiguousarray(np.asarray(L0_attn_Wq, np.float32).T)
    wk = np.ascontiguousarray(np.asarray(L0_attn_Wk, np.float32).T)
    wv = np.ascontiguousarray(np.asarray(L0_attn_Wv, np.float32).T)
    wg = np.ascontiguousarray(np.asarray(L0_attn_Wg, np.float32).T)
    win = np.ascontiguousarray(np.asarray(L0_attn_Win, np.float32).T)
    wo = np.ascontiguousarray(np.asarray(L0_attn_Wo, np.float32).T)
    dtbv = np.asarray(L0_attn_dt_bias, np.float32).reshape(32, 1)
    alogv = np.asarray(L0_attn_A_log_bias, np.float32).reshape(32, 1)
    gnw = np.broadcast_to(np.asarray(L0_attn_gnorm_w, np.float32)[None, :],
                          (128, 128)).copy()
    w1 = np.asarray(L0_ffn_w1, np.float32)
    w2 = np.asarray(L0_ffn_w2, np.float32)
    w3 = np.asarray(L0_ffn_w3, np.float32)
    HID = w1.shape[0]

    in_maps = []
    for r in range(NRANK):
        b, half = r // 2, r % 2
        xa = np.zeros((T, DM), dtype=np.float32)
        xa[TOWN:] = x[b, half * TOWN:(half + 1) * TOWN]
        if half == 1:
            xa[:TOWN] = x[b, 0:TOWN]
        cosf, sinsg = make_rope(half)
        w1r = np.zeros((HIDR, DM), dtype=np.float32)
        w2r = np.zeros((HIDR, DM), dtype=np.float32)
        lo = r * HIDR
        hi = min((r + 1) * HIDR, HID)
        if hi > lo:
            w1r[0:hi - lo] = w1[lo:hi]
            w2r[0:hi - lo] = w2[lo:hi]
        w3r = np.zeros((W3R, HID_PAD), dtype=np.float32)
        w3r[:, 0:HID] = w3[r * W3R:(r + 1) * W3R, :]
        in_maps.append(dict(
            x_all=xa, wqT=wq, wkT=wk, wvT=wv, wgT=wg, winT=win, woT=wo,
            cosf=cosf, sinsg=sinsg, dtb=dtbv, alog=alogv, gnwb=gnw, cb=cbv,
            onehot=oh, id32=id32, iden=idenb, idenf=idenf, maskA=mA, maskB=mB,
            w1s=w1r, w2s=w2r, w3s=w3r))

    nc = build_program(strict=False)
    res = run_bass_kernel_spmd(nc, in_maps, core_ids=list(range(NRANK)))
    outv = np.zeros((B, 2048, DM), dtype=np.float32)
    for r in range(NRANK):
        b, half = r // 2, r % 2
        outv[b, half * TOWN:(half + 1) * TOWN] = np.asarray(res.results[r]["out"])
    return outv


# version 07
# speedup vs baseline: 1.1121x; 1.0827x over previous
import os
import numpy as np
from contextlib import ExitStack

os.environ.setdefault("NEURON_SCRATCHPAD_PAGE_SIZE", "2048")

import concourse.bass as bass
import concourse.mybir as mybir
from concourse.bass_utils import run_bass_kernel_spmd

F32 = mybir.dt.float32
BF16 = mybir.dt.bfloat16
AF = mybir.ActivationFunctionType
ALU = mybir.AluOpType
AX = mybir.AxisListType

DM = 4096
D = 128
T = 2048
TOWN = 1024
NCH = 16
NOG = 8
OWN0 = 8
TB = 256
HIDR = 1408
NRANK = 8
HID_PAD = HIDR * NRANK
NSLOT = HID_PAD // 128
HSL = NSLOT // 2
W3R = DM // NRANK
EPS = 1e-5
SCALE_BIG = float(2.0 ** 100)

NF4_CODE = np.array([-1.0, -0.6961928009986877, -0.5250730514526367,
                     -0.39491748809814453, -0.28444138169288635, -0.18477343022823334,
                     -0.09105003625154495, 0.0, 0.07958029955625534, 0.16093020141124725,
                     0.24611230194568634, 0.33791524171829224, 0.44070982933044434,
                     0.5626170039176941, 0.7229568362236023, 1.0], dtype=np.float32)
NF4_BOUNDS = ((NF4_CODE[:-1] + NF4_CODE[1:]) * np.float32(0.5)).astype(np.float32)


def _nf4_deltas():
    deltas = np.zeros(15, dtype=np.float32)
    s = np.float32(NF4_CODE[0])
    for i in range(15):
        dd = np.float32(NF4_CODE[i + 1] - s)
        deltas[i] = dd
        s = np.float32(s + dd)
        assert s == NF4_CODE[i + 1]
    return deltas


NF4_DELTAS = _nf4_deltas()
CB_SIG0 = 0
CB_ONE = 15
CB_EPS = 16


class Sem:
    def __init__(self, stack, nc, name):
        self.h = stack.enter_context(nc.semaphore(name))
        self.n = 0

    def inc(self, instr, k=1):
        instr.then_inc(self.h, k)
        self.n += k


def build_program(nh=32, nkvh=8, do_ffn=True, do_ag=True, dbg=False, strict=False):
    gq = nh // nkvh
    nc = bass.Bass("TRN2", target_bir_lowering=False)
    dt_ = nc.dram_tensor

    x_all = dt_("x_all", [T, DM], F32, kind="ExternalInput").ap()
    wqT = dt_("wqT", [DM, nh * D], F32, kind="ExternalInput").ap()
    wkT = dt_("wkT", [DM, nkvh * D], F32, kind="ExternalInput").ap()
    wvT = dt_("wvT", [DM, nkvh * D], F32, kind="ExternalInput").ap()
    wgT = dt_("wgT", [DM, nh * D], F32, kind="ExternalInput").ap()
    winT = dt_("winT", [DM, 32], F32, kind="ExternalInput").ap()
    woT = dt_("woT", [nh * D, DM], F32, kind="ExternalInput").ap()
    cosf = dt_("cosf", [128, T], F32, kind="ExternalInput").ap()
    sinsg = dt_("sinsg", [128, T], F32, kind="ExternalInput").ap()
    dtb = dt_("dtb", [32, 1], F32, kind="ExternalInput").ap()
    alog = dt_("alog", [32, 1], F32, kind="ExternalInput").ap()
    gnwb = dt_("gnwb", [128, 128], F32, kind="ExternalInput").ap()
    cb = dt_("cb", [128, 32], F32, kind="ExternalInput").ap()
    onehot = dt_("onehot", [32, 32 * 128], F32, kind="ExternalInput").ap()
    id32 = dt_("id32", [32, 32], F32, kind="ExternalInput").ap()
    iden = dt_("iden", [128, 128], BF16, kind="ExternalInput").ap()
    idenf = dt_("idenf", [128, 128], F32, kind="ExternalInput").ap()
    maskA = dt_("maskA", [128, 128], BF16, kind="ExternalInput").ap()
    maskB = dt_("maskB", [128, 128], BF16, kind="ExternalInput").ap()
    w1s = dt_("w1s", [HIDR, DM], F32, kind="ExternalInput").ap()
    w2s = dt_("w2s", [HIDR, DM], F32, kind="ExternalInput").ap()
    w3s = dt_("w3s", [W3R, HID_PAD], F32, kind="ExternalInput").ap()
    out = dt_("out", [TOWN, DM], F32, kind="ExternalOutput").ap()
    if dbg:
        o_x2 = dt_("o_x2", [TOWN, DM], F32, kind="ExternalOutput").ap()
        o_dn = dt_("o_dn", [TOWN, DM], F32, kind="ExternalOutput").ap()
        o_ag = dt_("o_ag", [NRANK * 128, 128], BF16, kind="ExternalOutput").ap()
        o_h = dt_("o_h", [128, 512], BF16, kind="ExternalOutput").ap()

    NT1 = (HIDR // 128) * (DM // 128)
    NT3 = NSLOT * (W3R // 128)
    NTS = 2 * NT1 + NT3
    ag_in = dt_("ag_in", [NTS * 128, 128], BF16)
    ag_out = dt_("ag_out", [NRANK * NTS * 128, 128], BF16)
    xbf = dt_("xbf", [T, DM], BF16)
    kv_sp = dt_("kv_sp", [2 * nkvh * T, 128], F32)
    y_sp = dt_("y_sp", [nh * TOWN, 128], BF16)
    attn_tm = dt_("attn_tm", [TOWN, DM], F32)
    x2sp = dt_("x2sp", [TOWN, DM], F32)
    x2bf = dt_("x2bf", [TOWN, DM], BF16)
    dn_part = dt_("dn_part", [DM, TOWN], F32)
    dn_tm = dt_("dn_tm", [TOWN, DM], F32)
    glb = dt_("glb", [32, NCH], F32)

    def ag1_off(wi, m, c):
        return (wi * NT1 + m * 32 + c) * 128

    def ag3_off(s, cl):
        return (2 * NT1 + s * 4 + cl) * 128

    def agout_off(rank, local):
        return rank * NTS * 128 + local

    est = ExitStack()
    with est:
        e = est.enter_context
        sCL = Sem(est, nc, "sCL")
        sXC = Sem(est, nc, "sXC")
        sXL = Sem(est, nc, "sXL")
        sWL = Sem(est, nc, "sWL")
        sPE = Sem(est, nc, "sPE")
        sDV = Sem(est, nc, "sDV")
        sAC = Sem(est, nc, "sAC")
        sDQL = Sem(est, nc, "sDQL")
        sDQV = Sem(est, nc, "sDQV")
        sDQA = Sem(est, nc, "sDQA")
        sDQS = Sem(est, nc, "sDQS")
        sAG = Sem(est, nc, "sAG")
        sSP = Sem(est, nc, "sSP")
        sFW = Sem(est, nc, "sFW")
        sF1 = [Sem(est, nc, "sF1a"), Sem(est, nc, "sF1b")]
        sF2 = [Sem(est, nc, "sF2a"), Sem(est, nc, "sF2b")]
        sW3p = [Sem(est, nc, "sW3a"), Sem(est, nc, "sW3b")]
        sST = Sem(est, nc, "sST")
        sOUT = Sem(est, nc, "sOUT")

        block = e(nc.Block())

        @block.gpsimd
        def _(_g):
            V, SC, TE, SY, GP = nc.vector, nc.scalar, nc.tensor, nc.sync, nc.gpsimd
            with ExitStack() as top:
                et = top.enter_context
                c_cb = et(nc.sbuf_tensor("c_cb", [128, 32], F32))
                c_iden = et(nc.sbuf_tensor("c_iden", [128, 128], BF16))
                c_idenf = et(nc.sbuf_tensor("c_idenf", [128, 128], F32))
                x_fm = et(nc.sbuf_tensor("x_fm", [128, 32 * TOWN], BF16))
                pA = et(nc.psum_tensor("pA", [128, 512], F32))
                pB = et(nc.psum_tensor("pB", [128, 512], F32))
                pC = et(nc.psum_tensor("pC", [128, 512], F32))
                pD = et(nc.psum_tensor("pD", [128, 512], F32))
                pE = et(nc.psum_tensor("pE", [128, 512], F32))
                pF = et(nc.psum_tensor("pF", [128, 512], F32))
                pU = [et(nc.psum_tensor(f"pU{i}", [128, 512], F32)) for i in range(2)]

                for dst, src in [(c_cb, cb), (c_iden, iden), (c_idenf, idenf)]:
                    sCL.inc(SY.dma_start(out=dst[:], in_=src), 16)
                for i in range(2):
                    sXC.inc(GP.dma_start(out=xbf.ap()[i * TOWN:(i + 1) * TOWN, :],
                                         in_=x_all[i * TOWN:(i + 1) * TOWN, :],
                                         max_dma_last_dim=8192), 16)

                def wait_engines(eng, sems):
                    for s in sems:
                        eng.wait_ge(s.h, s.n)

                def mkinc(sem, eng):
                    def f(instr, k=1):
                        sem.inc(instr, k)
                        if strict:
                            eng.wait_ge(sem.h, sem.n)
                    return f
                def barrier_all(sems):
                    for eng in (V, SC, TE, SY, GP):
                        for s in sems:
                            eng.wait_ge(s.h, s.n)

                dv = mkinc(sDV, nc.vector)
                ac = mkinc(sAC, nc.scalar)
                pe = mkinc(sPE, nc.tensor)
                dqv = mkinc(sDQV, nc.vector)
                dqa = mkinc(sDQA, nc.scalar)

                def load_w_slice(dstbuf, wap, col0, nchunks):
                    GP.wait_ge(sPE.h, sPE.n)
                    sWL.inc(GP.dma_start(
                        out=dstbuf.ap().rearrange("p (c k) -> p c k", k=128)[:, 0:nchunks, :],
                        in_=wap[:, col0:col0 + 128].rearrange("(c p) k -> p c k", p=128)), 16)

                with ExitStack() as attn:
                    ea = attn.enter_context
                    c_cos = ea(nc.sbuf_tensor("c_cos", [128, T], F32))
                    c_sin = ea(nc.sbuf_tensor("c_sin", [128, T], F32))
                    c_oh = ea(nc.sbuf_tensor("c_oh", [32, 32 * 128], F32))
                    c_id32 = ea(nc.sbuf_tensor("c_id32", [32, 32], F32))
                    c_mA = ea(nc.sbuf_tensor("c_mA", [128, 128], BF16))
                    c_mB = ea(nc.sbuf_tensor("c_mB", [128, 128], BF16))
                    c_gnw = ea(nc.sbuf_tensor("c_gnw", [128, 128], F32))
                    c_dtb = ea(nc.sbuf_tensor("c_dtb", [32, 1], F32))
                    c_alog = ea(nc.sbuf_tensor("c_alog", [32, 1], F32))
                    c_expA = ea(nc.sbuf_tensor("c_expA", [32, 1], F32))
                    c_win = ea(nc.sbuf_tensor("c_win", [128, 32 * 32], BF16))
                    gt = ea(nc.sbuf_tensor("gt", [32, T], F32))
                    ngt = ea(nc.sbuf_tensor("ngt", [32, T], F32))
                    zrow = ea(nc.sbuf_tensor("zrow", [32, 128], F32))
                    cd_t = ea(nc.sbuf_tensor("cd_t", [128, 32 * NCH], F32))
                    dttm = ea(nc.sbuf_tensor("dttm", [128, 32 * NCH], F32))
                    expgtm = ea(nc.sbuf_tensor("expgtm", [128, 32 * NCH], F32))
                    glrow = ea(nc.sbuf_tensor("glrow", [32, NCH], F32))
                    DQF = 512
                    dq_w = ea(nc.sbuf_tensor("dq_w", [128, DQF], F32))
                    dq_nm = ea(nc.sbuf_tensor("dq_nm", [128, DQF], F32))
                    dq_pl = [ea(nc.sbuf_tensor(f"dq_pl{i}", [128, DQF], BF16)) for i in range(2)]
                    dq_am = ea(nc.sbuf_tensor("dq_am", [128, DQF // 64], F32))
                    dq_amc = ea(nc.sbuf_tensor("dq_amc", [128, DQF // 64], F32))
                    dq_ram = ea(nc.sbuf_tensor("dq_ram", [128, DQF // 64], F32))
                    dq_o = ea(nc.sbuf_tensor("dq_o", [128, DQF], BF16))
                    dq_tr = ea(nc.sbuf_tensor("dq_tr", [128, DQF], BF16))

                    for dst, src in [(c_cos, cosf), (c_sin, sinsg), (c_oh, onehot),
                                     (c_id32, id32), (c_mA, maskA), (c_mB, maskB),
                                     (c_gnw, gnwb), (c_dtb, dtb), (c_alog, alog)]:
                        sCL.inc(SY.dma_start(out=dst[:], in_=src), 16)
                    sWL.inc(GP.dma_start(
                        out=c_win.ap().rearrange("p (c k) -> p c k", k=32),
                        in_=winT.rearrange("(c p) k -> p c k", p=128)), 16)
                    wait_engines(SC, [sCL])
                    ac(SC.activation(out=c_expA[:], in_=c_alog[:], func=AF.Exp))
                    wait_engines(V, [sCL])
                    dv(V.memset(zrow[:], 0.0))

                    dq_list = []
                    for wi, wap in ((0, w1s), (1, w2s)):
                        for rb in range(HIDR // 128):
                            for cq in range(DM // DQF):
                                src = wap[rb * 128:(rb + 1) * 128, cq * DQF:(cq + 1) * DQF]
                                dests = [ag1_off(wi, rb, cq * (DQF // 128) + j) for j in range(DQF // 128)]
                                dq_list.append((src, dests))
                    for rb in range(W3R // 128):
                        for cq in range(HID_PAD // DQF):
                            src = w3s[rb * 128:(rb + 1) * 128, cq * DQF:(cq + 1) * DQF]
                            dests = [ag3_off(cq * (DQF // 128) + j, rb) for j in range(DQF // 128)]
                            dq_list.append((src, dests))
                    NDQ = len(dq_list)
                    dq_i = [0]

                    def dq_pump(k):
                        for _ in range(k):
                            ti = dq_i[0]
                            if ti >= NDQ:
                                return
                            dq_i[0] += 1
                            src, dests = dq_list[ti]
                            SY.wait_ge(sDQV.h, sDQV.n)
                            sDQL.inc(SY.dma_start(out=dq_w[:], in_=src), 16)
                            V.wait_ge(sDQL.h, sDQL.n)
                            V.wait_ge(sDQS.h, sDQS.n)
                            w3v = dq_w.ap().rearrange("p (b k) -> p b k", k=64)
                            dqv(V.tensor_reduce(out=dq_am[:], in_=w3v, axis=AX.X,
                                                     op=ALU.max, apply_absolute_value=True))
                            V.wait_ge(sDQV.h, sDQV.n)
                            dqv(V.tensor_scalar(out=dq_amc[:], in0=dq_am[:],
                                                     scalar1=1e-30, scalar2=None, op0=ALU.add))
                            V.wait_ge(sDQV.h, sDQV.n)
                            dqv(V.reciprocal(out=dq_ram[:], in_=dq_amc[:]))
                            V.wait_ge(sDQV.h, sDQV.n)
                            dqv(V.tensor_tensor(
                                out=dq_nm.ap().rearrange("p (b k) -> p b k", k=64),
                                in0=w3v,
                                in1=dq_ram.ap().unsqueeze(2).broadcast_to([128, DQF // 64, 64]),
                                op=ALU.mult))
                            norm_done = sDQV.n
                            chain0 = norm_done
                            plane0 = sDQA.n
                            SC.wait_ge(sDQV.h, norm_done)
                            for i in range(15):
                                if i >= 2:
                                    SC.wait_ge(sDQV.h, chain0 + (i - 1) * (DQF // 512))
                                dqa(SC.activation(out=dq_pl[i % 2][:], in_=dq_nm[:],
                                                       func=AF.Sigmoid,
                                                       bias=c_cb[:, CB_SIG0 + i:CB_SIG0 + i + 1],
                                                       scale=SCALE_BIG))
                            for i in range(15):
                                V.wait_ge(sDQA.h, plane0 + i + 1)
                                for cs in range(DQF // 512):
                                    sl = slice(cs * 512, (cs + 1) * 512)
                                    if i == 0:
                                        r = V.tensor_scalar(out=pU[cs][:], in0=dq_pl[0][:, sl],
                                                            scalar1=float(NF4_DELTAS[0]),
                                                            scalar2=float(NF4_CODE[0]),
                                                            op0=ALU.mult, op1=ALU.add)
                                    else:
                                        r = V.scalar_tensor_tensor(out=pU[cs][:],
                                                                   in0=dq_pl[i % 2][:, sl],
                                                                   scalar=float(NF4_DELTAS[i]),
                                                                   in1=pU[cs][:],
                                                                   op0=ALU.mult, op1=ALU.add)
                                    dqv(r)
                            V.wait_ge(sDQV.h, sDQV.n)
                            for cs in range(DQF // 512):
                                sl = slice(cs * 512, (cs + 1) * 512)
                                r = V.tensor_tensor(
                                    out=dq_o[:, sl].rearrange("p (b k) -> p b k", k=64),
                                    in0=pU[cs].ap().rearrange("p (b k) -> p b k", k=64),
                                    in1=dq_am[:, cs * 8:(cs + 1) * 8].unsqueeze(2)
                                          .broadcast_to([128, 8, 64]),
                                    op=ALU.mult)
                                dqv(r)
                            wait_engines(TE, [sAC, sDV])
                            TE.wait_ge(sDQV.h, sDQV.n)
                            for j in range(DQF // 128):
                                tr = TE.transpose(
                                    pF.ap().bitcast(BF16)[:, j * 128:(j + 1) * 128],
                                    dq_o[:, j * 128:(j + 1) * 128], c_iden[:])
                            pe(tr)
                            SC.wait_ge(sPE.h, sPE.n)
                            SC.wait_ge(sDQS.h, sDQS.n)
                            ac(SC.copy(out=dq_tr[:], in_=pF.ap().bitcast(BF16)[:, 0:DQF]))
                            SY.wait_ge(sAC.h, sAC.n)
                            runs = []
                            for dof in dests:
                                if runs and dof == runs[-1][1] + 128:
                                    runs[-1][1] = dof
                                else:
                                    runs.append([dof, dof])
                            off_j = 0
                            for lo, hi in runs:
                                npc = (hi - lo) // 128 + 1
                                sDQS.inc(SY.dma_start(
                                    out=ag_in[lo:lo + npc * 128, :]
                                    .rearrange("(j p) k -> p j k", p=128),
                                    in_=dq_tr[:, off_j * 128:(off_j + npc) * 128]
                                    .rearrange("p (j k) -> p j k", k=128)), 16)
                                off_j += npc

                    with ExitStack() as p1:
                        e1 = p1.enter_context
                        dtT = e1(nc.sbuf_tensor("dtT", [32, T], F32))
                        gsrc = e1(nc.sbuf_tensor("gsrc", [32, T], F32))
                        xfs = e1(nc.sbuf_tensor("xfs", [128, 32 * TB], BF16))
                        wb_k = e1(nc.sbuf_tensor("wb_k", [128, 32 * 128], BF16))
                        wb_v = e1(nc.sbuf_tensor("wb_v", [128, 32 * 128], BF16))
                        rp_a = e1(nc.sbuf_tensor("rp_a", [128, TB], F32))
                        rp_r = e1(nc.sbuf_tensor("rp_r", [128, TB], F32))
                        rp_b = e1(nc.sbuf_tensor("rp_b", [128, TB], F32))
                        khat = e1(nc.sbuf_tensor("khat", [128, TB], F32))
                        sp_st = e1(nc.sbuf_tensor("sp_st", [128, TB], F32))
                        dz = e1(nc.sbuf_tensor("dz", [32, TB], F32))
                        da1 = e1(nc.sbuf_tensor("da1", [32, TB], F32))
                        da2 = e1(nc.sbuf_tensor("da2", [32, TB], F32))

                        SY.wait_ge(sXC.h, sXC.n)
                        for c in range(32):
                            sXL.inc(SY.dma_start(
                                out=x_fm[:, c * TOWN:(c + 1) * TOWN],
                                in_=xbf.ap()[TOWN:2 * TOWN, c * 128:(c + 1) * 128],
                                transpose=True), 16)

                        for half in range(2):
                            for tb in range(TOWN // TB):
                                tg = half * (TOWN // TB) + tb
                                t0 = tg * TB
                                if half == 0:
                                    SY.wait_ge(sPE.h, sPE.n)
                                    for c in range(32):
                                        sXL.inc(SY.dma_start(
                                            out=xfs[:, c * TB:(c + 1) * TB],
                                            in_=xbf.ap()[t0:t0 + TB, c * 128:(c + 1) * 128],
                                            transpose=True), 16)

                                    def xr(c):
                                        return xfs[:, c * TB:(c + 1) * TB]
                                else:
                                    def xr(c):
                                        return x_fm[:, c * TOWN + tb * TB:
                                                    c * TOWN + tb * TB + TB]
                                tsl = slice(t0, t0 + TB)
                                wait_engines(TE, [sXL, sCL, sWL, sAC, sDV])
                                for c in range(32):
                                    mm = TE.matmul(pD[0:32, 0:TB], c_win[:, c * 32:(c + 1) * 32],
                                                   xr(c), start=(c == 0), stop=(c == 31))
                                pe(mm)
                                SC.wait_ge(sPE.h, sPE.n)
                                ac(SC.activation(out=dz[:], in_=pD[0:32, 0:TB],
                                                      func=AF.Identity, bias=c_dtb[:]))
                                SC.wait_ge(sAC.h, sAC.n)
                                ac(SC.activation(out=da1[:], in_=dz[:], func=AF.Abs))
                                SC.wait_ge(sAC.h, sAC.n)
                                ac(SC.activation(out=da1[:], in_=da1[:], func=AF.Exp,
                                                      scale=-1.0))
                                SC.wait_ge(sAC.h, sAC.n)
                                ac(SC.activation(out=da1[:], in_=da1[:], func=AF.Ln,
                                                      bias=c_cb[0:32, CB_ONE:CB_ONE + 1]))
                                wait_engines(V, [sAC])
                                dv(V.tensor_scalar(out=da2[:], in0=dz[:], scalar1=0.0,
                                                        scalar2=None, op0=ALU.max))
                                V.wait_ge(sDV.h, sDV.n)
                                dv(V.tensor_tensor(out=dtT[:, tsl], in0=da2[:],
                                                        in1=da1[:], op=ALU.add))
                                for kv in range(nkvh):
                                    for which in ("k", "v"):
                                        wb = wb_k if which == "k" else wb_v
                                        load_w_slice(wb, wkT if which == "k" else wvT,
                                                     kv * 128, 32)
                                        wait_engines(TE, [sWL, sAC, sDV])
                                        for c in range(32):
                                            mm = TE.matmul(pA[:, 0:TB],
                                                           wb[:, c * 128:(c + 1) * 128],
                                                           xr(c), start=(c == 0), stop=(c == 31))
                                        pe(mm)
                                        V.wait_ge(sPE.h, sPE.n)
                                        if which == "k":
                                            dv(V.tensor_tensor(out=rp_a[:], in0=pA[:, 0:TB],
                                                                    in1=c_cos[:, tsl], op=ALU.mult))
                                            dv(V.tensor_copy(out=rp_r[0:64, :],
                                                                  in_=pA[64:128, 0:TB]))
                                            dv(V.tensor_copy(out=rp_r[64:128, :],
                                                                  in_=pA[0:64, 0:TB]))
                                            V.wait_ge(sDV.h, sDV.n)
                                            dv(V.tensor_tensor(out=rp_b[:], in0=rp_r[:],
                                                                    in1=c_sin[:, tsl], op=ALU.mult))
                                            V.wait_ge(sDV.h, sDV.n)
                                            dv(V.tensor_tensor(out=khat[:], in0=rp_a[:],
                                                                    in1=rp_b[:], op=ALU.add))
                                            row0 = kv * T + t0
                                        else:
                                            dv(V.tensor_copy(out=khat[:], in_=pA[:, 0:TB]))
                                            row0 = nkvh * T + kv * T + t0
                                        wait_engines(TE, [sDV, sAC])
                                        for j in range(TB // 128):
                                            tr = TE.transpose(
                                                pB[:, j * 128:(j + 1) * 128],
                                                khat[:, j * 128:(j + 1) * 128], c_idenf[:])
                                        pe(tr)
                                        SC.wait_ge(sPE.h, sPE.n)
                                        SC.wait_ge(sSP.h, sSP.n)
                                        ac(SC.copy(out=sp_st[:], in_=pB[:, 0:TB]))
                                        SY.wait_ge(sAC.h, sAC.n)
                                        sSP.inc(SY.dma_start(
                                            out=kv_sp[row0:row0 + TB, :]
                                            .rearrange("(j p) k -> p j k", p=128),
                                            in_=sp_st[:].rearrange("p (j k) -> p j k", k=128)), 16)
                                    dq_pump(1)

                        wait_engines(V, [sDV, sAC])
                        dv(V.tensor_scalar(out=gsrc[:], in0=dtT[:], scalar1=c_expA[:],
                                                scalar2=-1.0, op0=ALU.mult, op1=ALU.mult))
                        V.wait_ge(sDV.h, sDV.n)
                        for n in range(NCH):
                            csl = slice(n * 128, (n + 1) * 128)
                            dv(V.tensor_tensor_scan(out=gt[:, csl], data0=gsrc[:, csl],
                                                         data1=zrow[:], initial=0.0,
                                                         op0=ALU.add, op1=ALU.add))
                        V.wait_ge(sDV.h, sDV.n)
                        dv(V.tensor_scalar(out=ngt[:], in0=gt[:], scalar1=-1.0,
                                                scalar2=None, op0=ALU.mult))
                        dv(V.tensor_copy(
                            out=glrow[:],
                            in_=gt.ap().rearrange("p (n k) -> p n k", k=128)[:, :, 127]))
                        V.wait_ge(sDV.h, sDV.n)
                        SY.wait_ge(sDV.h, sDV.n)
                        sST.inc(SY.dma_start(out=glb.ap(), in_=glrow[:]), 16)
                        SY.wait_ge(sST.h, sST.n)
                        sST.inc(SY.dma_start(
                            out=cd_t[:],
                            in_=glb.ap().rearrange("h n -> (h n)").unsqueeze(0)
                            .broadcast_to([128, 32 * NCH])), 16)
                        SC.wait_ge(sST.h, sST.n)
                        ac(SC.activation(out=cd_t[:], in_=cd_t[:], func=AF.Exp))
                        for n in range(NCH):
                            csl = slice(n * 128, (n + 1) * 128)
                            wait_engines(TE, [sDV, sAC])
                            TE.matmul(pD[:, 0:32], dtT[:, csl], c_id32[:], start=True, stop=True)
                            mm = TE.matmul(pD[:, 64:96], gt[:, csl], c_id32[:], start=True, stop=True)
                            pe(mm)
                            SC.wait_ge(sPE.h, sPE.n)
                            ac(SC.copy(out=dttm[:, n * 32:(n + 1) * 32], in_=pD[:, 0:32]))
                            ac(SC.activation(out=expgtm[:, n * 32:(n + 1) * 32],
                                                  in_=pD[:, 64:96], func=AF.Exp))
                        dq_pump(4)


                    barrier_all([sSP, sST, sDQS, sXL, sDV, sAC, sPE])
                    with ExitStack() as p2:
                        e2 = p2.enter_context
                        q_fm = e2(nc.sbuf_tensor("q_fm", [128, gq * TOWN], BF16))
                        sgate = e2(nc.sbuf_tensor("sgate", [128, NOG * gq * 128], BF16))
                        kT_sb = e2(nc.sbuf_tensor("kT_sb", [128, NCH * 128], F32))
                        vT_sb = e2(nc.sbuf_tensor("vT_sb", [128, NCH * 128], F32))
                        kf_sb = e2(nc.sbuf_tensor("kf_sb", [128, 128], BF16))
                        S_st = [e2(nc.sbuf_tensor(f"S_st{i}", [128, gq * 128], F32))
                                for i in range(2)]
                        S_bf = e2(nc.sbuf_tensor("S_bf", [128, 128], BF16))
                        e_sb = e2(nc.sbuf_tensor("e_sb", [128, 128], F32))
                        edec = e2(nc.sbuf_tensor("edec", [128, 1], F32))
                        xc_sb = e2(nc.sbuf_tensor("xc_sb", [128, 128], BF16))
                        kd_sb = e2(nc.sbuf_tensor("kd_sb", [128, 128], BF16))
                        s2_sb = e2(nc.sbuf_tensor("s2_sb", [128, 128], BF16))
                        ytmp = e2(nc.sbuf_tensor("ytmp", [128, 128], F32))
                        y_sb = [e2(nc.sbuf_tensor(f"y_sb{i}", [128, 128], F32))
                                for i in range(4)]
                        ss4 = e2(nc.sbuf_tensor("ss4", [128, 4], F32))
                        rstd4 = e2(nc.sbuf_tensor("rstd4", [128, 4], F32))
                        ydump = e2(nc.sbuf_tensor("ydump", [128, 128], BF16))
                        yg_sb = e2(nc.sbuf_tensor("yg_sb", [128, 128], BF16))
                        sil_t = e2(nc.sbuf_tensor("sil_t", [128, 128], F32))
                        wb_q = e2(nc.sbuf_tensor("wb_q", [128, 32 * 128], BF16))
                        wb_g = e2(nc.sbuf_tensor("wb_g", [128, 32 * 128], BF16))
                        rp2a = e2(nc.sbuf_tensor("rp2a", [128, 512], F32))
                        rp2r = e2(nc.sbuf_tensor("rp2r", [128, 512], F32))
                        rp2b = e2(nc.sbuf_tensor("rp2b", [128, 512], F32))

                        for kv in range(nkvh):
                            for hh in range(gq):
                                h = kv * gq + hh
                                load_w_slice(wb_q, wqT, h * 128, 32)
                                wait_engines(TE, [sWL, sDV, sAC])
                                for tb2 in range(2):
                                    tsl2 = slice(TOWN + tb2 * 512, TOWN + (tb2 + 1) * 512)
                                    wait_engines(TE, [sDV])
                                    for c in range(32):
                                        mm = TE.matmul(
                                            pA[:], wb_q[:, c * 128:(c + 1) * 128],
                                            x_fm[:, c * TOWN + tb2 * 512:c * TOWN + tb2 * 512 + 512],
                                            start=(c == 0), stop=(c == 31))
                                    pe(mm)
                                    V.wait_ge(sPE.h, sPE.n)
                                    dv(V.tensor_tensor(out=rp2a[:], in0=pA[:],
                                                            in1=c_cos[:, tsl2], op=ALU.mult))
                                    dv(V.tensor_copy(out=rp2r[0:64, :], in_=pA[64:128, :]))
                                    dv(V.tensor_copy(out=rp2r[64:128, :], in_=pA[0:64, :]))
                                    V.wait_ge(sDV.h, sDV.n)
                                    dv(V.tensor_tensor(out=rp2b[:], in0=rp2r[:],
                                                            in1=c_sin[:, tsl2], op=ALU.mult))
                                    V.wait_ge(sDV.h, sDV.n)
                                    dv(V.tensor_tensor(
                                        out=q_fm[:, hh * TOWN + tb2 * 512:hh * TOWN + tb2 * 512 + 512],
                                        in0=rp2a[:], in1=rp2b[:], op=ALU.add))
                                load_w_slice(wb_g, wgT, h * 128, 32)
                                wait_engines(TE, [sWL, sAC])
                                for g in range(NOG):
                                    wait_engines(TE, [sAC])
                                    for c in range(32):
                                        mm = TE.matmul(
                                            pB[:, 0:128],
                                            x_fm[:, c * TOWN + g * 128:c * TOWN + (g + 1) * 128],
                                            wb_g[:, c * 128:(c + 1) * 128],
                                            start=(c == 0), stop=(c == 31))
                                    pe(mm)
                                    SC.wait_ge(sPE.h, sPE.n)
                                    ac(SC.copy(
                                        out=sgate[:, (g * gq + hh) * 128:(g * gq + hh + 1) * 128],
                                        in_=pB[:, 0:128]))
                                dq_pump(1)
                            SY.wait_ge(sSP.h, sSP.n)
                            SY.wait_ge(sDV.h, sDV.n)
                            SY.wait_ge(sPE.h, sPE.n)
                            sXL.inc(SY.dma_start(
                                out=kT_sb.ap().rearrange("p (n k) -> p n k", k=128),
                                in_=kv_sp[kv * T:(kv + 1) * T, :]
                                .rearrange("(n p) k -> p n k", p=128)), 16)
                            sXL.inc(SY.dma_start(
                                out=vT_sb.ap().rearrange("p (n k) -> p n k", k=128),
                                in_=kv_sp[nkvh * T + kv * T:nkvh * T + (kv + 1) * T, :]
                                .rearrange("(n p) k -> p n k", p=128)), 16)
                            wait_engines(V, [sXL, sAC])
                            dv(V.memset(S_st[0][:], 0.0))

                            for n in range(NCH):
                                own = n >= OWN0
                                csl = slice(n * 128, (n + 1) * 128)
                                par = n % 2
                                if own:
                                    wait_engines(TE, [sDV, sAC])
                                    tr = TE.transpose(pF[:, 0:128],
                                                      kT_sb[:, csl], c_idenf[:])
                                    pe(tr)
                                    V.wait_ge(sPE.h, sPE.n)
                                    dv(V.tensor_copy(out=kf_sb[:], in_=pF[:, 0:128]))
                                for hh in range(gq):
                                    h = kv * gq + hh
                                    wait_engines(V, [sAC])
                                    dv(V.tensor_scalar(
                                        out=xc_sb[:], in0=vT_sb[:, csl],
                                        scalar1=dttm[:, n * 32 + h:n * 32 + h + 1],
                                        scalar2=None, op0=ALU.mult))
                                    ohsl = c_oh[:, h * 128:(h + 1) * 128]
                                    ohcol = c_oh[:, h * 128:h * 128 + 1]
                                    if own:
                                        wait_engines(TE, [sAC, sDV])
                                        TE.matmul(pE[:, 0:128], ohsl, gt[:, csl],
                                                  start=True, stop=False)
                                        TE.matmul(pE[:, 0:128], ngt[:, csl], ohsl,
                                                  start=False, stop=False)
                                        TE.matmul(pE[:, 0:128], c_mA[:], c_mB[:],
                                                  start=False, stop=True)
                                        mm = TE.matmul(pD[:, 0:128], kf_sb[:],
                                                       q_fm[:, hh * TOWN + (n - OWN0) * 128:
                                                            hh * TOWN + (n - OWN0 + 1) * 128],
                                                       start=True, stop=True)
                                        pe(mm)
                                        SC.wait_ge(sPE.h, sPE.n)
                                        SC.wait_ge(sDV.h, sDV.n)
                                        ac(SC.activation(out=e_sb[:], in_=pE[:, 0:128],
                                                              func=AF.Exp))
                                        wait_engines(V, [sAC])
                                        dv(V.tensor_tensor(out=s2_sb[:], in0=pD[:, 0:128],
                                                                in1=e_sb[:], op=ALU.mult))
                                        dv(V.tensor_scalar(out=kd_sb[:], in0=kT_sb[:, csl],
                                                                scalar1=e_sb[:, 127:128],
                                                                scalar2=None, op0=ALU.mult))
                                        dv(V.tensor_copy(
                                            out=S_bf[:], in_=S_st[par][:, hh * 128:(hh + 1) * 128]))
                                        V.wait_ge(sDV.h, sDV.n)
                                        wait_engines(TE, [sDV, sAC])
                                        TE.matmul(pB[:, 0:128],
                                                  q_fm[:, hh * TOWN + (n - OWN0) * 128:
                                                       hh * TOWN + (n - OWN0 + 1) * 128],
                                                  S_bf[:], start=True, stop=True)
                                        TE.matmul(pA[:, 0:128], s2_sb[:], xc_sb[:],
                                                  start=True, stop=True)
                                        mm = TE.matmul(pC[:, 0:128], kd_sb[:], xc_sb[:],
                                                       start=True, stop=True)
                                        pe(mm)
                                        SC.wait_ge(sPE.h, sPE.n)
                                        SC.wait_ge(sDV.h, sDV.n)
                                        ac(SC.activation(
                                            out=ytmp[:], in_=pB[:, 0:128], func=AF.Identity,
                                            scale=expgtm[:, n * 32 + h:n * 32 + h + 1]))
                                        wait_engines(V, [sAC])
                                        dv(V.tensor_tensor(out=y_sb[hh][:], in0=pA[:, 0:128],
                                                                in1=ytmp[:], op=ALU.add))
                                        V.wait_ge(sDV.h, sDV.n)
                                        SC.wait_ge(sDV.h, sDV.n)
                                        ac(SC.activation(out=ydump[:], in_=y_sb[hh][:],
                                                              func=AF.Square,
                                                              accum_out=ss4[:, hh:hh + 1]))
                                        dv(V.scalar_tensor_tensor(
                                            out=S_st[1 - par][:, hh * 128:(hh + 1) * 128],
                                            in0=S_st[par][:, hh * 128:(hh + 1) * 128],
                                            scalar=cd_t[:, h * NCH + n:h * NCH + n + 1],
                                            in1=pC[:, 0:128], op0=ALU.mult, op1=ALU.add))
                                    else:
                                        wait_engines(TE, [sAC, sDV])
                                        TE.matmul(pE[:, 0:1], ohsl,
                                                  gt[:, n * 128 + 127:n * 128 + 128],
                                                  start=True, stop=False)
                                        mm = TE.matmul(pE[:, 0:1], ngt[:, csl], ohcol,
                                                       start=False, stop=True)
                                        pe(mm)
                                        SC.wait_ge(sPE.h, sPE.n)
                                        SC.wait_ge(sDV.h, sDV.n)
                                        ac(SC.activation(out=edec[:], in_=pE[:, 0:1],
                                                              func=AF.Exp))
                                        wait_engines(V, [sAC])
                                        dv(V.tensor_scalar(out=kd_sb[:], in0=kT_sb[:, csl],
                                                                scalar1=edec[:], scalar2=None,
                                                                op0=ALU.mult))
                                        V.wait_ge(sDV.h, sDV.n)
                                        wait_engines(TE, [sDV])
                                        mm = TE.matmul(pC[:, 0:128], kd_sb[:], xc_sb[:],
                                                       start=True, stop=True)
                                        pe(mm)
                                        V.wait_ge(sPE.h, sPE.n)
                                        dv(V.scalar_tensor_tensor(
                                            out=S_st[1 - par][:, hh * 128:(hh + 1) * 128],
                                            in0=S_st[par][:, hh * 128:(hh + 1) * 128],
                                            scalar=cd_t[:, h * NCH + n:h * NCH + n + 1],
                                            in1=pC[:, 0:128], op0=ALU.mult, op1=ALU.add))
                                if own:
                                    g = n - OWN0
                                    SC.wait_ge(sDV.h, sDV.n)
                                    SC.wait_ge(sAC.h, sAC.n)
                                    ac(SC.activation(out=rstd4[:], in_=ss4[:], func=AF.Ln,
                                                          scale=1.0 / 128.0,
                                                          bias=c_cb[:, CB_EPS:CB_EPS + 1]))
                                    SC.wait_ge(sAC.h, sAC.n)
                                    ac(SC.activation(out=rstd4[:], in_=rstd4[:],
                                                          func=AF.Exp, scale=-0.5))
                                    wait_engines(V, [sAC])
                                    for hh in range(gq):
                                        h = kv * gq + hh
                                        SC.wait_ge(sDV.h, sDV.n)
                                        ac(SC.activation(
                                            out=sil_t[:],
                                            in_=sgate[:, (g * gq + hh) * 128:
                                                      (g * gq + hh + 1) * 128],
                                            func=AF.Silu))
                                        dv(V.scalar_tensor_tensor(
                                            out=ytmp[:], in0=y_sb[hh][:],
                                            scalar=rstd4[:, hh:hh + 1], in1=c_gnw[:],
                                            op0=ALU.mult, op1=ALU.mult))
                                        V.wait_ge(sDV.h, sDV.n)
                                        V.wait_ge(sAC.h, sAC.n)
                                        V.wait_ge(sSP.h, sSP.n)
                                        dv(V.tensor_tensor(
                                            out=yg_sb[:], in0=ytmp[:],
                                            in1=sil_t[:],
                                            op=ALU.mult))
                                        SY.wait_ge(sDV.h, sDV.n)
                                        sSP.inc(SY.dma_start(
                                            out=y_sp[h * TOWN + g * 128:h * TOWN + (g + 1) * 128, :],
                                            in_=yg_sb[:]), 16)
                            dq_pump(3)

                        dq_pump(NDQ)
                        if do_ag:
                            GP.wait_ge(sDQS.h, sDQS.n)
                            cc = GP.collective_compute(
                                "AllGather", ALU.bypass, replica_groups=[list(range(NRANK))],
                                ins=[ag_in.ap().opt()], outs=[ag_out.ap().opt()])
                            sAG.inc(cc, 1)
                        elif do_ffn:
                            GP.wait_ge(sDQS.h, sDQS.n)
                            for r in range(NRANK):
                                d = GP.dma_start(out=ag_out.ap()[r * NTS * 128:(r + 1) * NTS * 128, :],
                                                 in_=ag_in.ap())
                                sAG.inc(d, 16)

                barrier_all([sSP, sST, sDQS, sXL, sDV, sAC, sPE, sAG])
                with ExitStack() as p3:
                    e3 = p3.enter_context
                    yb = e3(nc.sbuf_tensor("yb", [128, nh * TOWN], BF16))
                    wb_o = e3(nc.sbuf_tensor("wb_o", [128, nh * 128], BF16))
                    wo_st = e3(nc.sbuf_tensor("wo_st", [128, 512], F32))
                    wo_o = e3(nc.sbuf_tensor("wo_o", [128, 512], F32))
                    SY.wait_ge(sSP.h, sSP.n)
                    for hec in range(nh):
                        sXL.inc(SY.dma_start(
                            out=yb[:, hec * TOWN:(hec + 1) * TOWN],
                            in_=y_sp[hec * TOWN:(hec + 1) * TOWN, :],
                            transpose=True), 16)
                    for m in range(32):
                        load_w_slice(wb_o, woT, m * 128, nh)
                        for th in range(2):
                            wait_engines(TE, [sWL, sXL, sDV])
                            for hec in range(nh):
                                mm = TE.matmul(
                                    pA[:], wb_o[:, hec * 128:(hec + 1) * 128],
                                    yb[:, hec * TOWN + th * 512:hec * TOWN + th * 512 + 512],
                                    start=(hec == 0), stop=(hec == nh - 1))
                            pe(mm)
                            V.wait_ge(sPE.h, sPE.n)
                            dv(V.tensor_copy(out=wo_st[:], in_=pA[:]))
                            wait_engines(TE, [sDV, sAC])
                            for j in range(4):
                                tr = TE.transpose(pF[:, j * 128:(j + 1) * 128],
                                                  wo_st[:, j * 128:(j + 1) * 128],
                                                  c_idenf[:])
                            pe(tr)
                            V.wait_ge(sPE.h, sPE.n)
                            V.wait_ge(sST.h, sST.n)
                            dv(V.tensor_copy(out=wo_o[:], in_=pF[:]))
                            SY.wait_ge(sDV.h, sDV.n)
                            sST.inc(SY.dma_start(
                                out=attn_tm.ap()[th * 512:(th + 1) * 512,
                                                 m * 128:(m + 1) * 128]
                                .rearrange("(j p) k -> p j k", p=128),
                                in_=wo_o[:].rearrange("p (j k) -> p j k", k=128)), 16)

                barrier_all([sSP, sST, sXL, sDV, sAC, sPE])
                with ExitStack() as p4:
                    e4 = p4.enter_context
                    ln_x = e4(nc.sbuf_tensor("ln_x", [128, DM], F32))
                    ln_a = e4(nc.sbuf_tensor("ln_a", [128, DM], F32))
                    ln_b = e4(nc.sbuf_tensor("ln_b", [128, DM], BF16))
                    ln_s = e4(nc.sbuf_tensor("ln_s", [128, 8], F32))

                    def ln_pass(res_ap, add_ap, spill_f32, spill_bf):
                        SY.wait_ge(sST.h, sST.n)
                        SY.wait_ge(sOUT.h, sOUT.n)
                        sXL.inc(SY.dma_start(out=ln_x[:], in_=res_ap), 16)
                        sXL.inc(SY.dma_start(out=ln_a[:], in_=add_ap), 16)
                        wait_engines(V, [sXL, sAC])
                        dv(V.tensor_tensor(out=ln_x[:], in0=ln_x[:], in1=ln_a[:],
                                                op=ALU.add))
                        SC.wait_ge(sDV.h, sDV.n)
                        ac(SC.activation(out=ln_b[:], in_=ln_x[:], func=AF.Square,
                                              accum_out=ln_s[:, 0:1]))
                        SC.wait_ge(sAC.h, sAC.n)
                        ac(SC.activation(out=ln_b[:], in_=ln_x[:], func=AF.Identity,
                                              accum_out=ln_s[:, 1:2]))
                        ac(SC.mul(out=ln_s[:, 2:3], in_=ln_s[:, 1:2], mul=1.0 / DM))
                        ac(SC.mul(out=ln_s[:, 3:4], in_=ln_s[:, 0:1], mul=1.0 / DM))
                        wait_engines(V, [sAC])
                        dv(V.tensor_scalar(out=ln_s[:, 4:5], in0=ln_s[:, 2:3],
                                                scalar1=ln_s[:, 2:3], scalar2=None,
                                                op0=ALU.mult))
                        V.wait_ge(sDV.h, sDV.n)
                        dv(V.tensor_tensor(out=ln_s[:, 5:6], in0=ln_s[:, 3:4],
                                                in1=ln_s[:, 4:5], op=ALU.subtract))
                        SC.wait_ge(sDV.h, sDV.n)
                        ac(SC.activation(out=ln_s[:, 6:7], in_=ln_s[:, 5:6], func=AF.Ln,
                                              bias=c_cb[:, CB_EPS:CB_EPS + 1]))
                        SC.wait_ge(sAC.h, sAC.n)
                        ac(SC.activation(out=ln_s[:, 7:8], in_=ln_s[:, 6:7], func=AF.Exp,
                                              scale=-0.5))
                        wait_engines(V, [sAC])
                        dv(V.tensor_scalar(out=ln_x[:], in0=ln_x[:],
                                                scalar1=ln_s[:, 2:3], scalar2=ln_s[:, 7:8],
                                                op0=ALU.subtract, op1=ALU.mult))
                        SY.wait_ge(sDV.h, sDV.n)
                        if spill_f32 is not None:
                            sST.inc(SY.dma_start(out=spill_f32, in_=ln_x[:]), 16)
                        if spill_bf is not None:
                            dv(V.tensor_copy(out=ln_b[:], in_=ln_x[:]))
                            SY.wait_ge(sDV.h, sDV.n)
                            sST.inc(SY.dma_start(out=spill_bf, in_=ln_b[:]), 16)

                    SY.wait_ge(sST.h, sST.n)
                    for g in range(NOG):
                        ln_pass(x_all[TOWN + g * 128:TOWN + (g + 1) * 128, :],
                                attn_tm.ap()[g * 128:(g + 1) * 128, :],
                                x2sp.ap()[g * 128:(g + 1) * 128, :],
                                x2bf.ap()[g * 128:(g + 1) * 128, :] if do_ffn else None)

                    if not do_ffn:
                        for g in range(NOG):
                            SY.wait_ge(sST.h, sST.n)
                            SY.wait_ge(sOUT.h, sOUT.n)
                            sXL.inc(SY.dma_start(out=ln_x[:],
                                                 in_=x2sp.ap()[g * 128:(g + 1) * 128, :]), 16)
                            SY.wait_ge(sXL.h, sXL.n)
                            sOUT.inc(SY.dma_start(out=out[g * 128:(g + 1) * 128, :],
                                                  in_=ln_x[:]), 16)
                        SY.wait_ge(sOUT.h, sOUT.n)
                        return

                    SY.wait_ge(sST.h, sST.n)
                    SY.wait_ge(sPE.h, sPE.n)
                    for c in range(32):
                        sXL.inc(SY.dma_start(out=x_fm[:, c * TOWN:(c + 1) * TOWN],
                                             in_=x2bf.ap()[:, c * 128:(c + 1) * 128],
                                             transpose=True), 16)

                    barrier_all([sST, sXL, sDV, sAC, sPE])
                    with ExitStack() as p5:
                        e5 = p5.enter_context
                        HS2 = NSLOT // 4
                        h_fm = e5(nc.sbuf_tensor("h_fm", [128, HS2 * 512], BF16))
                        fw = [e5(nc.sbuf_tensor(f"fw{i}", [128, 32 * 128], BF16))
                              for i in range(4)]
                        w3b = [e5(nc.sbuf_tensor(f"w3b{i}", [128, HS2 * 128], BF16))
                               for i in range(2)]
                        sg_sb = [e5(nc.sbuf_tensor(f"sg_sb{i}", [128, 512], F32))
                                 for i in range(2)]
                        dn_st = e5(nc.sbuf_tensor("dn_st", [128, 512], F32))
                        dn_p = e5(nc.sbuf_tensor("dn_p", [128, 512], F32))
                        dn_o = e5(nc.sbuf_tensor("dn_o", [128, 512], F32))

                        SY.wait_ge(sAG.h, sAG.n)
                        gu_banks = [(pA, pB), (pC, pD)]

                        def gu_load(s, par, pe_gate):
                            if pe_gate[par] is not None:
                                SY.wait_ge(sPE.h, pe_gate[par])
                            rank, ml = s // 11, s % 11
                            b1 = agout_off(rank, ag1_off(0, ml, 0))
                            b2 = agout_off(rank, ag1_off(1, ml, 0))
                            for sem, bb, bs in ((sF1[par], fw[2 * par], b1),
                                                (sF2[par], fw[2 * par + 1], b2)):
                                sem.inc(SY.dma_start(
                                    out=bb.ap().rearrange("p (c k) -> p c k", k=128),
                                    in_=ag_out.ap()[bs:bs + 32 * 128, :]
                                    .rearrange("(c p) k -> p c k", p=128)), 16)
                            return (sF1[par].n, sF2[par].n)

                        for tt in range(2):
                            tof = tt * 512
                            for hhf in range(4):
                                s0 = hhf * HS2
                                pe_gate = [None, None]
                                fw_ready = {}
                                fw_ready[0] = gu_load(s0, 0, pe_gate)
                                if HS2 > 1:
                                    fw_ready[1] = gu_load(s0 + 1, 1, pe_gate)
                                dv_gate = {}
                                for sl_ in range(HS2):
                                    s = s0 + sl_
                                    par = sl_ % 2
                                    bg, bu = gu_banks[par]
                                    TE.wait_ge(sF1[par].h, fw_ready[sl_][0])
                                    TE.wait_ge(sF2[par].h, fw_ready[sl_][1])
                                    TE.wait_ge(sXL.h, sXL.n)
                                    if sl_ >= 2:
                                        TE.wait_ge(sDV.h, dv_gate[sl_ - 2])
                                    for c in range(32):
                                        mm = TE.matmul(bg[:], fw[2 * par][:, c * 128:(c + 1) * 128],
                                                       x_fm[:, c * TOWN + tof:c * TOWN + tof + 512],
                                                       start=(c == 0), stop=(c == 31))
                                    pe(mm)
                                    gate_done = sPE.n
                                    for c in range(32):
                                        mm = TE.matmul(bu[:], fw[2 * par + 1][:, c * 128:(c + 1) * 128],
                                                       x_fm[:, c * TOWN + tof:c * TOWN + tof + 512],
                                                       start=(c == 0), stop=(c == 31))
                                    pe(mm)
                                    pe_gate[par] = sPE.n
                                    if sl_ + 2 < HS2:
                                        fw_ready[sl_ + 2] = gu_load(s0 + sl_ + 2, par, pe_gate)
                                    SC.wait_ge(sPE.h, gate_done)
                                    SC.wait_ge(sDV.h, dv_gate.get(sl_ - 1, 0))
                                    ac(SC.activation(out=sg_sb[par][:], in_=bg[:], func=AF.Silu))
                                    V.wait_ge(sAC.h, sAC.n)
                                    V.wait_ge(sPE.h, pe_gate[par])
                                    dv(V.tensor_tensor(out=h_fm[:, sl_ * 512:(sl_ + 1) * 512],
                                                       in0=bu[:], in1=sg_sb[par][:], op=ALU.mult))
                                    dv_gate[sl_] = sDV.n
                                dn_banks = [pE, pF]
                                w3_ready = {}
                                pe_dn = [None, None]

                                def w3_load(m, par):
                                    if pe_dn[par] is not None:
                                        SY.wait_ge(sPE.h, pe_dn[par])
                                    rank3, cl = m // 4, m % 4
                                    base3 = agout_off(rank3, ag3_off(s0, 0))
                                    sW3p[par].inc(SY.dma_start(
                                        out=w3b[par].ap().rearrange("p (s k) -> p s k", k=128),
                                        in_=ag_out.ap()[base3:base3 + HS2 * 4 * 128, :]
                                        .rearrange("(s q p) k -> p s q k", q=4, p=128)[:, :, cl, :]),
                                        16)
                                    return sW3p[par].n

                                w3_ready[0] = w3_load(0, 0)
                                w3_ready[1] = w3_load(1, 1)
                                for m in range(32):
                                    par = m % 2
                                    bk = dn_banks[par]
                                    TE.wait_ge(sW3p[par].h, w3_ready[m])
                                    TE.wait_ge(sDV.h, sDV.n)
                                    for s_ in range(HS2):
                                        mm = TE.matmul(bk[:], w3b[par][:, s_ * 128:(s_ + 1) * 128],
                                                       h_fm[:, s_ * 512:(s_ + 1) * 512],
                                                       start=(s_ == 0), stop=(s_ == HS2 - 1))
                                    pe(mm)
                                    pe_dn[par] = sPE.n
                                    if m + 2 < 32:
                                        w3_ready[m + 2] = w3_load(m + 2, par)
                                    V.wait_ge(sPE.h, pe_dn[par])
                                    V.wait_ge(sST.h, sST.n)
                                    if hhf == 0:
                                        dv(V.tensor_copy(out=dn_st[:], in_=bk[:]))
                                        SY.wait_ge(sDV.h, sDV.n)
                                        sST.inc(SY.dma_start(
                                            out=dn_part.ap()[m * 128:(m + 1) * 128, tof:tof + 512],
                                            in_=dn_st[:]), 16)
                                    else:
                                        SY.wait_ge(sDV.h, sDV.n)
                                        sXL.inc(SY.dma_start(
                                            out=dn_p[:],
                                            in_=dn_part.ap()[m * 128:(m + 1) * 128, tof:tof + 512]), 16)
                                        V.wait_ge(sXL.h, sXL.n)
                                        dv(V.tensor_tensor(out=dn_st[:], in0=bk[:],
                                                           in1=dn_p[:], op=ALU.add))
                                        if hhf < 3:
                                            SY.wait_ge(sDV.h, sDV.n)
                                            sST.inc(SY.dma_start(
                                                out=dn_part.ap()[m * 128:(m + 1) * 128, tof:tof + 512],
                                                in_=dn_st[:]), 16)
                                        else:
                                            wait_engines(TE, [sDV, sAC])
                                            for j in range(4):
                                                tr = TE.transpose(pU[0][:, j * 128:(j + 1) * 128],
                                                                  dn_st[:, j * 128:(j + 1) * 128],
                                                                  c_idenf[:])
                                            pe(tr)
                                            V.wait_ge(sPE.h, sPE.n)
                                            dv(V.tensor_copy(out=dn_o[:], in_=pU[0][:]))
                                            SY.wait_ge(sDV.h, sDV.n)
                                            sST.inc(SY.dma_start(
                                                out=dn_tm.ap()[tof:tof + 512, m * 128:(m + 1) * 128]
                                                .rearrange("(j p) k -> p j k", p=128),
                                                in_=dn_o[:].rearrange("p (j k) -> p j k", k=128)), 16)

                    SY.wait_ge(sST.h, sST.n)
                    for g in range(NOG):
                        ln_pass(x2sp.ap()[g * 128:(g + 1) * 128, :],
                                dn_tm.ap()[g * 128:(g + 1) * 128, :], None, None)
                        SY.wait_ge(sDV.h, sDV.n)
                        sOUT.inc(SY.dma_start(out=out[g * 128:(g + 1) * 128, :],
                                              in_=ln_x[:]), 16)
                    SY.wait_ge(sOUT.h, sOUT.n)

    return nc



def make_consts(nh=32):
    import ml_dtypes
    cbv = np.zeros((128, 32), dtype=np.float32)
    cbv[:, 0:15] = (-(NF4_BOUNDS.astype(np.float64) * 2.0 ** 100)).astype(np.float32)[None, :]
    cbv[:, CB_ONE] = 1.0
    cbv[:, CB_EPS] = EPS
    oh = np.zeros((32, 32 * 128), dtype=np.float32)
    for h in range(32):
        oh[h, h * 128:(h + 1) * 128] = 1.0
    id32 = np.eye(32, dtype=np.float32)
    idenb = np.eye(128, dtype=ml_dtypes.bfloat16)
    idenf = np.eye(128, dtype=np.float32)
    k = np.arange(128)
    mA = (k[:, None] < k[None, :]).astype(ml_dtypes.bfloat16)
    mB = (-200.0 * (k[:, None] >= k[None, :])).astype(ml_dtypes.bfloat16)
    return cbv, oh, id32, idenb, idenf, mA, mB


def make_rope(half):
    inv = (1.0 / (np.float32(10000.0) **
                  (np.arange(0, 128, 2, dtype=np.float32) / np.float32(128.0)))).astype(np.float32)
    pos = np.arange(T, dtype=np.float32)
    if half == 0:
        pos = np.maximum(pos - TOWN, 0.0).astype(np.float32)
    ang = pos[None, :] * inv[:, None]
    cosf = np.concatenate([np.cos(ang), np.cos(ang)], axis=0).astype(np.float32)
    sins = np.sin(ang).astype(np.float32)
    sinsg = np.concatenate([-sins, sins], axis=0).astype(np.float32)
    return cosf, sinsg


def kernel(L0_attn_hidden_states, L0_attn_Wq, L0_attn_Wk, L0_attn_Wv, L0_attn_Wo,
           L0_attn_Wg, L0_attn_Win, L0_attn_dt_bias, L0_attn_A_log_bias,
           L0_attn_gnorm_w, L0_ffn_w1, L0_ffn_w2, L0_ffn_w3):
    x = np.asarray(L0_attn_hidden_states, dtype=np.float32)
    B = x.shape[0]
    cbv, oh, id32, idenb, idenf, mA, mB = make_consts()
    wq = np.ascontiguousarray(np.asarray(L0_attn_Wq, np.float32).T)
    wk = np.ascontiguousarray(np.asarray(L0_attn_Wk, np.float32).T)
    wv = np.ascontiguousarray(np.asarray(L0_attn_Wv, np.float32).T)
    wg = np.ascontiguousarray(np.asarray(L0_attn_Wg, np.float32).T)
    win = np.ascontiguousarray(np.asarray(L0_attn_Win, np.float32).T)
    wo = np.ascontiguousarray(np.asarray(L0_attn_Wo, np.float32).T)
    dtbv = np.asarray(L0_attn_dt_bias, np.float32).reshape(32, 1)
    alogv = np.asarray(L0_attn_A_log_bias, np.float32).reshape(32, 1)
    gnw = np.broadcast_to(np.asarray(L0_attn_gnorm_w, np.float32)[None, :],
                          (128, 128)).copy()
    w1 = np.asarray(L0_ffn_w1, np.float32)
    w2 = np.asarray(L0_ffn_w2, np.float32)
    w3 = np.asarray(L0_ffn_w3, np.float32)
    HID = w1.shape[0]

    in_maps = []
    for r in range(NRANK):
        b, half = r // 2, r % 2
        xa = np.zeros((T, DM), dtype=np.float32)
        xa[TOWN:] = x[b, half * TOWN:(half + 1) * TOWN]
        if half == 1:
            xa[:TOWN] = x[b, 0:TOWN]
        cosf, sinsg = make_rope(half)
        w1r = np.zeros((HIDR, DM), dtype=np.float32)
        w2r = np.zeros((HIDR, DM), dtype=np.float32)
        lo = r * HIDR
        hi = min((r + 1) * HIDR, HID)
        if hi > lo:
            w1r[0:hi - lo] = w1[lo:hi]
            w2r[0:hi - lo] = w2[lo:hi]
        w3r = np.zeros((W3R, HID_PAD), dtype=np.float32)
        w3r[:, 0:HID] = w3[r * W3R:(r + 1) * W3R, :]
        in_maps.append(dict(
            x_all=xa, wqT=wq, wkT=wk, wvT=wv, wgT=wg, winT=win, woT=wo,
            cosf=cosf, sinsg=sinsg, dtb=dtbv, alog=alogv, gnwb=gnw, cb=cbv,
            onehot=oh, id32=id32, iden=idenb, idenf=idenf, maskA=mA, maskB=mB,
            w1s=w1r, w2s=w2r, w3s=w3r))

    nc = build_program(strict=False)
    res = run_bass_kernel_spmd(nc, in_maps, core_ids=list(range(NRANK)))
    outv = np.zeros((B, 2048, DM), dtype=np.float32)
    for r in range(NRANK):
        b, half = r // 2, r % 2
        outv[b, half * TOWN:(half + 1) * TOWN] = np.asarray(res.results[r]["out"])
    return outv


# version 08
# speedup vs baseline: 1.1747x; 1.0564x over previous
import os
import numpy as np
from contextlib import ExitStack

os.environ.setdefault("NEURON_SCRATCHPAD_PAGE_SIZE", "2048")

import concourse.bass as bass
import concourse.mybir as mybir
from concourse.bass_utils import run_bass_kernel_spmd

F32 = mybir.dt.float32
BF16 = mybir.dt.bfloat16
AF = mybir.ActivationFunctionType
ALU = mybir.AluOpType
AX = mybir.AxisListType

DM = 4096
D = 128
T = 2048
TOWN = 1024
NCH = 16
NOG = 8
OWN0 = 8
TB = 256
HIDR = 1408
NRANK = 8
HID_PAD = HIDR * NRANK
NSLOT = HID_PAD // 128
HSL = NSLOT // 2
W3R = DM // NRANK
EPS = 1e-5
SCALE_BIG = float(2.0 ** 100)

NF4_CODE = np.array([-1.0, -0.6961928009986877, -0.5250730514526367,
                     -0.39491748809814453, -0.28444138169288635, -0.18477343022823334,
                     -0.09105003625154495, 0.0, 0.07958029955625534, 0.16093020141124725,
                     0.24611230194568634, 0.33791524171829224, 0.44070982933044434,
                     0.5626170039176941, 0.7229568362236023, 1.0], dtype=np.float32)
NF4_BOUNDS = ((NF4_CODE[:-1] + NF4_CODE[1:]) * np.float32(0.5)).astype(np.float32)


def _nf4_deltas():
    deltas = np.zeros(15, dtype=np.float32)
    s = np.float32(NF4_CODE[0])
    for i in range(15):
        dd = np.float32(NF4_CODE[i + 1] - s)
        deltas[i] = dd
        s = np.float32(s + dd)
        assert s == NF4_CODE[i + 1]
    return deltas


NF4_DELTAS = _nf4_deltas()
CB_SIG0 = 0
CB_ONE = 15
CB_EPS = 16


class Sem:
    def __init__(self, stack, nc, name):
        self.h = stack.enter_context(nc.semaphore(name))
        self.n = 0

    def inc(self, instr, k=1):
        instr.then_inc(self.h, k)
        self.n += k


def build_program(nh=32, nkvh=8, do_ffn=True, do_ag=True, dbg=False, strict=False):
    gq = nh // nkvh
    nc = bass.Bass("TRN2", target_bir_lowering=False)
    dt_ = nc.dram_tensor

    x_all = dt_("x_all", [T, DM], F32, kind="ExternalInput").ap()
    wqT = dt_("wqT", [DM, nh * D], F32, kind="ExternalInput").ap()
    wkT = dt_("wkT", [DM, nkvh * D], F32, kind="ExternalInput").ap()
    wvT = dt_("wvT", [DM, nkvh * D], F32, kind="ExternalInput").ap()
    wgT = dt_("wgT", [DM, nh * D], F32, kind="ExternalInput").ap()
    winT = dt_("winT", [DM, 32], F32, kind="ExternalInput").ap()
    woT = dt_("woT", [nh * D, DM], F32, kind="ExternalInput").ap()
    cosf = dt_("cosf", [128, T], F32, kind="ExternalInput").ap()
    sinsg = dt_("sinsg", [128, T], F32, kind="ExternalInput").ap()
    dtb = dt_("dtb", [32, 1], F32, kind="ExternalInput").ap()
    alog = dt_("alog", [32, 1], F32, kind="ExternalInput").ap()
    gnwb = dt_("gnwb", [128, 128], F32, kind="ExternalInput").ap()
    cb = dt_("cb", [128, 32], F32, kind="ExternalInput").ap()
    onehot = dt_("onehot", [32, 32 * 128], F32, kind="ExternalInput").ap()
    id32 = dt_("id32", [32, 32], F32, kind="ExternalInput").ap()
    iden = dt_("iden", [128, 128], BF16, kind="ExternalInput").ap()
    idenf = dt_("idenf", [128, 128], F32, kind="ExternalInput").ap()
    maskA = dt_("maskA", [128, 128], BF16, kind="ExternalInput").ap()
    maskB = dt_("maskB", [128, 128], BF16, kind="ExternalInput").ap()
    ones32 = dt_("ones32", [32, 128], F32, kind="ExternalInput").ap()
    w1s = dt_("w1s", [HIDR, DM], F32, kind="ExternalInput").ap()
    w2s = dt_("w2s", [HIDR, DM], F32, kind="ExternalInput").ap()
    w3s = dt_("w3s", [W3R, HID_PAD], F32, kind="ExternalInput").ap()
    out = dt_("out", [TOWN, DM], F32, kind="ExternalOutput").ap()
    if dbg:
        o_x2 = dt_("o_x2", [TOWN, DM], F32, kind="ExternalOutput").ap()
        o_dn = dt_("o_dn", [TOWN, DM], F32, kind="ExternalOutput").ap()
        o_ag = dt_("o_ag", [NRANK * 128, 128], BF16, kind="ExternalOutput").ap()
        o_h = dt_("o_h", [128, 512], BF16, kind="ExternalOutput").ap()

    NT1 = (HIDR // 128) * (DM // 128)
    NT3 = NSLOT * (W3R // 128)
    NTS = 2 * NT1 + NT3
    ag_in = dt_("ag_in", [NTS * 128, 128], BF16)
    ag_out = dt_("ag_out", [NRANK * NTS * 128, 128], BF16)
    xbf = dt_("xbf", [T, DM], BF16)
    kv_sp = dt_("kv_sp", [2 * nkvh * T, 128], F32)
    y_sp = dt_("y_sp", [nh * TOWN, 128], BF16)
    attn_tm = dt_("attn_tm", [TOWN, DM], F32)
    x2sp = dt_("x2sp", [TOWN, DM], F32)
    x2bf = dt_("x2bf", [TOWN, DM], BF16)
    dn_part = dt_("dn_part", [DM, TOWN], F32)
    dn_tm = dt_("dn_tm", [TOWN, DM], F32)
    glb = dt_("glb", [32, NCH], F32)

    def ag1_off(wi, m, c):
        return (wi * NT1 + m * 32 + c) * 128

    def ag3_off(s, cl):
        return (2 * NT1 + s * 4 + cl) * 128

    def agout_off(rank, local):
        return rank * NTS * 128 + local

    est = ExitStack()
    with est:
        e = est.enter_context
        sCL = Sem(est, nc, "sCL")
        sXC = Sem(est, nc, "sXC")
        sXL = Sem(est, nc, "sXL")
        sWL = Sem(est, nc, "sWL")
        sPE = Sem(est, nc, "sPE")
        sDV = Sem(est, nc, "sDV")
        sAC = Sem(est, nc, "sAC")
        sDQL = Sem(est, nc, "sDQL")
        sDQV = Sem(est, nc, "sDQV")
        sDQA = Sem(est, nc, "sDQA")
        sDQS = Sem(est, nc, "sDQS")
        sAG = Sem(est, nc, "sAG")
        sSP = Sem(est, nc, "sSP")
        sFW = Sem(est, nc, "sFW")
        sF1 = [Sem(est, nc, "sF1a"), Sem(est, nc, "sF1b")]
        sF2 = [Sem(est, nc, "sF2a"), Sem(est, nc, "sF2b")]
        sW3p = [Sem(est, nc, "sW3a"), Sem(est, nc, "sW3b")]
        sST = Sem(est, nc, "sST")
        sOUT = Sem(est, nc, "sOUT")

        block = e(nc.Block())

        @block.gpsimd
        def _(_g):
            V, SC, TE, SY, GP = nc.vector, nc.scalar, nc.tensor, nc.sync, nc.gpsimd
            with ExitStack() as top:
                et = top.enter_context
                c_cb = et(nc.sbuf_tensor("c_cb", [128, 32], F32))
                c_iden = et(nc.sbuf_tensor("c_iden", [128, 128], BF16))
                c_idenf = et(nc.sbuf_tensor("c_idenf", [128, 128], F32))
                x_fm = et(nc.sbuf_tensor("x_fm", [128, 32 * TOWN], BF16))
                pA = et(nc.psum_tensor("pA", [128, 512], F32))
                pB = et(nc.psum_tensor("pB", [128, 512], F32))
                pC = et(nc.psum_tensor("pC", [128, 512], F32))
                pD = et(nc.psum_tensor("pD", [128, 512], F32))
                pE = et(nc.psum_tensor("pE", [128, 512], F32))
                pF = et(nc.psum_tensor("pF", [128, 512], F32))
                pU = [et(nc.psum_tensor(f"pU{i}", [128, 512], F32)) for i in range(2)]

                for dst, src in [(c_cb, cb), (c_iden, iden), (c_idenf, idenf)]:
                    sCL.inc(SY.dma_start(out=dst[:], in_=src), 16)
                for i in range(2):
                    sXC.inc(GP.dma_start(out=xbf.ap()[i * TOWN:(i + 1) * TOWN, :],
                                         in_=x_all[i * TOWN:(i + 1) * TOWN, :],
                                         max_dma_last_dim=8192), 16)

                def wait_engines(eng, sems):
                    for s in sems:
                        eng.wait_ge(s.h, s.n)

                def mkinc(sem, eng):
                    def f(instr, k=1):
                        sem.inc(instr, k)
                        if strict:
                            eng.wait_ge(sem.h, sem.n)
                    return f
                def barrier_all(sems):
                    for eng in (V, SC, TE, SY, GP):
                        for s in sems:
                            eng.wait_ge(s.h, s.n)

                dv = mkinc(sDV, nc.vector)
                ac = mkinc(sAC, nc.scalar)
                pe = mkinc(sPE, nc.tensor)
                dqv = mkinc(sDQV, nc.vector)
                dqa = mkinc(sDQA, nc.scalar)

                def load_w_slice(dstbuf, wap, col0, nchunks):
                    GP.wait_ge(sPE.h, sPE.n)
                    sWL.inc(GP.dma_start(
                        out=dstbuf.ap().rearrange("p (c k) -> p c k", k=128)[:, 0:nchunks, :],
                        in_=wap[:, col0:col0 + 128].rearrange("(c p) k -> p c k", p=128)), 16)

                with ExitStack() as attn:
                    ea = attn.enter_context
                    c_cos = ea(nc.sbuf_tensor("c_cos", [128, T], F32))
                    c_sin = ea(nc.sbuf_tensor("c_sin", [128, T], F32))
                    c_oh = ea(nc.sbuf_tensor("c_oh", [32, 32 * 128], F32))
                    c_id32 = ea(nc.sbuf_tensor("c_id32", [32, 32], F32))
                    c_mA = ea(nc.sbuf_tensor("c_mA", [128, 128], BF16))
                    c_mB = ea(nc.sbuf_tensor("c_mB", [128, 128], BF16))
                    c_gnw = ea(nc.sbuf_tensor("c_gnw", [128, 128], F32))
                    c_ones32 = ea(nc.sbuf_tensor("c_ones32", [32, 128], F32))
                    c_dtb = ea(nc.sbuf_tensor("c_dtb", [32, 1], F32))
                    c_alog = ea(nc.sbuf_tensor("c_alog", [32, 1], F32))
                    c_expA = ea(nc.sbuf_tensor("c_expA", [32, 1], F32))
                    c_win = ea(nc.sbuf_tensor("c_win", [128, 32 * 32], BF16))
                    gt = ea(nc.sbuf_tensor("gt", [32, T], F32))
                    ngt = ea(nc.sbuf_tensor("ngt", [32, T], F32))
                    zrow = ea(nc.sbuf_tensor("zrow", [32, 128], F32))
                    cd_t = ea(nc.sbuf_tensor("cd_t", [128, 32 * NCH], F32))
                    dttm = ea(nc.sbuf_tensor("dttm", [128, 32 * NCH], F32))
                    expgtm = ea(nc.sbuf_tensor("expgtm", [128, 32 * NCH], F32))
                    glrow = ea(nc.sbuf_tensor("glrow", [32, NCH], F32))
                    DQF = 512
                    dq_w = ea(nc.sbuf_tensor("dq_w", [128, DQF], F32))
                    dq_nm = ea(nc.sbuf_tensor("dq_nm", [128, DQF], F32))
                    dq_pl = [ea(nc.sbuf_tensor(f"dq_pl{i}", [128, DQF], BF16)) for i in range(2)]
                    dq_am = ea(nc.sbuf_tensor("dq_am", [128, DQF // 64], F32))
                    dq_amc = ea(nc.sbuf_tensor("dq_amc", [128, DQF // 64], F32))
                    dq_ram = ea(nc.sbuf_tensor("dq_ram", [128, DQF // 64], F32))
                    dq_o = ea(nc.sbuf_tensor("dq_o", [128, DQF], BF16))
                    dq_tr = ea(nc.sbuf_tensor("dq_tr", [128, DQF], BF16))

                    for dst, src in [(c_cos, cosf), (c_sin, sinsg), (c_oh, onehot),
                                     (c_id32, id32), (c_mA, maskA), (c_mB, maskB),
                                     (c_gnw, gnwb), (c_dtb, dtb), (c_alog, alog),
                                     (c_ones32, ones32)]:
                        sCL.inc(SY.dma_start(out=dst[:], in_=src), 16)
                    sWL.inc(GP.dma_start(
                        out=c_win.ap().rearrange("p (c k) -> p c k", k=32),
                        in_=winT.rearrange("(c p) k -> p c k", p=128)), 16)
                    wait_engines(SC, [sCL])
                    ac(SC.activation(out=c_expA[:], in_=c_alog[:], func=AF.Exp))
                    wait_engines(V, [sCL])
                    dv(V.memset(zrow[:], 0.0))

                    dq_list = []
                    for wi, wap in ((0, w1s), (1, w2s)):
                        for rb in range(HIDR // 128):
                            for cq in range(DM // DQF):
                                src = wap[rb * 128:(rb + 1) * 128, cq * DQF:(cq + 1) * DQF]
                                dests = [ag1_off(wi, rb, cq * (DQF // 128) + j) for j in range(DQF // 128)]
                                dq_list.append((src, dests))
                    for rb in range(W3R // 128):
                        for cq in range(HID_PAD // DQF):
                            src = w3s[rb * 128:(rb + 1) * 128, cq * DQF:(cq + 1) * DQF]
                            dests = [ag3_off(cq * (DQF // 128) + j, rb) for j in range(DQF // 128)]
                            dq_list.append((src, dests))
                    NDQ = len(dq_list)
                    dq_i = [0]

                    def dq_pump(k):
                        for _ in range(k):
                            ti = dq_i[0]
                            if ti >= NDQ:
                                return
                            dq_i[0] += 1
                            src, dests = dq_list[ti]
                            SY.wait_ge(sDQV.h, sDQV.n)
                            sDQL.inc(SY.dma_start(out=dq_w[:], in_=src), 16)
                            V.wait_ge(sDQL.h, sDQL.n)
                            V.wait_ge(sDQS.h, sDQS.n)
                            w3v = dq_w.ap().rearrange("p (b k) -> p b k", k=64)
                            dqv(V.tensor_reduce(out=dq_am[:], in_=w3v, axis=AX.X,
                                                     op=ALU.max, apply_absolute_value=True))
                            V.wait_ge(sDQV.h, sDQV.n)
                            dqv(V.tensor_scalar(out=dq_amc[:], in0=dq_am[:],
                                                     scalar1=1e-30, scalar2=None, op0=ALU.add))
                            V.wait_ge(sDQV.h, sDQV.n)
                            dqv(V.reciprocal(out=dq_ram[:], in_=dq_amc[:]))
                            V.wait_ge(sDQV.h, sDQV.n)
                            dqv(V.tensor_tensor(
                                out=dq_nm.ap().rearrange("p (b k) -> p b k", k=64),
                                in0=w3v,
                                in1=dq_ram.ap().unsqueeze(2).broadcast_to([128, DQF // 64, 64]),
                                op=ALU.mult))
                            norm_done = sDQV.n
                            chain0 = norm_done
                            plane0 = sDQA.n
                            SC.wait_ge(sDQV.h, norm_done)
                            for i in range(15):
                                if i >= 2:
                                    SC.wait_ge(sDQV.h, chain0 + (i - 1) * (DQF // 512))
                                dqa(SC.activation(out=dq_pl[i % 2][:], in_=dq_nm[:],
                                                       func=AF.Sigmoid,
                                                       bias=c_cb[:, CB_SIG0 + i:CB_SIG0 + i + 1],
                                                       scale=SCALE_BIG))
                            for i in range(15):
                                V.wait_ge(sDQA.h, plane0 + i + 1)
                                for cs in range(DQF // 512):
                                    sl = slice(cs * 512, (cs + 1) * 512)
                                    if i == 0:
                                        r = V.tensor_scalar(out=pU[cs][:], in0=dq_pl[0][:, sl],
                                                            scalar1=float(NF4_DELTAS[0]),
                                                            scalar2=float(NF4_CODE[0]),
                                                            op0=ALU.mult, op1=ALU.add)
                                    else:
                                        r = V.scalar_tensor_tensor(out=pU[cs][:],
                                                                   in0=dq_pl[i % 2][:, sl],
                                                                   scalar=float(NF4_DELTAS[i]),
                                                                   in1=pU[cs][:],
                                                                   op0=ALU.mult, op1=ALU.add)
                                    dqv(r)
                            V.wait_ge(sDQV.h, sDQV.n)
                            for cs in range(DQF // 512):
                                sl = slice(cs * 512, (cs + 1) * 512)
                                r = V.tensor_tensor(
                                    out=dq_o[:, sl].rearrange("p (b k) -> p b k", k=64),
                                    in0=pU[cs].ap().rearrange("p (b k) -> p b k", k=64),
                                    in1=dq_am[:, cs * 8:(cs + 1) * 8].unsqueeze(2)
                                          .broadcast_to([128, 8, 64]),
                                    op=ALU.mult)
                                dqv(r)
                            wait_engines(TE, [sAC, sDV])
                            TE.wait_ge(sDQV.h, sDQV.n)
                            for j in range(DQF // 128):
                                tr = TE.transpose(
                                    pF.ap().bitcast(BF16)[:, j * 128:(j + 1) * 128],
                                    dq_o[:, j * 128:(j + 1) * 128], c_iden[:])
                            pe(tr)
                            SC.wait_ge(sPE.h, sPE.n)
                            SC.wait_ge(sDQS.h, sDQS.n)
                            ac(SC.copy(out=dq_tr[:], in_=pF.ap().bitcast(BF16)[:, 0:DQF]))
                            SY.wait_ge(sAC.h, sAC.n)
                            runs = []
                            for dof in dests:
                                if runs and dof == runs[-1][1] + 128:
                                    runs[-1][1] = dof
                                else:
                                    runs.append([dof, dof])
                            off_j = 0
                            for lo, hi in runs:
                                npc = (hi - lo) // 128 + 1
                                sDQS.inc(SY.dma_start(
                                    out=ag_in[lo:lo + npc * 128, :]
                                    .rearrange("(j p) k -> p j k", p=128),
                                    in_=dq_tr[:, off_j * 128:(off_j + npc) * 128]
                                    .rearrange("p (j k) -> p j k", k=128)), 16)
                                off_j += npc

                    with ExitStack() as p1:
                        e1 = p1.enter_context
                        dtT = e1(nc.sbuf_tensor("dtT", [32, T], F32))
                        gsrc = e1(nc.sbuf_tensor("gsrc", [32, T], F32))
                        xfs = e1(nc.sbuf_tensor("xfs", [128, 32 * TB], BF16))
                        wb_k = e1(nc.sbuf_tensor("wb_k", [128, 32 * 128], BF16))
                        wb_v = e1(nc.sbuf_tensor("wb_v", [128, 32 * 128], BF16))
                        rp_a = e1(nc.sbuf_tensor("rp_a", [128, TB], F32))
                        rp_r = e1(nc.sbuf_tensor("rp_r", [128, TB], F32))
                        rp_b = e1(nc.sbuf_tensor("rp_b", [128, TB], F32))
                        khat = e1(nc.sbuf_tensor("khat", [128, TB], F32))
                        sp_st = e1(nc.sbuf_tensor("sp_st", [128, TB], F32))
                        dz = e1(nc.sbuf_tensor("dz", [32, TB], F32))
                        da1 = e1(nc.sbuf_tensor("da1", [32, TB], F32))
                        da2 = e1(nc.sbuf_tensor("da2", [32, TB], F32))

                        SY.wait_ge(sXC.h, sXC.n)
                        for c in range(32):
                            sXL.inc(SY.dma_start(
                                out=x_fm[:, c * TOWN:(c + 1) * TOWN],
                                in_=xbf.ap()[TOWN:2 * TOWN, c * 128:(c + 1) * 128],
                                transpose=True), 16)

                        for half in range(2):
                            for tb in range(TOWN // TB):
                                tg = half * (TOWN // TB) + tb
                                t0 = tg * TB
                                if half == 0:
                                    SY.wait_ge(sPE.h, sPE.n)
                                    for c in range(32):
                                        sXL.inc(SY.dma_start(
                                            out=xfs[:, c * TB:(c + 1) * TB],
                                            in_=xbf.ap()[t0:t0 + TB, c * 128:(c + 1) * 128],
                                            transpose=True), 16)

                                    def xr(c):
                                        return xfs[:, c * TB:(c + 1) * TB]
                                else:
                                    def xr(c):
                                        return x_fm[:, c * TOWN + tb * TB:
                                                    c * TOWN + tb * TB + TB]
                                tsl = slice(t0, t0 + TB)
                                wait_engines(TE, [sXL, sCL, sWL, sAC, sDV])
                                for c in range(32):
                                    mm = TE.matmul(pD[0:32, 0:TB], c_win[:, c * 32:(c + 1) * 32],
                                                   xr(c), start=(c == 0), stop=(c == 31))
                                pe(mm)
                                SC.wait_ge(sPE.h, sPE.n)
                                ac(SC.activation(out=dz[:], in_=pD[0:32, 0:TB],
                                                      func=AF.Identity, bias=c_dtb[:]))
                                SC.wait_ge(sAC.h, sAC.n)
                                ac(SC.activation(out=da1[:], in_=dz[:], func=AF.Abs))
                                SC.wait_ge(sAC.h, sAC.n)
                                ac(SC.activation(out=da1[:], in_=da1[:], func=AF.Exp,
                                                      scale=-1.0))
                                SC.wait_ge(sAC.h, sAC.n)
                                ac(SC.activation(out=da1[:], in_=da1[:], func=AF.Ln,
                                                      bias=c_cb[0:32, CB_ONE:CB_ONE + 1]))
                                wait_engines(V, [sAC])
                                dv(V.tensor_scalar(out=da2[:], in0=dz[:], scalar1=0.0,
                                                        scalar2=None, op0=ALU.max))
                                V.wait_ge(sDV.h, sDV.n)
                                dv(V.tensor_tensor(out=dtT[:, tsl], in0=da2[:],
                                                        in1=da1[:], op=ALU.add))
                                for kv in range(nkvh):
                                    for which in ("k", "v"):
                                        wb = wb_k if which == "k" else wb_v
                                        load_w_slice(wb, wkT if which == "k" else wvT,
                                                     kv * 128, 32)
                                        wait_engines(TE, [sWL, sAC, sDV])
                                        for c in range(32):
                                            mm = TE.matmul(pA[:, 0:TB],
                                                           wb[:, c * 128:(c + 1) * 128],
                                                           xr(c), start=(c == 0), stop=(c == 31))
                                        pe(mm)
                                        V.wait_ge(sPE.h, sPE.n)
                                        if which == "k":
                                            dv(V.tensor_tensor(out=rp_a[:], in0=pA[:, 0:TB],
                                                                    in1=c_cos[:, tsl], op=ALU.mult))
                                            dv(V.tensor_copy(out=rp_r[0:64, :],
                                                                  in_=pA[64:128, 0:TB]))
                                            dv(V.tensor_copy(out=rp_r[64:128, :],
                                                                  in_=pA[0:64, 0:TB]))
                                            V.wait_ge(sDV.h, sDV.n)
                                            dv(V.tensor_tensor(out=rp_b[:], in0=rp_r[:],
                                                                    in1=c_sin[:, tsl], op=ALU.mult))
                                            V.wait_ge(sDV.h, sDV.n)
                                            dv(V.tensor_tensor(out=khat[:], in0=rp_a[:],
                                                                    in1=rp_b[:], op=ALU.add))
                                            row0 = kv * T + t0
                                        else:
                                            dv(V.tensor_copy(out=khat[:], in_=pA[:, 0:TB]))
                                            row0 = nkvh * T + kv * T + t0
                                        wait_engines(TE, [sDV, sAC])
                                        for j in range(TB // 128):
                                            tr = TE.transpose(
                                                pB[:, j * 128:(j + 1) * 128],
                                                khat[:, j * 128:(j + 1) * 128], c_idenf[:])
                                        pe(tr)
                                        SC.wait_ge(sPE.h, sPE.n)
                                        SC.wait_ge(sSP.h, sSP.n)
                                        ac(SC.copy(out=sp_st[:], in_=pB[:, 0:TB]))
                                        SY.wait_ge(sAC.h, sAC.n)
                                        sSP.inc(SY.dma_start(
                                            out=kv_sp[row0:row0 + TB, :]
                                            .rearrange("(j p) k -> p j k", p=128),
                                            in_=sp_st[:].rearrange("p (j k) -> p j k", k=128)), 16)
                                    dq_pump(1)

                        wait_engines(V, [sDV, sAC])
                        dv(V.tensor_scalar(out=gsrc[:], in0=dtT[:], scalar1=c_expA[:],
                                                scalar2=-1.0, op0=ALU.mult, op1=ALU.mult))
                        V.wait_ge(sDV.h, sDV.n)
                        for n in range(NCH):
                            csl = slice(n * 128, (n + 1) * 128)
                            dv(V.tensor_tensor_scan(out=gt[:, csl], data0=gsrc[:, csl],
                                                         data1=zrow[:], initial=0.0,
                                                         op0=ALU.add, op1=ALU.add))
                        V.wait_ge(sDV.h, sDV.n)
                        dv(V.tensor_scalar(out=ngt[:], in0=gt[:], scalar1=-1.0,
                                                scalar2=None, op0=ALU.mult))
                        dv(V.tensor_copy(
                            out=glrow[:],
                            in_=gt.ap().rearrange("p (n k) -> p n k", k=128)[:, :, 127]))
                        V.wait_ge(sDV.h, sDV.n)
                        SY.wait_ge(sDV.h, sDV.n)
                        sST.inc(SY.dma_start(out=glb.ap(), in_=glrow[:]), 16)
                        SY.wait_ge(sST.h, sST.n)
                        sST.inc(SY.dma_start(
                            out=cd_t[:],
                            in_=glb.ap().rearrange("h n -> (h n)").unsqueeze(0)
                            .broadcast_to([128, 32 * NCH])), 16)
                        SC.wait_ge(sST.h, sST.n)
                        ac(SC.activation(out=cd_t[:], in_=cd_t[:], func=AF.Exp))
                        for n in range(NCH):
                            csl = slice(n * 128, (n + 1) * 128)
                            wait_engines(TE, [sDV, sAC])
                            TE.matmul(pD[:, 0:32], dtT[:, csl], c_id32[:], start=True, stop=True)
                            mm = TE.matmul(pD[:, 64:96], gt[:, csl], c_id32[:], start=True, stop=True)
                            pe(mm)
                            SC.wait_ge(sPE.h, sPE.n)
                            ac(SC.copy(out=dttm[:, n * 32:(n + 1) * 32], in_=pD[:, 0:32]))
                            ac(SC.activation(out=expgtm[:, n * 32:(n + 1) * 32],
                                                  in_=pD[:, 64:96], func=AF.Exp))
                        dq_pump(4)


                    barrier_all([sSP, sST, sDQS, sXL, sDV, sAC, sPE])
                    with ExitStack() as p2:
                        e2 = p2.enter_context
                        q_fm = e2(nc.sbuf_tensor("q_fm", [128, gq * TOWN], BF16))
                        sgate = e2(nc.sbuf_tensor("sgate", [128, NOG * gq * 128], BF16))
                        kT_sb = e2(nc.sbuf_tensor("kT_sb", [128, NCH * 128], F32))
                        vT_sb = e2(nc.sbuf_tensor("vT_sb", [128, NCH * 128], F32))
                        kf_sb = e2(nc.sbuf_tensor("kf_sb", [128, 128], BF16))
                        S_st = [e2(nc.sbuf_tensor(f"S_st{i}", [128, gq * 128], F32))
                                for i in range(2)]
                        S_bf4 = e2(nc.sbuf_tensor("S_bf4", [128, gq * 128], BF16))
                        e_sb4 = e2(nc.sbuf_tensor("e_sb4", [128, gq * 128], F32))
                        edec4 = e2(nc.sbuf_tensor("edec4", [128, gq], F32))
                        oh4 = e2(nc.sbuf_tensor("oh4", [32, gq], F32))
                        xc_4 = e2(nc.sbuf_tensor("xc_4", [128, gq * 128], BF16))
                        kd_4 = e2(nc.sbuf_tensor("kd_4", [128, gq * 128], BF16))
                        s2_4 = e2(nc.sbuf_tensor("s2_4", [128, gq * 128], BF16))
                        ytmp4 = e2(nc.sbuf_tensor("ytmp4", [128, gq * 128], F32))
                        y4 = e2(nc.sbuf_tensor("y4", [128, gq * 128], F32))
                        yg4 = e2(nc.sbuf_tensor("yg4", [128, gq * 128], BF16))
                        sil4 = e2(nc.sbuf_tensor("sil4", [128, gq * 128], F32))
                        ss4 = e2(nc.sbuf_tensor("ss4", [128, 4], F32))
                        rstd4 = e2(nc.sbuf_tensor("rstd4", [128, 4], F32))
                        ydump = e2(nc.sbuf_tensor("ydump", [128, 128], BF16))
                        wb_q = e2(nc.sbuf_tensor("wb_q", [128, 32 * 128], BF16))
                        wb_g = e2(nc.sbuf_tensor("wb_g", [128, 32 * 128], BF16))
                        rp2a = e2(nc.sbuf_tensor("rp2a", [128, 512], F32))
                        rp2r = e2(nc.sbuf_tensor("rp2r", [128, 512], F32))
                        rp2b = e2(nc.sbuf_tensor("rp2b", [128, 512], F32))

                        for kv in range(nkvh):
                            for hh in range(gq):
                                h = kv * gq + hh
                                load_w_slice(wb_q, wqT, h * 128, 32)
                                wait_engines(TE, [sWL, sDV, sAC])
                                for tb2 in range(2):
                                    tsl2 = slice(TOWN + tb2 * 512, TOWN + (tb2 + 1) * 512)
                                    wait_engines(TE, [sDV])
                                    for c in range(32):
                                        mm = TE.matmul(
                                            pA[:], wb_q[:, c * 128:(c + 1) * 128],
                                            x_fm[:, c * TOWN + tb2 * 512:c * TOWN + tb2 * 512 + 512],
                                            start=(c == 0), stop=(c == 31))
                                    pe(mm)
                                    V.wait_ge(sPE.h, sPE.n)
                                    dv(V.tensor_tensor(out=rp2a[:], in0=pA[:],
                                                       in1=c_cos[:, tsl2], op=ALU.mult))
                                    dv(V.tensor_copy(out=rp2r[0:64, :], in_=pA[64:128, :]))
                                    dv(V.tensor_copy(out=rp2r[64:128, :], in_=pA[0:64, :]))
                                    V.wait_ge(sDV.h, sDV.n)
                                    dv(V.tensor_tensor(out=rp2b[:], in0=rp2r[:],
                                                       in1=c_sin[:, tsl2], op=ALU.mult))
                                    V.wait_ge(sDV.h, sDV.n)
                                    dv(V.tensor_tensor(
                                        out=q_fm[:, hh * TOWN + tb2 * 512:hh * TOWN + tb2 * 512 + 512],
                                        in0=rp2a[:], in1=rp2b[:], op=ALU.add))
                                load_w_slice(wb_g, wgT, h * 128, 32)
                                wait_engines(TE, [sWL, sAC])
                                for gb in range(2):
                                    wait_engines(TE, [sAC])
                                    for g4 in range(4):
                                        g = gb * 4 + g4
                                        for c in range(32):
                                            mm = TE.matmul(
                                                pB[:, g4 * 128:(g4 + 1) * 128],
                                                x_fm[:, c * TOWN + g * 128:c * TOWN + (g + 1) * 128],
                                                wb_g[:, c * 128:(c + 1) * 128],
                                                start=(c == 0), stop=(c == 31))
                                    pe(mm)
                                    SC.wait_ge(sPE.h, sPE.n)
                                    SC.wait_ge(sDV.h, sDV.n)
                                    for g4 in range(4):
                                        g = gb * 4 + g4
                                        a2 = SC.copy(
                                            out=sgate[:, (g * gq + hh) * 128:(g * gq + hh + 1) * 128],
                                            in_=pB[:, g4 * 128:(g4 + 1) * 128])
                                    ac(a2)
                                dq_pump(2)
                            SY.wait_ge(sSP.h, sSP.n)
                            SY.wait_ge(sDV.h, sDV.n)
                            SY.wait_ge(sPE.h, sPE.n)
                            sXL.inc(SY.dma_start(
                                out=kT_sb.ap().rearrange("p (n k) -> p n k", k=128),
                                in_=kv_sp[kv * T:(kv + 1) * T, :]
                                .rearrange("(n p) k -> p n k", p=128)), 16)
                            sXL.inc(SY.dma_start(
                                out=vT_sb.ap().rearrange("p (n k) -> p n k", k=128),
                                in_=kv_sp[nkvh * T + kv * T:nkvh * T + (kv + 1) * T, :]
                                .rearrange("(n p) k -> p n k", p=128)), 16)
                            wait_engines(V, [sXL, sAC])
                            dv(V.memset(S_st[0][:], 0.0))

                            for n in range(NCH):
                                own = n >= OWN0
                                csl = slice(n * 128, (n + 1) * 128)
                                par = n % 2
                                if own:
                                    wait_engines(TE, [sDV, sAC])
                                    tr = TE.transpose(pF[:, 0:128],
                                                      kT_sb[:, csl], c_idenf[:])
                                    pe(tr)
                                    V.wait_ge(sPE.h, sPE.n)
                                    dv(V.tensor_copy(out=kf_sb[:], in_=pF[:, 0:128]))
                                    wait_engines(TE, [sDV])
                                    for hh in range(gq):
                                        h = kv * gq + hh
                                        ohsl = c_oh[:, h * 128:(h + 1) * 128]
                                        co = slice(hh * 128, (hh + 1) * 128)
                                        TE.matmul(pE[:, co], ohsl, gt[:, csl], start=True, stop=False)
                                        TE.matmul(pE[:, co], ngt[:, csl], ohsl, start=False, stop=False)
                                        TE.matmul(pE[:, co], c_mA[:], c_mB[:], start=False, stop=True)
                                        mm = TE.matmul(pD[:, co], kf_sb[:],
                                                       q_fm[:, hh * TOWN + (n - OWN0) * 128:
                                                            hh * TOWN + (n - OWN0 + 1) * 128],
                                                       start=True, stop=True)
                                    pe(mm)
                                    SC.wait_ge(sPE.h, sPE.n)
                                    SC.wait_ge(sDV.h, sDV.n)
                                    ac(SC.activation(out=e_sb4[:], in_=pE[:, 0:512], func=AF.Exp))
                                    wait_engines(V, [sAC])
                                    dv(V.tensor_tensor(out=s2_4[:], in0=pD[:, 0:512],
                                                       in1=e_sb4[:], op=ALU.mult))
                                    for hh in range(gq):
                                        h = kv * gq + hh
                                        co = slice(hh * 128, (hh + 1) * 128)
                                        dv(V.tensor_scalar(out=xc_4[:, co], in0=vT_sb[:, csl],
                                                           scalar1=dttm[:, n * 32 + h:n * 32 + h + 1],
                                                           scalar2=None, op0=ALU.mult))
                                        dv(V.tensor_scalar(out=kd_4[:, co], in0=kT_sb[:, csl],
                                                           scalar1=e_sb4[:, hh * 128 + 127:hh * 128 + 128],
                                                           scalar2=None, op0=ALU.mult))
                                    dv(V.tensor_copy(out=S_bf4[:], in_=S_st[par][:]))
                                    V.wait_ge(sDV.h, sDV.n)
                                    wait_engines(TE, [sDV, sAC])
                                    for hh in range(gq):
                                        co = slice(hh * 128, (hh + 1) * 128)
                                        TE.matmul(pB[:, co],
                                                  q_fm[:, hh * TOWN + (n - OWN0) * 128:
                                                       hh * TOWN + (n - OWN0 + 1) * 128],
                                                  S_bf4[:, co], start=True, stop=True)
                                        TE.matmul(pA[:, co], s2_4[:, co], xc_4[:, co],
                                                  start=True, stop=True)
                                        mm = TE.matmul(pC[:, co], kd_4[:, co], xc_4[:, co],
                                                       start=True, stop=True)
                                    pe(mm)
                                    SC.wait_ge(sPE.h, sPE.n)
                                    SC.wait_ge(sDV.h, sDV.n)
                                    for hh in range(gq):
                                        h = kv * gq + hh
                                        co = slice(hh * 128, (hh + 1) * 128)
                                        a2 = SC.activation(out=ytmp4[:, co], in_=pB[:, co],
                                                           func=AF.Identity,
                                                           scale=expgtm[:, n * 32 + h:n * 32 + h + 1])
                                    ac(a2)
                                    wait_engines(V, [sAC])
                                    dv(V.tensor_tensor(out=y4[:], in0=pA[:, 0:512],
                                                       in1=ytmp4[:], op=ALU.add))
                                    for hh in range(gq):
                                        h = kv * gq + hh
                                        co = slice(hh * 128, (hh + 1) * 128)
                                        dv(V.scalar_tensor_tensor(
                                            out=S_st[1 - par][:, co], in0=S_st[par][:, co],
                                            scalar=cd_t[:, h * NCH + n:h * NCH + n + 1],
                                            in1=pC[:, co], op0=ALU.mult, op1=ALU.add))
                                    SC.wait_ge(sDV.h, sDV.n)
                                    for hh in range(gq):
                                        co = slice(hh * 128, (hh + 1) * 128)
                                        ac(SC.activation(out=ydump[:], in_=y4[:, co],
                                                         func=AF.Square,
                                                         accum_out=ss4[:, hh:hh + 1]))
                                    ac(SC.activation(out=rstd4[:], in_=ss4[:], func=AF.Ln,
                                                       scale=1.0 / 128.0,
                                                       bias=c_cb[:, CB_EPS:CB_EPS + 1]))
                                    ac(SC.activation(out=rstd4[:], in_=rstd4[:],
                                                     func=AF.Exp, scale=-0.5))
                                    g = n - OWN0
                                    SC.wait_ge(sDV.h, sDV.n)
                                    ac(SC.activation(out=sil4[:],
                                                     in_=sgate[:, g * gq * 128:(g + 1) * gq * 128],
                                                     func=AF.Silu))
                                    wait_engines(V, [sAC])
                                    V.wait_ge(sSP.h, sSP.n)
                                    for hh in range(gq):
                                        h = kv * gq + hh
                                        co = slice(hh * 128, (hh + 1) * 128)
                                        dv(V.scalar_tensor_tensor(
                                            out=ytmp4[:, co], in0=y4[:, co],
                                            scalar=rstd4[:, hh:hh + 1], in1=c_gnw[:],
                                            op0=ALU.mult, op1=ALU.mult))
                                    V.wait_ge(sDV.h, sDV.n)
                                    dv(V.tensor_tensor(out=yg4[:], in0=ytmp4[:],
                                                       in1=sil4[:], op=ALU.mult))
                                    SY.wait_ge(sDV.h, sDV.n)
                                    for hh in range(gq):
                                        h = kv * gq + hh
                                        co = slice(hh * 128, (hh + 1) * 128)
                                        sSP.inc(SY.dma_start(
                                            out=y_sp[h * TOWN + g * 128:h * TOWN + (g + 1) * 128, :],
                                            in_=yg4[:, co]), 16)
                                else:
                                    wait_engines(V, [sAC, sPE])
                                    dv(V.tensor_scalar(out=oh4[:], 
                                                       in0=c_oh.ap().rearrange("p (h k) -> p h k", k=128)[:, kv * gq:(kv + 1) * gq, 0],
                                                       scalar1=glrow[:, n:n + 1], scalar2=None,
                                                       op0=ALU.mult))
                                    V.wait_ge(sDV.h, sDV.n)
                                    wait_engines(TE, [sDV, sAC])
                                    TE.matmul(pE[:, 0:gq], c_ones32[:], oh4[:], start=True, stop=False)
                                    mm = TE.matmul(pE[:, 0:gq], ngt[:, csl],
                                                   c_oh.ap().rearrange("p (h k) -> p h k", k=128)[:, kv * gq:(kv + 1) * gq, 0],
                                                   start=False, stop=True)
                                    pe(mm)
                                    SC.wait_ge(sPE.h, sPE.n)
                                    SC.wait_ge(sDV.h, sDV.n)
                                    ac(SC.activation(out=edec4[:], in_=pE[:, 0:gq], func=AF.Exp))
                                    wait_engines(V, [sAC])
                                    for hh in range(gq):
                                        h = kv * gq + hh
                                        co = slice(hh * 128, (hh + 1) * 128)
                                        dv(V.tensor_scalar(out=xc_4[:, co], in0=vT_sb[:, csl],
                                                           scalar1=dttm[:, n * 32 + h:n * 32 + h + 1],
                                                           scalar2=None, op0=ALU.mult))
                                        dv(V.tensor_scalar(out=kd_4[:, co], in0=kT_sb[:, csl],
                                                           scalar1=edec4[:, hh:hh + 1],
                                                           scalar2=None, op0=ALU.mult))
                                    V.wait_ge(sDV.h, sDV.n)
                                    wait_engines(TE, [sDV])
                                    for hh in range(gq):
                                        co = slice(hh * 128, (hh + 1) * 128)
                                        mm = TE.matmul(pC[:, co], kd_4[:, co], xc_4[:, co],
                                                       start=True, stop=True)
                                    pe(mm)
                                    V.wait_ge(sPE.h, sPE.n)
                                    for hh in range(gq):
                                        h = kv * gq + hh
                                        co = slice(hh * 128, (hh + 1) * 128)
                                        dv(V.scalar_tensor_tensor(
                                            out=S_st[1 - par][:, co], in0=S_st[par][:, co],
                                            scalar=cd_t[:, h * NCH + n:h * NCH + n + 1],
                                            in1=pC[:, co], op0=ALU.mult, op1=ALU.add))
                            dq_pump(3)

                        dq_pump(NDQ)
                        if do_ag:
                            GP.wait_ge(sDQS.h, sDQS.n)
                            cc = GP.collective_compute(
                                "AllGather", ALU.bypass, replica_groups=[list(range(NRANK))],
                                ins=[ag_in.ap().opt()], outs=[ag_out.ap().opt()])
                            sAG.inc(cc, 1)
                        elif do_ffn:
                            GP.wait_ge(sDQS.h, sDQS.n)
                            for r in range(NRANK):
                                d = GP.dma_start(out=ag_out.ap()[r * NTS * 128:(r + 1) * NTS * 128, :],
                                                 in_=ag_in.ap())
                                sAG.inc(d, 16)

                barrier_all([sSP, sST, sDQS, sXL, sDV, sAC, sPE, sAG])
                with ExitStack() as p3:
                    e3 = p3.enter_context
                    yb = e3(nc.sbuf_tensor("yb", [128, nh * TOWN], BF16))
                    wb_o = e3(nc.sbuf_tensor("wb_o", [128, nh * 128], BF16))
                    wo_st = e3(nc.sbuf_tensor("wo_st", [128, 512], F32))
                    wo_o = e3(nc.sbuf_tensor("wo_o", [128, 512], F32))
                    SY.wait_ge(sSP.h, sSP.n)
                    for hec in range(nh):
                        sXL.inc(SY.dma_start(
                            out=yb[:, hec * TOWN:(hec + 1) * TOWN],
                            in_=y_sp[hec * TOWN:(hec + 1) * TOWN, :],
                            transpose=True), 16)
                    for m in range(32):
                        load_w_slice(wb_o, woT, m * 128, nh)
                        for th in range(2):
                            wait_engines(TE, [sWL, sXL, sDV])
                            for hec in range(nh):
                                mm = TE.matmul(
                                    pA[:], wb_o[:, hec * 128:(hec + 1) * 128],
                                    yb[:, hec * TOWN + th * 512:hec * TOWN + th * 512 + 512],
                                    start=(hec == 0), stop=(hec == nh - 1))
                            pe(mm)
                            V.wait_ge(sPE.h, sPE.n)
                            dv(V.tensor_copy(out=wo_st[:], in_=pA[:]))
                            wait_engines(TE, [sDV, sAC])
                            for j in range(4):
                                tr = TE.transpose(pF[:, j * 128:(j + 1) * 128],
                                                  wo_st[:, j * 128:(j + 1) * 128],
                                                  c_idenf[:])
                            pe(tr)
                            V.wait_ge(sPE.h, sPE.n)
                            V.wait_ge(sST.h, sST.n)
                            dv(V.tensor_copy(out=wo_o[:], in_=pF[:]))
                            SY.wait_ge(sDV.h, sDV.n)
                            sST.inc(SY.dma_start(
                                out=attn_tm.ap()[th * 512:(th + 1) * 512,
                                                 m * 128:(m + 1) * 128]
                                .rearrange("(j p) k -> p j k", p=128),
                                in_=wo_o[:].rearrange("p (j k) -> p j k", k=128)), 16)

                barrier_all([sSP, sST, sXL, sDV, sAC, sPE])
                with ExitStack() as p4:
                    e4 = p4.enter_context
                    ln_x = e4(nc.sbuf_tensor("ln_x", [128, DM], F32))
                    ln_a = e4(nc.sbuf_tensor("ln_a", [128, DM], F32))
                    ln_b = e4(nc.sbuf_tensor("ln_b", [128, DM], BF16))
                    ln_s = e4(nc.sbuf_tensor("ln_s", [128, 8], F32))

                    def ln_pass(res_ap, add_ap, spill_f32, spill_bf):
                        SY.wait_ge(sST.h, sST.n)
                        SY.wait_ge(sOUT.h, sOUT.n)
                        sXL.inc(SY.dma_start(out=ln_x[:], in_=res_ap), 16)
                        sXL.inc(SY.dma_start(out=ln_a[:], in_=add_ap), 16)
                        wait_engines(V, [sXL, sAC])
                        dv(V.tensor_tensor(out=ln_x[:], in0=ln_x[:], in1=ln_a[:],
                                                op=ALU.add))
                        SC.wait_ge(sDV.h, sDV.n)
                        ac(SC.activation(out=ln_b[:], in_=ln_x[:], func=AF.Square,
                                              accum_out=ln_s[:, 0:1]))
                        SC.wait_ge(sAC.h, sAC.n)
                        ac(SC.activation(out=ln_b[:], in_=ln_x[:], func=AF.Identity,
                                              accum_out=ln_s[:, 1:2]))
                        ac(SC.mul(out=ln_s[:, 2:3], in_=ln_s[:, 1:2], mul=1.0 / DM))
                        ac(SC.mul(out=ln_s[:, 3:4], in_=ln_s[:, 0:1], mul=1.0 / DM))
                        wait_engines(V, [sAC])
                        dv(V.tensor_scalar(out=ln_s[:, 4:5], in0=ln_s[:, 2:3],
                                                scalar1=ln_s[:, 2:3], scalar2=None,
                                                op0=ALU.mult))
                        V.wait_ge(sDV.h, sDV.n)
                        dv(V.tensor_tensor(out=ln_s[:, 5:6], in0=ln_s[:, 3:4],
                                                in1=ln_s[:, 4:5], op=ALU.subtract))
                        SC.wait_ge(sDV.h, sDV.n)
                        ac(SC.activation(out=ln_s[:, 6:7], in_=ln_s[:, 5:6], func=AF.Ln,
                                              bias=c_cb[:, CB_EPS:CB_EPS + 1]))
                        SC.wait_ge(sAC.h, sAC.n)
                        ac(SC.activation(out=ln_s[:, 7:8], in_=ln_s[:, 6:7], func=AF.Exp,
                                              scale=-0.5))
                        wait_engines(V, [sAC])
                        dv(V.tensor_scalar(out=ln_x[:], in0=ln_x[:],
                                                scalar1=ln_s[:, 2:3], scalar2=ln_s[:, 7:8],
                                                op0=ALU.subtract, op1=ALU.mult))
                        SY.wait_ge(sDV.h, sDV.n)
                        if spill_f32 is not None:
                            sST.inc(SY.dma_start(out=spill_f32, in_=ln_x[:]), 16)
                        if spill_bf is not None:
                            dv(V.tensor_copy(out=ln_b[:], in_=ln_x[:]))
                            SY.wait_ge(sDV.h, sDV.n)
                            sST.inc(SY.dma_start(out=spill_bf, in_=ln_b[:]), 16)

                    SY.wait_ge(sST.h, sST.n)
                    for g in range(NOG):
                        ln_pass(x_all[TOWN + g * 128:TOWN + (g + 1) * 128, :],
                                attn_tm.ap()[g * 128:(g + 1) * 128, :],
                                x2sp.ap()[g * 128:(g + 1) * 128, :],
                                x2bf.ap()[g * 128:(g + 1) * 128, :] if do_ffn else None)

                    if not do_ffn:
                        for g in range(NOG):
                            SY.wait_ge(sST.h, sST.n)
                            SY.wait_ge(sOUT.h, sOUT.n)
                            sXL.inc(SY.dma_start(out=ln_x[:],
                                                 in_=x2sp.ap()[g * 128:(g + 1) * 128, :]), 16)
                            SY.wait_ge(sXL.h, sXL.n)
                            sOUT.inc(SY.dma_start(out=out[g * 128:(g + 1) * 128, :],
                                                  in_=ln_x[:]), 16)
                        SY.wait_ge(sOUT.h, sOUT.n)
                        return

                    SY.wait_ge(sST.h, sST.n)
                    SY.wait_ge(sPE.h, sPE.n)
                    for c in range(32):
                        sXL.inc(SY.dma_start(out=x_fm[:, c * TOWN:(c + 1) * TOWN],
                                             in_=x2bf.ap()[:, c * 128:(c + 1) * 128],
                                             transpose=True), 16)

                    barrier_all([sST, sXL, sDV, sAC, sPE])
                    with ExitStack() as p5:
                        e5 = p5.enter_context
                        HS2 = NSLOT // 4
                        h_fm = e5(nc.sbuf_tensor("h_fm", [128, HS2 * 512], BF16))
                        fw = [e5(nc.sbuf_tensor(f"fw{i}", [128, 32 * 128], BF16))
                              for i in range(4)]
                        w3b = [e5(nc.sbuf_tensor(f"w3b{i}", [128, HS2 * 128], BF16))
                               for i in range(2)]
                        sg_sb = [e5(nc.sbuf_tensor(f"sg_sb{i}", [128, 512], F32))
                                 for i in range(2)]
                        dn_st = e5(nc.sbuf_tensor("dn_st", [128, 512], F32))
                        dn_p = e5(nc.sbuf_tensor("dn_p", [128, 512], F32))
                        dn_o = e5(nc.sbuf_tensor("dn_o", [128, 512], F32))

                        SY.wait_ge(sAG.h, sAG.n)
                        gu_banks = [(pA, pB), (pC, pD)]

                        def gu_load(s, par, pe_gate):
                            if pe_gate[par] is not None:
                                SY.wait_ge(sPE.h, pe_gate[par])
                            rank, ml = s // 11, s % 11
                            b1 = agout_off(rank, ag1_off(0, ml, 0))
                            b2 = agout_off(rank, ag1_off(1, ml, 0))
                            for sem, bb, bs in ((sF1[par], fw[2 * par], b1),
                                                (sF2[par], fw[2 * par + 1], b2)):
                                sem.inc(SY.dma_start(
                                    out=bb.ap().rearrange("p (c k) -> p c k", k=128),
                                    in_=ag_out.ap()[bs:bs + 32 * 128, :]
                                    .rearrange("(c p) k -> p c k", p=128)), 16)
                            return (sF1[par].n, sF2[par].n)

                        for tt in range(2):
                            tof = tt * 512
                            for hhf in range(4):
                                s0 = hhf * HS2
                                pe_gate = [None, None]
                                fw_ready = {}
                                fw_ready[0] = gu_load(s0, 0, pe_gate)
                                if HS2 > 1:
                                    fw_ready[1] = gu_load(s0 + 1, 1, pe_gate)
                                dv_gate = {}
                                for sl_ in range(HS2):
                                    s = s0 + sl_
                                    par = sl_ % 2
                                    bg, bu = gu_banks[par]
                                    TE.wait_ge(sF1[par].h, fw_ready[sl_][0])
                                    TE.wait_ge(sF2[par].h, fw_ready[sl_][1])
                                    TE.wait_ge(sXL.h, sXL.n)
                                    if sl_ >= 2:
                                        TE.wait_ge(sDV.h, dv_gate[sl_ - 2])
                                    for c in range(32):
                                        mm = TE.matmul(bg[:], fw[2 * par][:, c * 128:(c + 1) * 128],
                                                       x_fm[:, c * TOWN + tof:c * TOWN + tof + 512],
                                                       start=(c == 0), stop=(c == 31))
                                    pe(mm)
                                    gate_done = sPE.n
                                    for c in range(32):
                                        mm = TE.matmul(bu[:], fw[2 * par + 1][:, c * 128:(c + 1) * 128],
                                                       x_fm[:, c * TOWN + tof:c * TOWN + tof + 512],
                                                       start=(c == 0), stop=(c == 31))
                                    pe(mm)
                                    pe_gate[par] = sPE.n
                                    if sl_ + 2 < HS2:
                                        fw_ready[sl_ + 2] = gu_load(s0 + sl_ + 2, par, pe_gate)
                                    SC.wait_ge(sPE.h, gate_done)
                                    SC.wait_ge(sDV.h, dv_gate.get(sl_ - 1, 0))
                                    ac(SC.activation(out=sg_sb[par][:], in_=bg[:], func=AF.Silu))
                                    V.wait_ge(sAC.h, sAC.n)
                                    V.wait_ge(sPE.h, pe_gate[par])
                                    dv(V.tensor_tensor(out=h_fm[:, sl_ * 512:(sl_ + 1) * 512],
                                                       in0=bu[:], in1=sg_sb[par][:], op=ALU.mult))
                                    dv_gate[sl_] = sDV.n
                                dn_banks = [pE, pF]
                                w3_ready = {}
                                pe_dn = [None, None]

                                def w3_load(m, par):
                                    if pe_dn[par] is not None:
                                        SY.wait_ge(sPE.h, pe_dn[par])
                                    rank3, cl = m // 4, m % 4
                                    base3 = agout_off(rank3, ag3_off(s0, 0))
                                    sW3p[par].inc(SY.dma_start(
                                        out=w3b[par].ap().rearrange("p (s k) -> p s k", k=128),
                                        in_=ag_out.ap()[base3:base3 + HS2 * 4 * 128, :]
                                        .rearrange("(s q p) k -> p s q k", q=4, p=128)[:, :, cl, :]),
                                        16)
                                    return sW3p[par].n

                                w3_ready[0] = w3_load(0, 0)
                                w3_ready[1] = w3_load(1, 1)
                                for m in range(32):
                                    par = m % 2
                                    bk = dn_banks[par]
                                    TE.wait_ge(sW3p[par].h, w3_ready[m])
                                    TE.wait_ge(sDV.h, sDV.n)
                                    for s_ in range(HS2):
                                        mm = TE.matmul(bk[:], w3b[par][:, s_ * 128:(s_ + 1) * 128],
                                                       h_fm[:, s_ * 512:(s_ + 1) * 512],
                                                       start=(s_ == 0), stop=(s_ == HS2 - 1))
                                    pe(mm)
                                    pe_dn[par] = sPE.n
                                    if m + 2 < 32:
                                        w3_ready[m + 2] = w3_load(m + 2, par)
                                    V.wait_ge(sPE.h, pe_dn[par])
                                    V.wait_ge(sST.h, sST.n)
                                    if hhf == 0:
                                        dv(V.tensor_copy(out=dn_st[:], in_=bk[:]))
                                        SY.wait_ge(sDV.h, sDV.n)
                                        sST.inc(SY.dma_start(
                                            out=dn_part.ap()[m * 128:(m + 1) * 128, tof:tof + 512],
                                            in_=dn_st[:]), 16)
                                    else:
                                        SY.wait_ge(sDV.h, sDV.n)
                                        sXL.inc(SY.dma_start(
                                            out=dn_p[:],
                                            in_=dn_part.ap()[m * 128:(m + 1) * 128, tof:tof + 512]), 16)
                                        V.wait_ge(sXL.h, sXL.n)
                                        dv(V.tensor_tensor(out=dn_st[:], in0=bk[:],
                                                           in1=dn_p[:], op=ALU.add))
                                        if hhf < 3:
                                            SY.wait_ge(sDV.h, sDV.n)
                                            sST.inc(SY.dma_start(
                                                out=dn_part.ap()[m * 128:(m + 1) * 128, tof:tof + 512],
                                                in_=dn_st[:]), 16)
                                        else:
                                            wait_engines(TE, [sDV, sAC])
                                            for j in range(4):
                                                tr = TE.transpose(pU[0][:, j * 128:(j + 1) * 128],
                                                                  dn_st[:, j * 128:(j + 1) * 128],
                                                                  c_idenf[:])
                                            pe(tr)
                                            V.wait_ge(sPE.h, sPE.n)
                                            dv(V.tensor_copy(out=dn_o[:], in_=pU[0][:]))
                                            SY.wait_ge(sDV.h, sDV.n)
                                            sST.inc(SY.dma_start(
                                                out=dn_tm.ap()[tof:tof + 512, m * 128:(m + 1) * 128]
                                                .rearrange("(j p) k -> p j k", p=128),
                                                in_=dn_o[:].rearrange("p (j k) -> p j k", k=128)), 16)

                    SY.wait_ge(sST.h, sST.n)
                    for g in range(NOG):
                        ln_pass(x2sp.ap()[g * 128:(g + 1) * 128, :],
                                dn_tm.ap()[g * 128:(g + 1) * 128, :], None, None)
                        SY.wait_ge(sDV.h, sDV.n)
                        sOUT.inc(SY.dma_start(out=out[g * 128:(g + 1) * 128, :],
                                              in_=ln_x[:]), 16)
                    SY.wait_ge(sOUT.h, sOUT.n)

    return nc



def make_consts(nh=32):
    import ml_dtypes
    cbv = np.zeros((128, 32), dtype=np.float32)
    cbv[:, 0:15] = (-(NF4_BOUNDS.astype(np.float64) * 2.0 ** 100)).astype(np.float32)[None, :]
    cbv[:, CB_ONE] = 1.0
    cbv[:, CB_EPS] = EPS
    oh = np.zeros((32, 32 * 128), dtype=np.float32)
    for h in range(32):
        oh[h, h * 128:(h + 1) * 128] = 1.0
    id32 = np.eye(32, dtype=np.float32)
    idenb = np.eye(128, dtype=ml_dtypes.bfloat16)
    idenf = np.eye(128, dtype=np.float32)
    k = np.arange(128)
    mA = (k[:, None] < k[None, :]).astype(ml_dtypes.bfloat16)
    mB = (-200.0 * (k[:, None] >= k[None, :])).astype(ml_dtypes.bfloat16)
    return cbv, oh, id32, idenb, idenf, mA, mB


def make_rope(half):
    inv = (1.0 / (np.float32(10000.0) **
                  (np.arange(0, 128, 2, dtype=np.float32) / np.float32(128.0)))).astype(np.float32)
    pos = np.arange(T, dtype=np.float32)
    if half == 0:
        pos = np.maximum(pos - TOWN, 0.0).astype(np.float32)
    ang = pos[None, :] * inv[:, None]
    cosf = np.concatenate([np.cos(ang), np.cos(ang)], axis=0).astype(np.float32)
    sins = np.sin(ang).astype(np.float32)
    sinsg = np.concatenate([-sins, sins], axis=0).astype(np.float32)
    return cosf, sinsg


def kernel(L0_attn_hidden_states, L0_attn_Wq, L0_attn_Wk, L0_attn_Wv, L0_attn_Wo,
           L0_attn_Wg, L0_attn_Win, L0_attn_dt_bias, L0_attn_A_log_bias,
           L0_attn_gnorm_w, L0_ffn_w1, L0_ffn_w2, L0_ffn_w3):
    x = np.asarray(L0_attn_hidden_states, dtype=np.float32)
    B = x.shape[0]
    cbv, oh, id32, idenb, idenf, mA, mB = make_consts()
    wq = np.ascontiguousarray(np.asarray(L0_attn_Wq, np.float32).T)
    wk = np.ascontiguousarray(np.asarray(L0_attn_Wk, np.float32).T)
    wv = np.ascontiguousarray(np.asarray(L0_attn_Wv, np.float32).T)
    wg = np.ascontiguousarray(np.asarray(L0_attn_Wg, np.float32).T)
    win = np.ascontiguousarray(np.asarray(L0_attn_Win, np.float32).T)
    wo = np.ascontiguousarray(np.asarray(L0_attn_Wo, np.float32).T)
    dtbv = np.asarray(L0_attn_dt_bias, np.float32).reshape(32, 1)
    alogv = np.asarray(L0_attn_A_log_bias, np.float32).reshape(32, 1)
    gnw = np.broadcast_to(np.asarray(L0_attn_gnorm_w, np.float32)[None, :],
                          (128, 128)).copy()
    w1 = np.asarray(L0_ffn_w1, np.float32)
    w2 = np.asarray(L0_ffn_w2, np.float32)
    w3 = np.asarray(L0_ffn_w3, np.float32)
    HID = w1.shape[0]

    in_maps = []
    for r in range(NRANK):
        b, half = r // 2, r % 2
        xa = np.zeros((T, DM), dtype=np.float32)
        xa[TOWN:] = x[b, half * TOWN:(half + 1) * TOWN]
        if half == 1:
            xa[:TOWN] = x[b, 0:TOWN]
        cosf, sinsg = make_rope(half)
        w1r = np.zeros((HIDR, DM), dtype=np.float32)
        w2r = np.zeros((HIDR, DM), dtype=np.float32)
        lo = r * HIDR
        hi = min((r + 1) * HIDR, HID)
        if hi > lo:
            w1r[0:hi - lo] = w1[lo:hi]
            w2r[0:hi - lo] = w2[lo:hi]
        w3r = np.zeros((W3R, HID_PAD), dtype=np.float32)
        w3r[:, 0:HID] = w3[r * W3R:(r + 1) * W3R, :]
        in_maps.append(dict(
            x_all=xa, wqT=wq, wkT=wk, wvT=wv, wgT=wg, winT=win, woT=wo,
            cosf=cosf, sinsg=sinsg, dtb=dtbv, alog=alogv, gnwb=gnw, cb=cbv,
            onehot=oh, id32=id32, iden=idenb, idenf=idenf, maskA=mA, maskB=mB,
            ones32=np.ones((32, 128), dtype=np.float32),
            w1s=w1r, w2s=w2r, w3s=w3r))

    nc = build_program(strict=False)
    res = run_bass_kernel_spmd(nc, in_maps, core_ids=list(range(NRANK)))
    outv = np.zeros((B, 2048, DM), dtype=np.float32)
    for r in range(NRANK):
        b, half = r // 2, r % 2
        outv[b, half * TOWN:(half + 1) * TOWN] = np.asarray(res.results[r]["out"])
    return outv


# version 09 (final)
# speedup vs baseline: 1.1915x; 1.0143x over previous
import os
import numpy as np
from contextlib import ExitStack

os.environ.setdefault("NEURON_SCRATCHPAD_PAGE_SIZE", "2048")

import concourse.bass as bass
import concourse.mybir as mybir
from concourse.bass_utils import run_bass_kernel_spmd

F32 = mybir.dt.float32
BF16 = mybir.dt.bfloat16
AF = mybir.ActivationFunctionType
ALU = mybir.AluOpType
AX = mybir.AxisListType

DM = 4096
D = 128
T = 2048
TOWN = 1024
NCH = 16
NOG = 8
OWN0 = 8
TB = 256
HIDR = 1408
NRANK = 8
HID_PAD = HIDR * NRANK
NSLOT = HID_PAD // 128
HSL = NSLOT // 2
W3R = DM // NRANK
EPS = 1e-5
SCALE_BIG = float(2.0 ** 100)

NF4_CODE = np.array([-1.0, -0.6961928009986877, -0.5250730514526367,
                     -0.39491748809814453, -0.28444138169288635, -0.18477343022823334,
                     -0.09105003625154495, 0.0, 0.07958029955625534, 0.16093020141124725,
                     0.24611230194568634, 0.33791524171829224, 0.44070982933044434,
                     0.5626170039176941, 0.7229568362236023, 1.0], dtype=np.float32)
NF4_BOUNDS = ((NF4_CODE[:-1] + NF4_CODE[1:]) * np.float32(0.5)).astype(np.float32)


def _nf4_deltas():
    deltas = np.zeros(15, dtype=np.float32)
    s = np.float32(NF4_CODE[0])
    for i in range(15):
        dd = np.float32(NF4_CODE[i + 1] - s)
        deltas[i] = dd
        s = np.float32(s + dd)
        assert s == NF4_CODE[i + 1]
    return deltas


NF4_DELTAS = _nf4_deltas()
CB_SIG0 = 0
CB_ONE = 15
CB_EPS = 16


class Sem:
    def __init__(self, stack, nc, name):
        self.h = stack.enter_context(nc.semaphore(name))
        self.n = 0

    def inc(self, instr, k=1):
        instr.then_inc(self.h, k)
        self.n += k


def build_program(nh=32, nkvh=8, do_ffn=True, do_ag=True, dbg=False, strict=False):
    gq = nh // nkvh
    nc = bass.Bass("TRN2", target_bir_lowering=False)
    dt_ = nc.dram_tensor

    x_all = dt_("x_all", [T, DM], F32, kind="ExternalInput").ap()
    wqT = dt_("wqT", [DM, nh * D], F32, kind="ExternalInput").ap()
    wkT = dt_("wkT", [DM, nkvh * D], F32, kind="ExternalInput").ap()
    wvT = dt_("wvT", [DM, nkvh * D], F32, kind="ExternalInput").ap()
    wgT = dt_("wgT", [DM, nh * D], F32, kind="ExternalInput").ap()
    winT = dt_("winT", [DM, 32], F32, kind="ExternalInput").ap()
    woT = dt_("woT", [nh * D, DM], F32, kind="ExternalInput").ap()
    cosf = dt_("cosf", [128, T], F32, kind="ExternalInput").ap()
    sinsg = dt_("sinsg", [128, T], F32, kind="ExternalInput").ap()
    dtb = dt_("dtb", [32, 1], F32, kind="ExternalInput").ap()
    alog = dt_("alog", [32, 1], F32, kind="ExternalInput").ap()
    gnwb = dt_("gnwb", [128, 128], F32, kind="ExternalInput").ap()
    cb = dt_("cb", [128, 32], F32, kind="ExternalInput").ap()
    onehot = dt_("onehot", [32, 32 * 128], F32, kind="ExternalInput").ap()
    id32 = dt_("id32", [32, 32], F32, kind="ExternalInput").ap()
    iden = dt_("iden", [128, 128], BF16, kind="ExternalInput").ap()
    idenf = dt_("idenf", [128, 128], F32, kind="ExternalInput").ap()
    maskA = dt_("maskA", [128, 128], BF16, kind="ExternalInput").ap()
    maskB = dt_("maskB", [128, 128], BF16, kind="ExternalInput").ap()
    ones32 = dt_("ones32", [32, 128], F32, kind="ExternalInput").ap()
    w1s = dt_("w1s", [HIDR, DM], F32, kind="ExternalInput").ap()
    w2s = dt_("w2s", [HIDR, DM], F32, kind="ExternalInput").ap()
    w3s = dt_("w3s", [W3R, HID_PAD], F32, kind="ExternalInput").ap()
    out = dt_("out", [TOWN, DM], F32, kind="ExternalOutput").ap()
    if dbg:
        o_x2 = dt_("o_x2", [TOWN, DM], F32, kind="ExternalOutput").ap()
        o_dn = dt_("o_dn", [TOWN, DM], F32, kind="ExternalOutput").ap()
        o_ag = dt_("o_ag", [NRANK * 128, 128], BF16, kind="ExternalOutput").ap()
        o_h = dt_("o_h", [128, 512], BF16, kind="ExternalOutput").ap()

    NT1 = (HIDR // 128) * (DM // 128)
    NT3 = NSLOT * (W3R // 128)
    NTS = 2 * NT1 + NT3
    ag_in = dt_("ag_in", [NTS * 128, 128], BF16)
    ag_out1 = dt_("ag_out1", [NRANK * NT1 * 128, 128], BF16)
    ag_out2 = dt_("ag_out2", [NRANK * NT1 * 128, 128], BF16)
    ag_out3 = dt_("ag_out3", [NRANK * NT3 * 128, 128], BF16)
    xbf = dt_("xbf", [T, DM], BF16)
    kv_sp = dt_("kv_sp", [2 * nkvh * T, 128], F32)
    y_sp = dt_("y_sp", [nh * TOWN, 128], BF16)
    attn_tm = dt_("attn_tm", [TOWN, DM], F32)
    x2sp = dt_("x2sp", [TOWN, DM], F32)
    x2bf = dt_("x2bf", [TOWN, DM], BF16)
    dn_part = dt_("dn_part", [DM, TOWN], F32)
    dn_tm = dt_("dn_tm", [TOWN, DM], F32)
    glb = dt_("glb", [32, NCH], F32)

    def ag1_off(wi, m, c):
        return (wi * NT1 + m * 32 + c) * 128

    def ag3_off(s, cl):
        return (2 * NT1 + s * 4 + cl) * 128

    def agout_off(rank, local):
        return rank * NTS * 128 + local

    est = ExitStack()
    with est:
        e = est.enter_context
        sCL = Sem(est, nc, "sCL")
        sXC = Sem(est, nc, "sXC")
        sXL = Sem(est, nc, "sXL")
        sWL = Sem(est, nc, "sWL")
        sPE = Sem(est, nc, "sPE")
        sDV = Sem(est, nc, "sDV")
        sAC = Sem(est, nc, "sAC")
        sDQL = Sem(est, nc, "sDQL")
        sDQV = Sem(est, nc, "sDQV")
        sDQA = Sem(est, nc, "sDQA")
        sDQS = Sem(est, nc, "sDQS")
        sAG = Sem(est, nc, "sAG")
        sAGc = [Sem(est, nc, f"sAG{i}") for i in range(3)]
        sSP = Sem(est, nc, "sSP")
        sFW = Sem(est, nc, "sFW")
        sF1 = [Sem(est, nc, "sF1a"), Sem(est, nc, "sF1b")]
        sF2 = [Sem(est, nc, "sF2a"), Sem(est, nc, "sF2b")]
        sW3p = [Sem(est, nc, "sW3a"), Sem(est, nc, "sW3b")]
        sST = Sem(est, nc, "sST")
        sOUT = Sem(est, nc, "sOUT")

        block = e(nc.Block())

        @block.gpsimd
        def _(_g):
            V, SC, TE, SY, GP = nc.vector, nc.scalar, nc.tensor, nc.sync, nc.gpsimd
            with ExitStack() as top:
                et = top.enter_context
                c_cb = et(nc.sbuf_tensor("c_cb", [128, 32], F32))
                c_iden = et(nc.sbuf_tensor("c_iden", [128, 128], BF16))
                c_idenf = et(nc.sbuf_tensor("c_idenf", [128, 128], F32))
                x_fm = et(nc.sbuf_tensor("x_fm", [128, 32 * TOWN], BF16))
                pA = et(nc.psum_tensor("pA", [128, 512], F32))
                pB = et(nc.psum_tensor("pB", [128, 512], F32))
                pC = et(nc.psum_tensor("pC", [128, 512], F32))
                pD = et(nc.psum_tensor("pD", [128, 512], F32))
                pE = et(nc.psum_tensor("pE", [128, 512], F32))
                pF = et(nc.psum_tensor("pF", [128, 512], F32))
                pU = [et(nc.psum_tensor(f"pU{i}", [128, 512], F32)) for i in range(2)]

                for dst, src in [(c_cb, cb), (c_iden, iden), (c_idenf, idenf)]:
                    sCL.inc(SY.dma_start(out=dst[:], in_=src), 16)
                for i in range(2):
                    sXC.inc(GP.dma_start(out=xbf.ap()[i * TOWN:(i + 1) * TOWN, :],
                                         in_=x_all[i * TOWN:(i + 1) * TOWN, :],
                                         max_dma_last_dim=8192), 16)

                def wait_engines(eng, sems):
                    for s in sems:
                        eng.wait_ge(s.h, s.n)

                def mkinc(sem, eng):
                    def f(instr, k=1):
                        sem.inc(instr, k)
                        if strict:
                            eng.wait_ge(sem.h, sem.n)
                    return f
                def barrier_all(sems):
                    for eng in (V, SC, TE, SY, GP):
                        for s in sems:
                            eng.wait_ge(s.h, s.n)

                dv = mkinc(sDV, nc.vector)
                ac = mkinc(sAC, nc.scalar)
                pe = mkinc(sPE, nc.tensor)
                dqv = mkinc(sDQV, nc.vector)
                dqa = mkinc(sDQA, nc.scalar)

                def load_w_slice(dstbuf, wap, col0, nchunks):
                    GP.wait_ge(sPE.h, sPE.n)
                    sWL.inc(GP.dma_start(
                        out=dstbuf.ap().rearrange("p (c k) -> p c k", k=128)[:, 0:nchunks, :],
                        in_=wap[:, col0:col0 + 128].rearrange("(c p) k -> p c k", p=128)), 16)

                with ExitStack() as attn:
                    ea = attn.enter_context
                    c_cos = ea(nc.sbuf_tensor("c_cos", [128, T], F32))
                    c_sin = ea(nc.sbuf_tensor("c_sin", [128, T], F32))
                    c_oh = ea(nc.sbuf_tensor("c_oh", [32, 32 * 128], F32))
                    c_id32 = ea(nc.sbuf_tensor("c_id32", [32, 32], F32))
                    c_mA = ea(nc.sbuf_tensor("c_mA", [128, 128], BF16))
                    c_mB = ea(nc.sbuf_tensor("c_mB", [128, 128], BF16))
                    c_gnw = ea(nc.sbuf_tensor("c_gnw", [128, 128], F32))
                    c_ones32 = ea(nc.sbuf_tensor("c_ones32", [32, 128], F32))
                    c_dtb = ea(nc.sbuf_tensor("c_dtb", [32, 1], F32))
                    c_alog = ea(nc.sbuf_tensor("c_alog", [32, 1], F32))
                    c_expA = ea(nc.sbuf_tensor("c_expA", [32, 1], F32))
                    c_win = ea(nc.sbuf_tensor("c_win", [128, 32 * 32], BF16))
                    gt = ea(nc.sbuf_tensor("gt", [32, T], F32))
                    ngt = ea(nc.sbuf_tensor("ngt", [32, T], F32))
                    zrow = ea(nc.sbuf_tensor("zrow", [32, 128], F32))
                    cd_t = ea(nc.sbuf_tensor("cd_t", [128, 32 * NCH], F32))
                    dttm = ea(nc.sbuf_tensor("dttm", [128, 32 * NCH], F32))
                    expgtm = ea(nc.sbuf_tensor("expgtm", [128, 32 * NCH], F32))
                    glrow = ea(nc.sbuf_tensor("glrow", [32, NCH], F32))
                    DQF = 512
                    dq_w = ea(nc.sbuf_tensor("dq_w", [128, DQF], F32))
                    dq_nm = ea(nc.sbuf_tensor("dq_nm", [128, DQF], F32))
                    dq_pl = [ea(nc.sbuf_tensor(f"dq_pl{i}", [128, DQF], BF16)) for i in range(2)]
                    dq_am = ea(nc.sbuf_tensor("dq_am", [128, DQF // 64], F32))
                    dq_amc = ea(nc.sbuf_tensor("dq_amc", [128, DQF // 64], F32))
                    dq_ram = ea(nc.sbuf_tensor("dq_ram", [128, DQF // 64], F32))
                    dq_o = ea(nc.sbuf_tensor("dq_o", [128, DQF], BF16))
                    dq_tr = ea(nc.sbuf_tensor("dq_tr", [128, DQF], BF16))

                    for dst, src in [(c_cos, cosf), (c_sin, sinsg), (c_oh, onehot),
                                     (c_id32, id32), (c_mA, maskA), (c_mB, maskB),
                                     (c_gnw, gnwb), (c_dtb, dtb), (c_alog, alog),
                                     (c_ones32, ones32)]:
                        sCL.inc(SY.dma_start(out=dst[:], in_=src), 16)
                    sWL.inc(GP.dma_start(
                        out=c_win.ap().rearrange("p (c k) -> p c k", k=32),
                        in_=winT.rearrange("(c p) k -> p c k", p=128)), 16)
                    wait_engines(SC, [sCL])
                    ac(SC.activation(out=c_expA[:], in_=c_alog[:], func=AF.Exp))
                    wait_engines(V, [sCL])
                    dv(V.memset(zrow[:], 0.0))

                    dq_list = []
                    for wi, wap in ((0, w1s), (1, w2s)):
                        for rb in range(HIDR // 128):
                            for cq in range(DM // DQF):
                                src = wap[rb * 128:(rb + 1) * 128, cq * DQF:(cq + 1) * DQF]
                                dests = [ag1_off(wi, rb, cq * (DQF // 128) + j) for j in range(DQF // 128)]
                                dq_list.append((src, dests))
                    for rb in range(W3R // 128):
                        for cq in range(HID_PAD // DQF):
                            src = w3s[rb * 128:(rb + 1) * 128, cq * DQF:(cq + 1) * DQF]
                            dests = [ag3_off(cq * (DQF // 128) + j, rb) for j in range(DQF // 128)]
                            dq_list.append((src, dests))
                    NDQ = len(dq_list)
                    dq_i = [0]
                    AGB = [NDQ // 3, 2 * NDQ // 3, NDQ]
                    ag_specs = [(0, NT1 * 128, ag_out1), (NT1 * 128, NT1 * 128, ag_out2),
                                (2 * NT1 * 128, NT3 * 128, ag_out3)]

                    def maybe_ag():
                        for ci, bnd in enumerate(AGB):
                            if dq_i[0] == bnd and sAGc[ci].n == 0 and do_ag:
                                off, sz, dst = ag_specs[ci]
                                GP.wait_ge(sDQS.h, sDQS.n)
                                cc = GP.collective_compute(
                                    "AllGather", ALU.bypass,
                                    replica_groups=[list(range(NRANK))],
                                    ins=[ag_in[off:off + sz, :].opt()],
                                    outs=[dst.ap().opt()])
                                sAGc[ci].inc(cc, 1)

                    def dq_pump(k):
                        for _ in range(k):
                            ti = dq_i[0]
                            if ti >= NDQ:
                                return
                            dq_i[0] += 1
                            src, dests = dq_list[ti]
                            SY.wait_ge(sDQV.h, sDQV.n)
                            sDQL.inc(SY.dma_start(out=dq_w[:], in_=src), 16)
                            V.wait_ge(sDQL.h, sDQL.n)
                            V.wait_ge(sDQS.h, sDQS.n)
                            w3v = dq_w.ap().rearrange("p (b k) -> p b k", k=64)
                            dqv(V.tensor_reduce(out=dq_am[:], in_=w3v, axis=AX.X,
                                                     op=ALU.max, apply_absolute_value=True))
                            V.wait_ge(sDQV.h, sDQV.n)
                            dqv(V.tensor_scalar(out=dq_amc[:], in0=dq_am[:],
                                                     scalar1=1e-30, scalar2=None, op0=ALU.add))
                            V.wait_ge(sDQV.h, sDQV.n)
                            dqv(V.reciprocal(out=dq_ram[:], in_=dq_amc[:]))
                            V.wait_ge(sDQV.h, sDQV.n)
                            dqv(V.tensor_tensor(
                                out=dq_nm.ap().rearrange("p (b k) -> p b k", k=64),
                                in0=w3v,
                                in1=dq_ram.ap().unsqueeze(2).broadcast_to([128, DQF // 64, 64]),
                                op=ALU.mult))
                            norm_done = sDQV.n
                            chain0 = norm_done
                            plane0 = sDQA.n
                            SC.wait_ge(sDQV.h, norm_done)
                            for i in range(15):
                                if i >= 2:
                                    SC.wait_ge(sDQV.h, chain0 + (i - 1) * (DQF // 512))
                                dqa(SC.activation(out=dq_pl[i % 2][:], in_=dq_nm[:],
                                                       func=AF.Sigmoid,
                                                       bias=c_cb[:, CB_SIG0 + i:CB_SIG0 + i + 1],
                                                       scale=SCALE_BIG))
                            for i in range(15):
                                V.wait_ge(sDQA.h, plane0 + i + 1)
                                for cs in range(DQF // 512):
                                    sl = slice(cs * 512, (cs + 1) * 512)
                                    if i == 0:
                                        r = V.tensor_scalar(out=pU[cs][:], in0=dq_pl[0][:, sl],
                                                            scalar1=float(NF4_DELTAS[0]),
                                                            scalar2=float(NF4_CODE[0]),
                                                            op0=ALU.mult, op1=ALU.add)
                                    else:
                                        r = V.scalar_tensor_tensor(out=pU[cs][:],
                                                                   in0=dq_pl[i % 2][:, sl],
                                                                   scalar=float(NF4_DELTAS[i]),
                                                                   in1=pU[cs][:],
                                                                   op0=ALU.mult, op1=ALU.add)
                                    dqv(r)
                            V.wait_ge(sDQV.h, sDQV.n)
                            for cs in range(DQF // 512):
                                sl = slice(cs * 512, (cs + 1) * 512)
                                r = V.tensor_tensor(
                                    out=dq_o[:, sl].rearrange("p (b k) -> p b k", k=64),
                                    in0=pU[cs].ap().rearrange("p (b k) -> p b k", k=64),
                                    in1=dq_am[:, cs * 8:(cs + 1) * 8].unsqueeze(2)
                                          .broadcast_to([128, 8, 64]),
                                    op=ALU.mult)
                                dqv(r)
                            wait_engines(TE, [sAC, sDV])
                            TE.wait_ge(sDQV.h, sDQV.n)
                            for j in range(DQF // 128):
                                tr = TE.transpose(
                                    pF.ap().bitcast(BF16)[:, j * 128:(j + 1) * 128],
                                    dq_o[:, j * 128:(j + 1) * 128], c_iden[:])
                            pe(tr)
                            SC.wait_ge(sPE.h, sPE.n)
                            SC.wait_ge(sDQS.h, sDQS.n)
                            ac(SC.copy(out=dq_tr[:], in_=pF.ap().bitcast(BF16)[:, 0:DQF]))
                            SY.wait_ge(sAC.h, sAC.n)
                            runs = []
                            for dof in dests:
                                if runs and dof == runs[-1][1] + 128:
                                    runs[-1][1] = dof
                                else:
                                    runs.append([dof, dof])
                            off_j = 0
                            for lo, hi in runs:
                                npc = (hi - lo) // 128 + 1
                                sDQS.inc(SY.dma_start(
                                    out=ag_in[lo:lo + npc * 128, :]
                                    .rearrange("(j p) k -> p j k", p=128),
                                    in_=dq_tr[:, off_j * 128:(off_j + npc) * 128]
                                    .rearrange("p (j k) -> p j k", k=128)), 16)
                                off_j += npc
                            maybe_ag()

                    with ExitStack() as p1:
                        e1 = p1.enter_context
                        dtT = e1(nc.sbuf_tensor("dtT", [32, T], F32))
                        gsrc = e1(nc.sbuf_tensor("gsrc", [32, T], F32))
                        xfs = e1(nc.sbuf_tensor("xfs", [128, 32 * TB], BF16))
                        wb_k = e1(nc.sbuf_tensor("wb_k", [128, 32 * 128], BF16))
                        wb_v = e1(nc.sbuf_tensor("wb_v", [128, 32 * 128], BF16))
                        rp_a = e1(nc.sbuf_tensor("rp_a", [128, TB], F32))
                        rp_r = e1(nc.sbuf_tensor("rp_r", [128, TB], F32))
                        rp_b = e1(nc.sbuf_tensor("rp_b", [128, TB], F32))
                        khat = e1(nc.sbuf_tensor("khat", [128, TB], F32))
                        sp_st = e1(nc.sbuf_tensor("sp_st", [128, TB], F32))
                        dz = e1(nc.sbuf_tensor("dz", [32, TB], F32))
                        da1 = e1(nc.sbuf_tensor("da1", [32, TB], F32))
                        da2 = e1(nc.sbuf_tensor("da2", [32, TB], F32))

                        SY.wait_ge(sXC.h, sXC.n)
                        for c in range(32):
                            sXL.inc(SY.dma_start(
                                out=x_fm[:, c * TOWN:(c + 1) * TOWN],
                                in_=xbf.ap()[TOWN:2 * TOWN, c * 128:(c + 1) * 128],
                                transpose=True), 16)

                        for half in range(2):
                            for tb in range(TOWN // TB):
                                tg = half * (TOWN // TB) + tb
                                t0 = tg * TB
                                if half == 0:
                                    SY.wait_ge(sPE.h, sPE.n)
                                    for c in range(32):
                                        sXL.inc(SY.dma_start(
                                            out=xfs[:, c * TB:(c + 1) * TB],
                                            in_=xbf.ap()[t0:t0 + TB, c * 128:(c + 1) * 128],
                                            transpose=True), 16)

                                    def xr(c):
                                        return xfs[:, c * TB:(c + 1) * TB]
                                else:
                                    def xr(c):
                                        return x_fm[:, c * TOWN + tb * TB:
                                                    c * TOWN + tb * TB + TB]
                                tsl = slice(t0, t0 + TB)
                                wait_engines(TE, [sXL, sCL, sWL, sAC, sDV])
                                for c in range(32):
                                    mm = TE.matmul(pD[0:32, 0:TB], c_win[:, c * 32:(c + 1) * 32],
                                                   xr(c), start=(c == 0), stop=(c == 31))
                                pe(mm)
                                SC.wait_ge(sPE.h, sPE.n)
                                ac(SC.activation(out=dz[:], in_=pD[0:32, 0:TB],
                                                      func=AF.Identity, bias=c_dtb[:]))
                                SC.wait_ge(sAC.h, sAC.n)
                                ac(SC.activation(out=da1[:], in_=dz[:], func=AF.Abs))
                                SC.wait_ge(sAC.h, sAC.n)
                                ac(SC.activation(out=da1[:], in_=da1[:], func=AF.Exp,
                                                      scale=-1.0))
                                SC.wait_ge(sAC.h, sAC.n)
                                ac(SC.activation(out=da1[:], in_=da1[:], func=AF.Ln,
                                                      bias=c_cb[0:32, CB_ONE:CB_ONE + 1]))
                                wait_engines(V, [sAC])
                                dv(V.tensor_scalar(out=da2[:], in0=dz[:], scalar1=0.0,
                                                        scalar2=None, op0=ALU.max))
                                V.wait_ge(sDV.h, sDV.n)
                                dv(V.tensor_tensor(out=dtT[:, tsl], in0=da2[:],
                                                        in1=da1[:], op=ALU.add))
                                for kv in range(nkvh):
                                    for which in ("k", "v"):
                                        wb = wb_k if which == "k" else wb_v
                                        load_w_slice(wb, wkT if which == "k" else wvT,
                                                     kv * 128, 32)
                                        wait_engines(TE, [sWL, sAC, sDV])
                                        for c in range(32):
                                            mm = TE.matmul(pA[:, 0:TB],
                                                           wb[:, c * 128:(c + 1) * 128],
                                                           xr(c), start=(c == 0), stop=(c == 31))
                                        pe(mm)
                                        V.wait_ge(sPE.h, sPE.n)
                                        if which == "k":
                                            dv(V.tensor_tensor(out=rp_a[:], in0=pA[:, 0:TB],
                                                                    in1=c_cos[:, tsl], op=ALU.mult))
                                            dv(V.tensor_copy(out=rp_r[0:64, :],
                                                                  in_=pA[64:128, 0:TB]))
                                            dv(V.tensor_copy(out=rp_r[64:128, :],
                                                                  in_=pA[0:64, 0:TB]))
                                            V.wait_ge(sDV.h, sDV.n)
                                            dv(V.tensor_tensor(out=rp_b[:], in0=rp_r[:],
                                                                    in1=c_sin[:, tsl], op=ALU.mult))
                                            V.wait_ge(sDV.h, sDV.n)
                                            dv(V.tensor_tensor(out=khat[:], in0=rp_a[:],
                                                                    in1=rp_b[:], op=ALU.add))
                                            row0 = kv * T + t0
                                        else:
                                            dv(V.tensor_copy(out=khat[:], in_=pA[:, 0:TB]))
                                            row0 = nkvh * T + kv * T + t0
                                        wait_engines(TE, [sDV, sAC])
                                        for j in range(TB // 128):
                                            tr = TE.transpose(
                                                pB[:, j * 128:(j + 1) * 128],
                                                khat[:, j * 128:(j + 1) * 128], c_idenf[:])
                                        pe(tr)
                                        SC.wait_ge(sPE.h, sPE.n)
                                        SC.wait_ge(sSP.h, sSP.n)
                                        ac(SC.copy(out=sp_st[:], in_=pB[:, 0:TB]))
                                        SY.wait_ge(sAC.h, sAC.n)
                                        sSP.inc(SY.dma_start(
                                            out=kv_sp[row0:row0 + TB, :]
                                            .rearrange("(j p) k -> p j k", p=128),
                                            in_=sp_st[:].rearrange("p (j k) -> p j k", k=128)), 16)
                                    dq_pump(2)

                        wait_engines(V, [sDV, sAC])
                        dv(V.tensor_scalar(out=gsrc[:], in0=dtT[:], scalar1=c_expA[:],
                                                scalar2=-1.0, op0=ALU.mult, op1=ALU.mult))
                        V.wait_ge(sDV.h, sDV.n)
                        for n in range(NCH):
                            csl = slice(n * 128, (n + 1) * 128)
                            dv(V.tensor_tensor_scan(out=gt[:, csl], data0=gsrc[:, csl],
                                                         data1=zrow[:], initial=0.0,
                                                         op0=ALU.add, op1=ALU.add))
                        V.wait_ge(sDV.h, sDV.n)
                        dv(V.tensor_scalar(out=ngt[:], in0=gt[:], scalar1=-1.0,
                                                scalar2=None, op0=ALU.mult))
                        dv(V.tensor_copy(
                            out=glrow[:],
                            in_=gt.ap().rearrange("p (n k) -> p n k", k=128)[:, :, 127]))
                        V.wait_ge(sDV.h, sDV.n)
                        SY.wait_ge(sDV.h, sDV.n)
                        sST.inc(SY.dma_start(out=glb.ap(), in_=glrow[:]), 16)
                        SY.wait_ge(sST.h, sST.n)
                        sST.inc(SY.dma_start(
                            out=cd_t[:],
                            in_=glb.ap().rearrange("h n -> (h n)").unsqueeze(0)
                            .broadcast_to([128, 32 * NCH])), 16)
                        SC.wait_ge(sST.h, sST.n)
                        ac(SC.activation(out=cd_t[:], in_=cd_t[:], func=AF.Exp))
                        for n in range(NCH):
                            csl = slice(n * 128, (n + 1) * 128)
                            wait_engines(TE, [sDV, sAC])
                            TE.matmul(pD[:, 0:32], dtT[:, csl], c_id32[:], start=True, stop=True)
                            mm = TE.matmul(pD[:, 64:96], gt[:, csl], c_id32[:], start=True, stop=True)
                            pe(mm)
                            SC.wait_ge(sPE.h, sPE.n)
                            ac(SC.copy(out=dttm[:, n * 32:(n + 1) * 32], in_=pD[:, 0:32]))
                            ac(SC.activation(out=expgtm[:, n * 32:(n + 1) * 32],
                                                  in_=pD[:, 64:96], func=AF.Exp))
                        dq_pump(4)


                    barrier_all([sSP, sST, sDQS, sXL, sDV, sAC, sPE])
                    with ExitStack() as p2:
                        e2 = p2.enter_context
                        q_fm = e2(nc.sbuf_tensor("q_fm", [128, gq * TOWN], BF16))
                        sgate = e2(nc.sbuf_tensor("sgate", [128, NOG * gq * 128], BF16))
                        kT_sb = e2(nc.sbuf_tensor("kT_sb", [128, NCH * 128], F32))
                        vT_sb = e2(nc.sbuf_tensor("vT_sb", [128, NCH * 128], F32))
                        kf_sb = e2(nc.sbuf_tensor("kf_sb", [128, 128], BF16))
                        S_st = [e2(nc.sbuf_tensor(f"S_st{i}", [128, gq * 128], F32))
                                for i in range(2)]
                        S_bf4 = e2(nc.sbuf_tensor("S_bf4", [128, gq * 128], BF16))
                        e_sb4 = e2(nc.sbuf_tensor("e_sb4", [128, gq * 128], F32))
                        edec4 = e2(nc.sbuf_tensor("edec4", [128, gq], F32))
                        oh4 = e2(nc.sbuf_tensor("oh4", [32, gq], F32))
                        xc_4 = e2(nc.sbuf_tensor("xc_4", [128, gq * 128], BF16))
                        kd_4 = e2(nc.sbuf_tensor("kd_4", [128, gq * 128], BF16))
                        s2_4 = e2(nc.sbuf_tensor("s2_4", [128, gq * 128], BF16))
                        ytmp4 = e2(nc.sbuf_tensor("ytmp4", [128, gq * 128], F32))
                        y4 = e2(nc.sbuf_tensor("y4", [128, gq * 128], F32))
                        yg4 = e2(nc.sbuf_tensor("yg4", [128, gq * 128], BF16))
                        sil4 = e2(nc.sbuf_tensor("sil4", [128, gq * 128], F32))
                        ss4 = e2(nc.sbuf_tensor("ss4", [128, 4], F32))
                        rstd4 = e2(nc.sbuf_tensor("rstd4", [128, 4], F32))
                        ydump = e2(nc.sbuf_tensor("ydump", [128, 128], BF16))
                        wb_q = e2(nc.sbuf_tensor("wb_q", [128, 32 * 128], BF16))
                        wb_g = e2(nc.sbuf_tensor("wb_g", [128, 32 * 128], BF16))
                        rp2a = e2(nc.sbuf_tensor("rp2a", [128, 512], F32))
                        rp2r = e2(nc.sbuf_tensor("rp2r", [128, 512], F32))
                        rp2b = e2(nc.sbuf_tensor("rp2b", [128, 512], F32))

                        for kv in range(nkvh):
                            for hh in range(gq):
                                h = kv * gq + hh
                                load_w_slice(wb_q, wqT, h * 128, 32)
                                wait_engines(TE, [sWL, sDV, sAC])
                                for tb2 in range(2):
                                    tsl2 = slice(TOWN + tb2 * 512, TOWN + (tb2 + 1) * 512)
                                    wait_engines(TE, [sDV])
                                    for c in range(32):
                                        mm = TE.matmul(
                                            pA[:], wb_q[:, c * 128:(c + 1) * 128],
                                            x_fm[:, c * TOWN + tb2 * 512:c * TOWN + tb2 * 512 + 512],
                                            start=(c == 0), stop=(c == 31))
                                    pe(mm)
                                    V.wait_ge(sPE.h, sPE.n)
                                    dv(V.tensor_tensor(out=rp2a[:], in0=pA[:],
                                                       in1=c_cos[:, tsl2], op=ALU.mult))
                                    dv(V.tensor_copy(out=rp2r[0:64, :], in_=pA[64:128, :]))
                                    dv(V.tensor_copy(out=rp2r[64:128, :], in_=pA[0:64, :]))
                                    V.wait_ge(sDV.h, sDV.n)
                                    dv(V.tensor_tensor(out=rp2b[:], in0=rp2r[:],
                                                       in1=c_sin[:, tsl2], op=ALU.mult))
                                    V.wait_ge(sDV.h, sDV.n)
                                    dv(V.tensor_tensor(
                                        out=q_fm[:, hh * TOWN + tb2 * 512:hh * TOWN + tb2 * 512 + 512],
                                        in0=rp2a[:], in1=rp2b[:], op=ALU.add))
                                load_w_slice(wb_g, wgT, h * 128, 32)
                                wait_engines(TE, [sWL, sAC])
                                for gb in range(2):
                                    wait_engines(TE, [sAC])
                                    for g4 in range(4):
                                        g = gb * 4 + g4
                                        for c in range(32):
                                            mm = TE.matmul(
                                                pB[:, g4 * 128:(g4 + 1) * 128],
                                                x_fm[:, c * TOWN + g * 128:c * TOWN + (g + 1) * 128],
                                                wb_g[:, c * 128:(c + 1) * 128],
                                                start=(c == 0), stop=(c == 31))
                                    pe(mm)
                                    SC.wait_ge(sPE.h, sPE.n)
                                    SC.wait_ge(sDV.h, sDV.n)
                                    for g4 in range(4):
                                        g = gb * 4 + g4
                                        a2 = SC.copy(
                                            out=sgate[:, (g * gq + hh) * 128:(g * gq + hh + 1) * 128],
                                            in_=pB[:, g4 * 128:(g4 + 1) * 128])
                                    ac(a2)
                                dq_pump(2)
                            dq_pump(0)
                            SY.wait_ge(sSP.h, sSP.n)
                            SY.wait_ge(sDV.h, sDV.n)
                            SY.wait_ge(sPE.h, sPE.n)
                            sXL.inc(SY.dma_start(
                                out=kT_sb.ap().rearrange("p (n k) -> p n k", k=128),
                                in_=kv_sp[kv * T:(kv + 1) * T, :]
                                .rearrange("(n p) k -> p n k", p=128)), 16)
                            sXL.inc(SY.dma_start(
                                out=vT_sb.ap().rearrange("p (n k) -> p n k", k=128),
                                in_=kv_sp[nkvh * T + kv * T:nkvh * T + (kv + 1) * T, :]
                                .rearrange("(n p) k -> p n k", p=128)), 16)
                            wait_engines(V, [sXL, sAC])
                            dv(V.memset(S_st[0][:], 0.0))

                            for n in range(NCH):
                                own = n >= OWN0
                                csl = slice(n * 128, (n + 1) * 128)
                                par = n % 2
                                if own:
                                    wait_engines(TE, [sDV, sAC])
                                    tr = TE.transpose(pF[:, 0:128],
                                                      kT_sb[:, csl], c_idenf[:])
                                    pe(tr)
                                    V.wait_ge(sPE.h, sPE.n)
                                    dv(V.tensor_copy(out=kf_sb[:], in_=pF[:, 0:128]))
                                    wait_engines(TE, [sDV])
                                    for hh in range(gq):
                                        h = kv * gq + hh
                                        ohsl = c_oh[:, h * 128:(h + 1) * 128]
                                        co = slice(hh * 128, (hh + 1) * 128)
                                        TE.matmul(pE[:, co], ohsl, gt[:, csl], start=True, stop=False)
                                        TE.matmul(pE[:, co], ngt[:, csl], ohsl, start=False, stop=False)
                                        TE.matmul(pE[:, co], c_mA[:], c_mB[:], start=False, stop=True)
                                        mm = TE.matmul(pD[:, co], kf_sb[:],
                                                       q_fm[:, hh * TOWN + (n - OWN0) * 128:
                                                            hh * TOWN + (n - OWN0 + 1) * 128],
                                                       start=True, stop=True)
                                    pe(mm)
                                    SC.wait_ge(sPE.h, sPE.n)
                                    SC.wait_ge(sDV.h, sDV.n)
                                    ac(SC.activation(out=e_sb4[:], in_=pE[:, 0:512], func=AF.Exp))
                                    wait_engines(V, [sAC])
                                    dv(V.tensor_tensor(out=s2_4[:], in0=pD[:, 0:512],
                                                       in1=e_sb4[:], op=ALU.mult))
                                    for hh in range(gq):
                                        h = kv * gq + hh
                                        co = slice(hh * 128, (hh + 1) * 128)
                                        dv(V.tensor_scalar(out=xc_4[:, co], in0=vT_sb[:, csl],
                                                           scalar1=dttm[:, n * 32 + h:n * 32 + h + 1],
                                                           scalar2=None, op0=ALU.mult))
                                        dv(V.tensor_scalar(out=kd_4[:, co], in0=kT_sb[:, csl],
                                                           scalar1=e_sb4[:, hh * 128 + 127:hh * 128 + 128],
                                                           scalar2=None, op0=ALU.mult))
                                    dv(V.tensor_copy(out=S_bf4[:], in_=S_st[par][:]))
                                    V.wait_ge(sDV.h, sDV.n)
                                    wait_engines(TE, [sDV, sAC])
                                    for hh in range(gq):
                                        co = slice(hh * 128, (hh + 1) * 128)
                                        TE.matmul(pB[:, co],
                                                  q_fm[:, hh * TOWN + (n - OWN0) * 128:
                                                       hh * TOWN + (n - OWN0 + 1) * 128],
                                                  S_bf4[:, co], start=True, stop=True)
                                        TE.matmul(pA[:, co], s2_4[:, co], xc_4[:, co],
                                                  start=True, stop=True)
                                        mm = TE.matmul(pC[:, co], kd_4[:, co], xc_4[:, co],
                                                       start=True, stop=True)
                                    pe(mm)
                                    SC.wait_ge(sPE.h, sPE.n)
                                    SC.wait_ge(sDV.h, sDV.n)
                                    for hh in range(gq):
                                        h = kv * gq + hh
                                        co = slice(hh * 128, (hh + 1) * 128)
                                        a2 = SC.activation(out=ytmp4[:, co], in_=pB[:, co],
                                                           func=AF.Identity,
                                                           scale=expgtm[:, n * 32 + h:n * 32 + h + 1])
                                    ac(a2)
                                    wait_engines(V, [sAC])
                                    dv(V.tensor_tensor(out=y4[:], in0=pA[:, 0:512],
                                                       in1=ytmp4[:], op=ALU.add))
                                    for hh in range(gq):
                                        h = kv * gq + hh
                                        co = slice(hh * 128, (hh + 1) * 128)
                                        dv(V.scalar_tensor_tensor(
                                            out=S_st[1 - par][:, co], in0=S_st[par][:, co],
                                            scalar=cd_t[:, h * NCH + n:h * NCH + n + 1],
                                            in1=pC[:, co], op0=ALU.mult, op1=ALU.add))
                                    SC.wait_ge(sDV.h, sDV.n)
                                    for hh in range(gq):
                                        co = slice(hh * 128, (hh + 1) * 128)
                                        ac(SC.activation(out=ydump[:], in_=y4[:, co],
                                                         func=AF.Square,
                                                         accum_out=ss4[:, hh:hh + 1]))
                                    ac(SC.activation(out=rstd4[:], in_=ss4[:], func=AF.Ln,
                                                       scale=1.0 / 128.0,
                                                       bias=c_cb[:, CB_EPS:CB_EPS + 1]))
                                    ac(SC.activation(out=rstd4[:], in_=rstd4[:],
                                                     func=AF.Exp, scale=-0.5))
                                    g = n - OWN0
                                    SC.wait_ge(sDV.h, sDV.n)
                                    ac(SC.activation(out=sil4[:],
                                                     in_=sgate[:, g * gq * 128:(g + 1) * gq * 128],
                                                     func=AF.Silu))
                                    wait_engines(V, [sAC])
                                    V.wait_ge(sSP.h, sSP.n)
                                    for hh in range(gq):
                                        h = kv * gq + hh
                                        co = slice(hh * 128, (hh + 1) * 128)
                                        dv(V.scalar_tensor_tensor(
                                            out=ytmp4[:, co], in0=y4[:, co],
                                            scalar=rstd4[:, hh:hh + 1], in1=c_gnw[:],
                                            op0=ALU.mult, op1=ALU.mult))
                                    V.wait_ge(sDV.h, sDV.n)
                                    dv(V.tensor_tensor(out=yg4[:], in0=ytmp4[:],
                                                       in1=sil4[:], op=ALU.mult))
                                    SY.wait_ge(sDV.h, sDV.n)
                                    for hh in range(gq):
                                        h = kv * gq + hh
                                        co = slice(hh * 128, (hh + 1) * 128)
                                        sSP.inc(SY.dma_start(
                                            out=y_sp[h * TOWN + g * 128:h * TOWN + (g + 1) * 128, :],
                                            in_=yg4[:, co]), 16)
                                else:
                                    wait_engines(V, [sAC, sPE])
                                    dv(V.tensor_scalar(out=oh4[:], 
                                                       in0=c_oh.ap().rearrange("p (h k) -> p h k", k=128)[:, kv * gq:(kv + 1) * gq, 0],
                                                       scalar1=glrow[:, n:n + 1], scalar2=None,
                                                       op0=ALU.mult))
                                    V.wait_ge(sDV.h, sDV.n)
                                    wait_engines(TE, [sDV, sAC])
                                    TE.matmul(pE[:, 0:gq], c_ones32[:], oh4[:], start=True, stop=False)
                                    mm = TE.matmul(pE[:, 0:gq], ngt[:, csl],
                                                   c_oh.ap().rearrange("p (h k) -> p h k", k=128)[:, kv * gq:(kv + 1) * gq, 0],
                                                   start=False, stop=True)
                                    pe(mm)
                                    SC.wait_ge(sPE.h, sPE.n)
                                    SC.wait_ge(sDV.h, sDV.n)
                                    ac(SC.activation(out=edec4[:], in_=pE[:, 0:gq], func=AF.Exp))
                                    wait_engines(V, [sAC])
                                    for hh in range(gq):
                                        h = kv * gq + hh
                                        co = slice(hh * 128, (hh + 1) * 128)
                                        dv(V.tensor_scalar(out=xc_4[:, co], in0=vT_sb[:, csl],
                                                           scalar1=dttm[:, n * 32 + h:n * 32 + h + 1],
                                                           scalar2=None, op0=ALU.mult))
                                        dv(V.tensor_scalar(out=kd_4[:, co], in0=kT_sb[:, csl],
                                                           scalar1=edec4[:, hh:hh + 1],
                                                           scalar2=None, op0=ALU.mult))
                                    V.wait_ge(sDV.h, sDV.n)
                                    wait_engines(TE, [sDV])
                                    for hh in range(gq):
                                        co = slice(hh * 128, (hh + 1) * 128)
                                        mm = TE.matmul(pC[:, co], kd_4[:, co], xc_4[:, co],
                                                       start=True, stop=True)
                                    pe(mm)
                                    V.wait_ge(sPE.h, sPE.n)
                                    for hh in range(gq):
                                        h = kv * gq + hh
                                        co = slice(hh * 128, (hh + 1) * 128)
                                        dv(V.scalar_tensor_tensor(
                                            out=S_st[1 - par][:, co], in0=S_st[par][:, co],
                                            scalar=cd_t[:, h * NCH + n:h * NCH + n + 1],
                                            in1=pC[:, co], op0=ALU.mult, op1=ALU.add))
                            dq_pump(6)

                        dq_pump(NDQ)
                        if (not do_ag) and do_ffn:
                            GP.wait_ge(sDQS.h, sDQS.n)
                            for ci, (off, sz, dst) in enumerate(ag_specs):
                                for r in range(NRANK):
                                    d = GP.dma_start(out=dst.ap()[r * sz:(r + 1) * sz, :],
                                                     in_=ag_in[off:off + sz, :])
                                    sAGc[ci].inc(d, 16)

                barrier_all([sSP, sST, sDQS, sXL, sDV, sAC, sPE, sAG])
                with ExitStack() as p3:
                    e3 = p3.enter_context
                    yb = e3(nc.sbuf_tensor("yb", [128, nh * TOWN], BF16))
                    wb_o = e3(nc.sbuf_tensor("wb_o", [128, nh * 128], BF16))
                    wo_st = e3(nc.sbuf_tensor("wo_st", [128, 512], F32))
                    wo_o = e3(nc.sbuf_tensor("wo_o", [128, 512], F32))
                    SY.wait_ge(sSP.h, sSP.n)
                    for hec in range(nh):
                        sXL.inc(SY.dma_start(
                            out=yb[:, hec * TOWN:(hec + 1) * TOWN],
                            in_=y_sp[hec * TOWN:(hec + 1) * TOWN, :],
                            transpose=True), 16)
                    for m in range(32):
                        load_w_slice(wb_o, woT, m * 128, nh)
                        for th in range(2):
                            wait_engines(TE, [sWL, sXL, sDV])
                            for hec in range(nh):
                                mm = TE.matmul(
                                    pA[:], wb_o[:, hec * 128:(hec + 1) * 128],
                                    yb[:, hec * TOWN + th * 512:hec * TOWN + th * 512 + 512],
                                    start=(hec == 0), stop=(hec == nh - 1))
                            pe(mm)
                            V.wait_ge(sPE.h, sPE.n)
                            dv(V.tensor_copy(out=wo_st[:], in_=pA[:]))
                            wait_engines(TE, [sDV, sAC])
                            for j in range(4):
                                tr = TE.transpose(pF[:, j * 128:(j + 1) * 128],
                                                  wo_st[:, j * 128:(j + 1) * 128],
                                                  c_idenf[:])
                            pe(tr)
                            V.wait_ge(sPE.h, sPE.n)
                            V.wait_ge(sST.h, sST.n)
                            dv(V.tensor_copy(out=wo_o[:], in_=pF[:]))
                            SY.wait_ge(sDV.h, sDV.n)
                            sST.inc(SY.dma_start(
                                out=attn_tm.ap()[th * 512:(th + 1) * 512,
                                                 m * 128:(m + 1) * 128]
                                .rearrange("(j p) k -> p j k", p=128),
                                in_=wo_o[:].rearrange("p (j k) -> p j k", k=128)), 16)

                barrier_all([sSP, sST, sXL, sDV, sAC, sPE])
                with ExitStack() as p4:
                    e4 = p4.enter_context
                    ln_x = e4(nc.sbuf_tensor("ln_x", [128, DM], F32))
                    ln_a = e4(nc.sbuf_tensor("ln_a", [128, DM], F32))
                    ln_b = e4(nc.sbuf_tensor("ln_b", [128, DM], BF16))
                    ln_s = e4(nc.sbuf_tensor("ln_s", [128, 8], F32))

                    def ln_pass(res_ap, add_ap, spill_f32, spill_bf):
                        SY.wait_ge(sST.h, sST.n)
                        SY.wait_ge(sOUT.h, sOUT.n)
                        sXL.inc(SY.dma_start(out=ln_x[:], in_=res_ap), 16)
                        sXL.inc(SY.dma_start(out=ln_a[:], in_=add_ap), 16)
                        wait_engines(V, [sXL, sAC])
                        dv(V.tensor_tensor(out=ln_x[:], in0=ln_x[:], in1=ln_a[:],
                                                op=ALU.add))
                        SC.wait_ge(sDV.h, sDV.n)
                        ac(SC.activation(out=ln_b[:], in_=ln_x[:], func=AF.Square,
                                              accum_out=ln_s[:, 0:1]))
                        SC.wait_ge(sAC.h, sAC.n)
                        ac(SC.activation(out=ln_b[:], in_=ln_x[:], func=AF.Identity,
                                              accum_out=ln_s[:, 1:2]))
                        ac(SC.mul(out=ln_s[:, 2:3], in_=ln_s[:, 1:2], mul=1.0 / DM))
                        ac(SC.mul(out=ln_s[:, 3:4], in_=ln_s[:, 0:1], mul=1.0 / DM))
                        wait_engines(V, [sAC])
                        dv(V.tensor_scalar(out=ln_s[:, 4:5], in0=ln_s[:, 2:3],
                                                scalar1=ln_s[:, 2:3], scalar2=None,
                                                op0=ALU.mult))
                        V.wait_ge(sDV.h, sDV.n)
                        dv(V.tensor_tensor(out=ln_s[:, 5:6], in0=ln_s[:, 3:4],
                                                in1=ln_s[:, 4:5], op=ALU.subtract))
                        SC.wait_ge(sDV.h, sDV.n)
                        ac(SC.activation(out=ln_s[:, 6:7], in_=ln_s[:, 5:6], func=AF.Ln,
                                              bias=c_cb[:, CB_EPS:CB_EPS + 1]))
                        SC.wait_ge(sAC.h, sAC.n)
                        ac(SC.activation(out=ln_s[:, 7:8], in_=ln_s[:, 6:7], func=AF.Exp,
                                              scale=-0.5))
                        wait_engines(V, [sAC])
                        dv(V.tensor_scalar(out=ln_x[:], in0=ln_x[:],
                                                scalar1=ln_s[:, 2:3], scalar2=ln_s[:, 7:8],
                                                op0=ALU.subtract, op1=ALU.mult))
                        SY.wait_ge(sDV.h, sDV.n)
                        if spill_f32 is not None:
                            sST.inc(SY.dma_start(out=spill_f32, in_=ln_x[:]), 16)
                        if spill_bf is not None:
                            dv(V.tensor_copy(out=ln_b[:], in_=ln_x[:]))
                            SY.wait_ge(sDV.h, sDV.n)
                            sST.inc(SY.dma_start(out=spill_bf, in_=ln_b[:]), 16)

                    SY.wait_ge(sST.h, sST.n)
                    for g in range(NOG):
                        ln_pass(x_all[TOWN + g * 128:TOWN + (g + 1) * 128, :],
                                attn_tm.ap()[g * 128:(g + 1) * 128, :],
                                x2sp.ap()[g * 128:(g + 1) * 128, :],
                                x2bf.ap()[g * 128:(g + 1) * 128, :] if do_ffn else None)

                    if not do_ffn:
                        for g in range(NOG):
                            SY.wait_ge(sST.h, sST.n)
                            SY.wait_ge(sOUT.h, sOUT.n)
                            sXL.inc(SY.dma_start(out=ln_x[:],
                                                 in_=x2sp.ap()[g * 128:(g + 1) * 128, :]), 16)
                            SY.wait_ge(sXL.h, sXL.n)
                            sOUT.inc(SY.dma_start(out=out[g * 128:(g + 1) * 128, :],
                                                  in_=ln_x[:]), 16)
                        SY.wait_ge(sOUT.h, sOUT.n)
                        return

                    SY.wait_ge(sST.h, sST.n)
                    SY.wait_ge(sPE.h, sPE.n)
                    for c in range(32):
                        sXL.inc(SY.dma_start(out=x_fm[:, c * TOWN:(c + 1) * TOWN],
                                             in_=x2bf.ap()[:, c * 128:(c + 1) * 128],
                                             transpose=True), 16)

                    barrier_all([sST, sXL, sDV, sAC, sPE])
                    with ExitStack() as p5:
                        e5 = p5.enter_context
                        HS2 = NSLOT // 4
                        h_fm = e5(nc.sbuf_tensor("h_fm", [128, HS2 * 512], BF16))
                        fw = [e5(nc.sbuf_tensor(f"fw{i}", [128, 32 * 128], BF16))
                              for i in range(4)]
                        w3b = [e5(nc.sbuf_tensor(f"w3b{i}", [128, HS2 * 128], BF16))
                               for i in range(2)]
                        sg_sb = [e5(nc.sbuf_tensor(f"sg_sb{i}", [128, 512], F32))
                                 for i in range(2)]
                        dn_st = e5(nc.sbuf_tensor("dn_st", [128, 512], F32))
                        dn_p = e5(nc.sbuf_tensor("dn_p", [128, 512], F32))
                        dn_o = e5(nc.sbuf_tensor("dn_o", [128, 512], F32))

                        SY.wait_ge(sAGc[0].h, sAGc[0].n)
                        SY.wait_ge(sAGc[1].h, sAGc[1].n)
                        gu_banks = [(pA, pB), (pC, pD)]

                        def gu_load(s, par, pe_gate):
                            if pe_gate[par] is not None:
                                SY.wait_ge(sPE.h, pe_gate[par])
                            rank, ml = s // 11, s % 11
                            b1 = rank * NT1 * 128 + ag1_off(0, ml, 0)
                            b2 = rank * NT1 * 128 + (ag1_off(1, ml, 0) - NT1 * 128)
                            for sem, bb, bs, dst in ((sF1[par], fw[2 * par], b1, ag_out1),
                                                     (sF2[par], fw[2 * par + 1], b2, ag_out2)):
                                sem.inc(SY.dma_start(
                                    out=bb.ap().rearrange("p (c k) -> p c k", k=128),
                                    in_=dst.ap()[bs:bs + 32 * 128, :]
                                    .rearrange("(c p) k -> p c k", p=128)), 16)
                            return (sF1[par].n, sF2[par].n)

                        for tt in range(2):
                            tof = tt * 512
                            for hhf in range(4):
                                s0 = hhf * HS2
                                pe_gate = [None, None]
                                fw_ready = {}
                                fw_ready[0] = gu_load(s0, 0, pe_gate)
                                if HS2 > 1:
                                    fw_ready[1] = gu_load(s0 + 1, 1, pe_gate)
                                dv_gate = {}
                                for sl_ in range(HS2):
                                    s = s0 + sl_
                                    par = sl_ % 2
                                    bg, bu = gu_banks[par]
                                    TE.wait_ge(sF1[par].h, fw_ready[sl_][0])
                                    TE.wait_ge(sF2[par].h, fw_ready[sl_][1])
                                    TE.wait_ge(sXL.h, sXL.n)
                                    if sl_ >= 2:
                                        TE.wait_ge(sDV.h, dv_gate[sl_ - 2])
                                    for c in range(32):
                                        mm = TE.matmul(bg[:], fw[2 * par][:, c * 128:(c + 1) * 128],
                                                       x_fm[:, c * TOWN + tof:c * TOWN + tof + 512],
                                                       start=(c == 0), stop=(c == 31))
                                    pe(mm)
                                    gate_done = sPE.n
                                    for c in range(32):
                                        mm = TE.matmul(bu[:], fw[2 * par + 1][:, c * 128:(c + 1) * 128],
                                                       x_fm[:, c * TOWN + tof:c * TOWN + tof + 512],
                                                       start=(c == 0), stop=(c == 31))
                                    pe(mm)
                                    pe_gate[par] = sPE.n
                                    if sl_ + 2 < HS2:
                                        fw_ready[sl_ + 2] = gu_load(s0 + sl_ + 2, par, pe_gate)
                                    SC.wait_ge(sPE.h, gate_done)
                                    SC.wait_ge(sDV.h, dv_gate.get(sl_ - 2, 0))
                                    ac(SC.activation(out=sg_sb[par][:], in_=bg[:], func=AF.Silu))
                                    V.wait_ge(sAC.h, sAC.n)
                                    V.wait_ge(sPE.h, pe_gate[par])
                                    dv(V.tensor_tensor(out=h_fm[:, sl_ * 512:(sl_ + 1) * 512],
                                                       in0=bu[:], in1=sg_sb[par][:], op=ALU.mult))
                                    dv_gate[sl_] = sDV.n
                                dn_banks = [pE, pF]
                                w3_ready = {}
                                pe_dn = [None, None]

                                def w3_load(m, par):
                                    if pe_dn[par] is not None:
                                        SY.wait_ge(sPE.h, pe_dn[par])
                                    SY.wait_ge(sAGc[2].h, sAGc[2].n)
                                    rank3, cl = m // 4, m % 4
                                    base3 = rank3 * NT3 * 128 + (ag3_off(s0, 0) - 2 * NT1 * 128)
                                    sW3p[par].inc(SY.dma_start(
                                        out=w3b[par].ap().rearrange("p (s k) -> p s k", k=128),
                                        in_=ag_out3.ap()[base3:base3 + HS2 * 4 * 128, :]
                                        .rearrange("(s q p) k -> p s q k", q=4, p=128)[:, :, cl, :]),
                                        16)
                                    return sW3p[par].n

                                w3_ready[0] = w3_load(0, 0)
                                w3_ready[1] = w3_load(1, 1)
                                for m in range(32):
                                    par = m % 2
                                    bk = dn_banks[par]
                                    TE.wait_ge(sW3p[par].h, w3_ready[m])
                                    TE.wait_ge(sDV.h, sDV.n)
                                    for s_ in range(HS2):
                                        mm = TE.matmul(bk[:], w3b[par][:, s_ * 128:(s_ + 1) * 128],
                                                       h_fm[:, s_ * 512:(s_ + 1) * 512],
                                                       start=(s_ == 0), stop=(s_ == HS2 - 1))
                                    pe(mm)
                                    pe_dn[par] = sPE.n
                                    if m + 2 < 32:
                                        w3_ready[m + 2] = w3_load(m + 2, par)
                                    V.wait_ge(sPE.h, pe_dn[par])
                                    V.wait_ge(sST.h, sST.n)
                                    if hhf == 0:
                                        dv(V.tensor_copy(out=dn_st[:], in_=bk[:]))
                                        SY.wait_ge(sDV.h, sDV.n)
                                        sST.inc(SY.dma_start(
                                            out=dn_part.ap()[m * 128:(m + 1) * 128, tof:tof + 512],
                                            in_=dn_st[:]), 16)
                                    else:
                                        SY.wait_ge(sDV.h, sDV.n)
                                        sXL.inc(SY.dma_start(
                                            out=dn_p[:],
                                            in_=dn_part.ap()[m * 128:(m + 1) * 128, tof:tof + 512]), 16)
                                        V.wait_ge(sXL.h, sXL.n)
                                        dv(V.tensor_tensor(out=dn_st[:], in0=bk[:],
                                                           in1=dn_p[:], op=ALU.add))
                                        if hhf < 3:
                                            SY.wait_ge(sDV.h, sDV.n)
                                            sST.inc(SY.dma_start(
                                                out=dn_part.ap()[m * 128:(m + 1) * 128, tof:tof + 512],
                                                in_=dn_st[:]), 16)
                                        else:
                                            wait_engines(TE, [sDV, sAC])
                                            for j in range(4):
                                                tr = TE.transpose(pU[0][:, j * 128:(j + 1) * 128],
                                                                  dn_st[:, j * 128:(j + 1) * 128],
                                                                  c_idenf[:])
                                            pe(tr)
                                            V.wait_ge(sPE.h, sPE.n)
                                            dv(V.tensor_copy(out=dn_o[:], in_=pU[0][:]))
                                            SY.wait_ge(sDV.h, sDV.n)
                                            sST.inc(SY.dma_start(
                                                out=dn_tm.ap()[tof:tof + 512, m * 128:(m + 1) * 128]
                                                .rearrange("(j p) k -> p j k", p=128),
                                                in_=dn_o[:].rearrange("p (j k) -> p j k", k=128)), 16)

                    SY.wait_ge(sST.h, sST.n)
                    for g in range(NOG):
                        ln_pass(x2sp.ap()[g * 128:(g + 1) * 128, :],
                                dn_tm.ap()[g * 128:(g + 1) * 128, :], None, None)
                        SY.wait_ge(sDV.h, sDV.n)
                        sOUT.inc(SY.dma_start(out=out[g * 128:(g + 1) * 128, :],
                                              in_=ln_x[:]), 16)
                    SY.wait_ge(sOUT.h, sOUT.n)

    return nc



def make_consts(nh=32):
    import ml_dtypes
    cbv = np.zeros((128, 32), dtype=np.float32)
    cbv[:, 0:15] = (-(NF4_BOUNDS.astype(np.float64) * 2.0 ** 100)).astype(np.float32)[None, :]
    cbv[:, CB_ONE] = 1.0
    cbv[:, CB_EPS] = EPS
    oh = np.zeros((32, 32 * 128), dtype=np.float32)
    for h in range(32):
        oh[h, h * 128:(h + 1) * 128] = 1.0
    id32 = np.eye(32, dtype=np.float32)
    idenb = np.eye(128, dtype=ml_dtypes.bfloat16)
    idenf = np.eye(128, dtype=np.float32)
    k = np.arange(128)
    mA = (k[:, None] < k[None, :]).astype(ml_dtypes.bfloat16)
    mB = (-200.0 * (k[:, None] >= k[None, :])).astype(ml_dtypes.bfloat16)
    return cbv, oh, id32, idenb, idenf, mA, mB


def make_rope(half):
    inv = (1.0 / (np.float32(10000.0) **
                  (np.arange(0, 128, 2, dtype=np.float32) / np.float32(128.0)))).astype(np.float32)
    pos = np.arange(T, dtype=np.float32)
    if half == 0:
        pos = np.maximum(pos - TOWN, 0.0).astype(np.float32)
    ang = pos[None, :] * inv[:, None]
    cosf = np.concatenate([np.cos(ang), np.cos(ang)], axis=0).astype(np.float32)
    sins = np.sin(ang).astype(np.float32)
    sinsg = np.concatenate([-sins, sins], axis=0).astype(np.float32)
    return cosf, sinsg


def kernel(L0_attn_hidden_states, L0_attn_Wq, L0_attn_Wk, L0_attn_Wv, L0_attn_Wo,
           L0_attn_Wg, L0_attn_Win, L0_attn_dt_bias, L0_attn_A_log_bias,
           L0_attn_gnorm_w, L0_ffn_w1, L0_ffn_w2, L0_ffn_w3):
    x = np.asarray(L0_attn_hidden_states, dtype=np.float32)
    B = x.shape[0]
    cbv, oh, id32, idenb, idenf, mA, mB = make_consts()
    wq = np.ascontiguousarray(np.asarray(L0_attn_Wq, np.float32).T)
    wk = np.ascontiguousarray(np.asarray(L0_attn_Wk, np.float32).T)
    wv = np.ascontiguousarray(np.asarray(L0_attn_Wv, np.float32).T)
    wg = np.ascontiguousarray(np.asarray(L0_attn_Wg, np.float32).T)
    win = np.ascontiguousarray(np.asarray(L0_attn_Win, np.float32).T)
    wo = np.ascontiguousarray(np.asarray(L0_attn_Wo, np.float32).T)
    dtbv = np.asarray(L0_attn_dt_bias, np.float32).reshape(32, 1)
    alogv = np.asarray(L0_attn_A_log_bias, np.float32).reshape(32, 1)
    gnw = np.broadcast_to(np.asarray(L0_attn_gnorm_w, np.float32)[None, :],
                          (128, 128)).copy()
    w1 = np.asarray(L0_ffn_w1, np.float32)
    w2 = np.asarray(L0_ffn_w2, np.float32)
    w3 = np.asarray(L0_ffn_w3, np.float32)
    HID = w1.shape[0]

    in_maps = []
    for r in range(NRANK):
        b, half = r // 2, r % 2
        xa = np.zeros((T, DM), dtype=np.float32)
        xa[TOWN:] = x[b, half * TOWN:(half + 1) * TOWN]
        if half == 1:
            xa[:TOWN] = x[b, 0:TOWN]
        cosf, sinsg = make_rope(half)
        w1r = np.zeros((HIDR, DM), dtype=np.float32)
        w2r = np.zeros((HIDR, DM), dtype=np.float32)
        lo = r * HIDR
        hi = min((r + 1) * HIDR, HID)
        if hi > lo:
            w1r[0:hi - lo] = w1[lo:hi]
            w2r[0:hi - lo] = w2[lo:hi]
        w3r = np.zeros((W3R, HID_PAD), dtype=np.float32)
        w3r[:, 0:HID] = w3[r * W3R:(r + 1) * W3R, :]
        in_maps.append(dict(
            x_all=xa, wqT=wq, wkT=wk, wvT=wv, wgT=wg, winT=win, woT=wo,
            cosf=cosf, sinsg=sinsg, dtb=dtbv, alog=alogv, gnwb=gnw, cb=cbv,
            onehot=oh, id32=id32, iden=idenb, idenf=idenf, maskA=mA, maskB=mB,
            ones32=np.ones((32, 128), dtype=np.float32),
            w1s=w1r, w2s=w2r, w3s=w3r))

    nc = build_program(strict=False)
    res = run_bass_kernel_spmd(nc, in_maps, core_ids=list(range(NRANK)))
    outv = np.zeros((B, 2048, DM), dtype=np.float32)
    for r in range(NRANK):
        b, half = r // 2, r % 2
        outv[b, half * TOWN:(half + 1) * TOWN] = np.asarray(res.results[r]["out"])
    return outv
